# version 26
# baseline (speedup 1.0000x reference)
import numpy as np
import concourse.bass as bass
import concourse.mybir as mybir
from concourse.bass_utils import run_bass_kernel_spmd

F32 = mybir.dt.float32
BF16 = mybir.dt.bfloat16
AF = mybir.ActivationFunctionType
ALU = mybir.AluOpType

NCORE = 8
D = 1024
KC = 8
T = 4096
NSW = 8
SW = 512
CH = 128
NCHS = 4
NP = 8
DH = 64
M = 32
DFF = 4096
MJ = 32
MOE_I = 8192
MOE_J = 64
MTOK = 512
L = 3
LN_EPS = 1e-5
ATTN_EPS = 1e-6
SC = float(DH) ** -0.25
CSQ = 0.0625
LOGM = float(np.log((2.0 * M) ** -0.5))
LOGEPS = float(np.log(ATTN_EPS))
GC1 = 0.044715
GC2 = float(2.0 * np.sqrt(2.0 / np.pi))
RG = [[0, 1, 2, 3, 4, 5, 6, 7]]
TG = 512
LN_V2 = False

ENGINES = ("tensor", "vector", "scalar", "gpsimd", "sync")
G, V, A, TE, SY = "gpsimd", "vector", "scalar", "tensor", "sync"


class Sched:
    AUTO_CHAIN = ("vector", "scalar", "gpsimd")

    def __init__(self, nc, ctx):
        self.nc = nc
        self.ops = {e: [] for e in ENGINES}
        self.cnt = {}
        self.last_waited = {e: {} for e in ENGINES}
        self.sems = {}
        self._ctx = ctx
        for e in ENGINES:
            self.sems[e] = ctx.enter_context(nc.semaphore(f"s_{e}"))
            self.cnt[e] = 0
        self.chan = {}

    def op(self, eng, fn):
        sem = self.sems[eng]
        if eng in self.AUTO_CHAIN and self.cnt[eng] > 0:
            prev = self.cnt[eng]
            if self.last_waited[eng].get(eng, 0) < prev:
                self.last_waited[eng][eng] = prev
                def emitw(e, sem=sem, val=prev):
                    e.wait_ge(sem, val)
                self.ops[eng].append(emitw)
        self.cnt[eng] += 1
        want = self.cnt[eng]
        def emit(e, fn=fn, sem=sem):
            fn(e).then_inc(sem, 1)
        self.ops[eng].append(emit)
        return (eng, want)

    def mkchan(self, name):
        sems = [self._ctx.enter_context(self.nc.semaphore(f"c_{name}{i}"))
                for i in range(2)]
        self.chan[name] = {"sems": sems, "slot": 0}
        for i in range(2):
            self.sems[f"{name}{i}"] = sems[i]
            self.cnt[f"{name}{i}"] = 0

    def dma(self, eng, fn, ch, flip=False):
        c = self.chan[ch]
        if flip:
            c["slot"] ^= 1
        slot = c["slot"]
        key = f"{ch}{slot}"
        sem = c["sems"][slot]
        if flip and self.cnt[key] > 0:
            prev = self.cnt[key]
            if self.last_waited[eng].get(key, 0) < prev:
                self.last_waited[eng][key] = prev
                def emitw(e, sem=sem, val=prev):
                    e.wait_ge(sem, val)
                self.ops[eng].append(emitw)
        self.cnt[key] += 16
        want = self.cnt[key]
        def emit(e, fn=fn, sem=sem):
            fn(e).then_inc(sem, 16)
        self.ops[eng].append(emit)
        return (key, want)

    def chtok(self, ch):
        return [(f"{ch}{i}", self.cnt[f"{ch}{i}"]) for i in range(2)]

    def collective(self, kind, op, ins, outs):
        name = f"cc{len([k for k in self.sems if k.startswith('cc')])}"
        sem = self._ctx.enter_context(self.nc.semaphore(name))
        self.sems[name] = sem
        self.cnt[name] = 1
        def emit(e, sem=sem):
            e.collective_compute(kind, op, replica_groups=RG,
                                 ins=[i.opt() for i in ins],
                                 outs=[o.opt() for o in outs]).then_inc(sem)
        self.ops["gpsimd"].append(emit)
        return (name, 1)

    def w(self, eng, *tokens):
        for t in tokens:
            if t is None:
                continue
            (src, val) = t
            if src == eng:
                continue
            if val <= self.last_waited[eng].get(src, 0):
                continue
            self.last_waited[eng][src] = val
            sem = self.sems[src]
            def emit(e, sem=sem, val=val):
                e.wait_ge(sem, val)
            self.ops[eng].append(emit)

    def token(self, eng):
        return (eng, self.cnt[eng])

    def drain_stores(self):
        self.w(SY, *self.chtok("st"), *self.chtok("ou"),
               *self.chtok("hw"), *self.chtok("sn"))

    def barrier(self):
        self.drain_stores()
        self.op(SY, lambda e: e.nop())
        toks = [self.token(e) for e in ENGINES]
        for e in ENGINES:
            self.w(e, *toks)

    def replay(self, block):
        for e in ENGINES:
            ops = self.ops[e]
            def mk(ops):
                def run(engine):
                    for fn in ops:
                        fn(engine)
                return run
            getattr(block, e)(mk(ops))


def build(nocc=False):
    import contextlib
    nc = bass.Bass("TRN2", target_bir_lowering=False)
    DT = nc.dram_tensor
    xshard = DT("xshard", [KC, 128, T], F32, kind="ExternalInput").ap()
    xmoe = DT("xmoe", [KC, 128, MTOK], F32, kind="ExternalInput").ap()
    ewT = DT("ewT", [2, MTOK], F32, kind="ExternalInput").ap()
    pvec = DT("pvec", [128, 312], F32, kind="ExternalInput").ap()
    rowv = DT("rowv", [1, L * D], F32, kind="ExternalInput").ap()
    omg_in = DT("omg_in", [L, DH, M], F32, kind="ExternalInput").ap()
    wq_in = DT("wq_in", [L, D, D], F32, kind="ExternalInput").ap()
    wk_in = DT("wk_in", [L, D, D], F32, kind="ExternalInput").ap()
    wv_in = DT("wv_in", [L, D, D], F32, kind="ExternalInput").ap()
    wo_in = DT("wo_in", [L, D, D], F32, kind="ExternalInput").ap()
    w1_in = DT("w1_in", [L, D, DFF], F32, kind="ExternalInput").ap()
    w2_in = DT("w2_in", [L, DFF, D], F32, kind="ExternalInput").ap()
    mw1_in = DT("mw1_in", [D, MOE_I], F32, kind="ExternalInput").ap()
    mw2_in = DT("mw2_in", [MOE_I, D], F32, kind="ExternalInput").ap()
    mask0 = DT("mask0", [128, 1], F32, kind="ExternalInput").ap()
    scoef = DT("scoef", [128, 8], F32, kind="ExternalInput").ap()
    yshard = DT("yshard", [KC, 128, T], F32, kind="ExternalOutput").ap()
    ymoe3 = DT("ymoe3", [KC, 128, MTOK], F32, kind="ExternalOutput").ap()
    st_x1 = DT("st_x1", [KC, 128, T], F32).ap()
    st_a = DT("st_a", [KC, 128, T], F32).ap()
    st_h = DT("st_h", [KC, 128, T], F32).ap()
    phk_d = DT("phk_d", [NP, 128, T], BF16).ap()
    vraw_d = DT("vraw_d", [T, D], BF16).ap()
    snap_d = DT("snap_d", [32, 128, 520], F32).ap()
    ccm1_in = DT("ccm1_in", [128, KC * SW], F32).ap()
    ccm1_out = DT("ccm1_out", [NCORE * 128, KC * SW], F32,
                  addr_space=("Local" if nocc else "Shared")).ap()
    ccst_in = DT("ccst_in", [128, 520], F32).ap()
    ccst_out = DT("ccst_out", [NCORE * 128, 520], F32,
                  addr_space=("Local" if nocc else "Shared")).ap()
    ccx2_in = DT("ccx2_in", [T, D], F32).ap()
    ccx2_out = DT("ccx2_out", [MTOK, D], F32).ap()

    ctx = contextlib.ExitStack()
    with ctx:
        Sb = lambda *a: ctx.enter_context(nc.sbuf_tensor(*a))
        Ps = lambda *a: ctx.enter_context(nc.psum_tensor(*a))
        pv = Sb("pv", [128, 312], F32)
        rvb = Sb("rvb", [1, D], BF16)
        omgs = Sb("omgs", [128, L * M], F32)
        obs = Sb("obs", [128, L * 128], BF16)
        cbt = Sb("cbt", [128, 128], BF16)
        hm2 = Sb("hm2", [128, 2], BF16)
        identb = Sb("identb", [128, 128], BF16)
        maskf = Sb("maskf", [128, 128], F32)
        onesf = Sb("onesf", [128, 128], F32)
        onesbr = Sb("onesbr", [1, 128], BF16)
        halfo = Sb("halfo", [2, 1], F32)
        biasm = Sb("biasm", [128, 2], F32)
        m0 = Sb("m0", [128, 1], F32)
        scf = Sb("scf", [128, 8], F32)
        fbufA = Sb("fbufA", [128, KC * SW], F32)
        tscr = Sb("tscr", [128, 2 * SW], F32)
        hsl = Sb("hsl", [128, 2 * SW], F32)
        rows = Sb("rows", [1, 4 * SW], F32)
        wreg = Sb("wreg", [128, 65536], BF16)
        arena = Sb("arena", [128, 20480], BF16)
        srun = arena[:, 0:2080].bitcast(F32)
        sinit = arena[:, 2080:3120].bitcast(F32)
        srq = arena[:, 3120:4160].bitcast(F32)
        snapb = arena[:, 4160:6240].bitcast(F32)
        sbe = arena[:, 6240:6760]
        sbo = arena[:, 6760:7280]
        vts = arena[:, 7280:9360]
        phkt = arena[:, 9360:10384]
        eps2 = arena[0:2, 10384:14480]
        h1f = arena[:, 0:16384]
        hbf = arena[:, 16384:20480]
        ATMB = 14480
        atk0 = lambda pi, hh: arena[:, ATMB + pi * 768 + hh * 256:ATMB + pi * 768 + (hh + 1) * 256]
        atk1 = lambda pi, hh: arena[:, ATMB + pi * 768 + 512 + hh * 128:ATMB + pi * 768 + 512 + (hh + 1) * 128]
        vts1 = arena[:, 16528:18608]
        rdbs2 = [arena[:, 16016:16528].bitcast(F32),
                 arena[:, 19632:20144].bitcast(F32)]
        mw = arena[:, 9360:10384].bitcast(F32)
        sq = arena[:, 14480:15504].bitcast(F32)
        ewbs = arena[:, 15504:16528].bitcast(F32)
        ewts = arena[0:2, 16528:17552].bitcast(F32)
        idf = fbufA[:, 0:128]
        rvf = fbufA[0:1, SW:SW + D]

        wqsb = wreg[:, 0:8192]
        wksb = wreg[:, 8192:16384]
        wvsb = wreg[:, 16384:24576]
        wosb = wreg[:, 24576:32768]
        xbw = [wreg[:, 32768 + i * 4096:32768 + (i + 1) * 4096] for i in range(2)]
        ksb = wreg[:, 40960:45056]
        k2sb = wreg[:, 45056:49152]
        phkb = wreg[:, 49152:53248]
        phqb = wreg[:, 53248:57344]
        asw = wreg[:, 57344:61440]
        vstage = wreg[:, 61440:63488]
        phz = wreg[:, 61440:65536]
        w1sb = wreg[:, 0:32768]
        w2sb = wreg[:, 32768:65536]
        mw1b = [wreg[:, i * 1024:(i + 1) * 1024] for i in range(2)]
        mw2b = [wreg[:, 4096 + i * 8192:4096 + (i + 1) * 8192] for i in range(2)]
        h1m = wreg[:, 20480:53248]
        xg = wreg[:, 53248:57344]

        pf = [Ps(f"pf{i}", [128, 512], F32) for i in range(7)]
        pbt = Ps("pbt", [128, 1024], BF16)
        pbtF = pbt[:, 0:1024].bitcast(F32)

        s = Sched(nc, ctx)
        for chn in ("wq", "wf", "xb", "xf", "st", "ld", "ou", "ph", "sn", "hw"):
            s.mkchan(chn)
        block = ctx.enter_context(nc.Block())

        def pvcol(l, name, k):
            off = {"bq": 0, "bk": 8, "bv": 16, "bo": 24, "ln1g": 32, "ln1b": 40,
                   "ln2g": 48, "ln2b": 56, "bf2": 64, "bf1": 72}[name]
            cidx = l * 104 + off + k
            return pv[:, cidx:cidx + 1]

        def do_cc(kind, op, cin, cout, nrow):
            if nocc:
                return s.dma(G, lambda g: g.dma_start(cout[0:nrow, :], cin), "ou")
            return s.collective(kind, op, [cin], [cout])

        s.dma(G, lambda g: g.dma_start(pv[:, :], pvec), "wq")
        s.dma(G, lambda g: g.dma_start(m0[:, :], mask0), "wq")
        s.dma(G, lambda g: g.dma_start(scf[:, :], scoef), "wq")
        tldw = None
        for l in range(L):
            s.dma(G, lambda g, l=l: g.dma_start(omgs[0:64, l * M:(l + 1) * M], omg_in[l]), "wq")
            tldw = s.dma(G, lambda g, l=l: g.dma_start(omgs[64:128, l * M:(l + 1) * M], omg_in[l]), "wq")
        s.op(G, lambda g: g.memset(onesf[:, :], 1.0))
        s.op(G, lambda g: g.memset(onesbr[:, :], 1.0))
        s.op(G, lambda g: g.memset(halfo[:, :], 0.5))
        s.op(G, lambda g: g.memset(biasm[:, 0:1], LOGM))
        s.op(G, lambda g: g.memset(biasm[:, 1:2], LOGEPS))
        s.op(G, lambda g: g.memset(cbt[:, :], 0.0))
        s.op(G, lambda g: g.memset(hm2[:, :], 0.0))
        s.op(G, lambda g: g.memset(obs[:, :], 0.0))
        s.op(G, lambda g: g.memset(sbe[:, :], 0.0))
        s.op(G, lambda g: g.memset(sbo[:, :], 0.0))
        for sl in range(2):
            for i in range(16):
                s.op(G, lambda g, sl=sl, i=i: g.memset(
                    vts[:, sl * 1040 + i * 65 + 64:sl * 1040 + i * 65 + 65], 1.0))
        s.op(G, lambda g: g.affine_select(idf[:, :], onesf[:, :], pattern=[[1, 128]],
                                          compare_op=ALU.is_equal, fill=0.0,
                                          base=0, channel_multiplier=-1))
        tgm = s.op(G, lambda g: g.affine_select(maskf[:, :], onesf[:, :], pattern=[[1, 128]],
                                               compare_op=ALU.is_ge, fill=0.0,
                                               base=0, channel_multiplier=-1))
        s.w(A, tgm, tldw)
        s.op(A, lambda e: e.activation(identb[:, :], idf[:, :], AF.Identity, bias=0.0, scale=1.0))
        s.op(A, lambda e: e.activation(cbt[0:64, 0:64], onesf[0:64, 0:64], AF.Identity, bias=0.0, scale=-CSQ))
        s.op(A, lambda e: e.activation(cbt[64:128, 64:128], onesf[64:128, 0:64], AF.Identity, bias=0.0, scale=-CSQ))
        s.op(A, lambda e: e.activation(hm2[0:64, 0:1], onesf[0:64, 0:1], AF.Identity, bias=0.0, scale=CSQ))
        s.op(A, lambda e: e.activation(hm2[64:128, 1:2], onesf[64:128, 0:1], AF.Identity, bias=0.0, scale=CSQ))
        tcons = None
        for l in range(L):
            ob = obs[:, l * 128:(l + 1) * 128]
            om0 = omgs[0:64, l * M:(l + 1) * M]
            om1 = omgs[64:128, l * M:(l + 1) * M]
            s.op(A, lambda e, ob=ob, om0=om0: e.activation(ob[0:64, 0:32], om0, AF.Identity, bias=0.0, scale=SC))
            s.op(A, lambda e, ob=ob, om0=om0: e.activation(ob[0:64, 32:64], om0, AF.Identity, bias=0.0, scale=-SC))
            s.op(A, lambda e, ob=ob, om1=om1: e.activation(ob[64:128, 64:96], om1, AF.Identity, bias=0.0, scale=SC))
            tcons = s.op(A, lambda e, ob=ob, om1=om1: e.activation(ob[64:128, 96:128], om1, AF.Identity, bias=0.0, scale=-SC))

        def build_ewbs(gate):
            tld = s.dma(G, lambda g: g.dma_start(ewts[:, :], ewT), "wq", flip=True)
            s.w(TE, tld, gate, tcons)
            t1 = s.op(TE, lambda e: e.matmul(pf[4][0:1, 0:MTOK], halfo[:, :], ewts[:, :], start=True, stop=True))
            s.w(V, t1)
            t2 = s.op(V, lambda e: e.tensor_copy(rows[0:1, 0:MTOK], pf[4][0:1, 0:MTOK]))
            s.w(TE, t2)
            t3 = s.op(TE, lambda e: e.matmul(pf[6][:, 0:MTOK], onesf[0:1, :], rows[0:1, 0:MTOK], start=True, stop=True))
            s.w(A, t3)
            return s.op(A, lambda e: e.activation(ewbs[:, :], pf[6][:, 0:MTOK], AF.Identity, bias=0.0, scale=1.0))

        def emit_ln_v1(rbA, rbB, pbA, pbB, gates=()):
            s.w(V, *gates)
            s.w(A, *gates)
            tv = s.op(V, lambda e: e.tensor_add(tscr[:, 0:SW], fbufA[:, 0:SW], fbufA[:, SW:2 * SW]))
            for k in range(2, KC):
                tv = s.op(V, lambda e, k=k: e.tensor_add(tscr[:, 0:SW], tscr[:, 0:SW], fbufA[:, k * SW:(k + 1) * SW]))
            ta = s.op(A, lambda e: e.activation(sq[:, :], fbufA[:, 0:SW], AF.Square))
            s.w(V, ta)
            tq = s.op(V, lambda e: e.tensor_copy(tscr[:, SW:2 * SW], sq[:, :]))
            for k in range(1, KC):
                s.w(A, tq)
                ta = s.op(A, lambda e, k=k: e.activation(sq[:, :], fbufA[:, k * SW:(k + 1) * SW], AF.Square))
                s.w(V, ta)
                tq = s.op(V, lambda e: e.tensor_add(tscr[:, SW:2 * SW], tscr[:, SW:2 * SW], sq[:, :]))
            s.w(TE, tq, tv)
            tm1 = s.op(TE, lambda e, rbA=rbA: e.matmul(rbA[0:1, 0:SW], onesf[:, 0:1], tscr[:, 0:SW], start=True, stop=True))
            tm2 = s.op(TE, lambda e, rbB=rbB: e.matmul(rbB[0:1, 0:SW], onesf[:, 0:1], tscr[:, SW:2 * SW], start=True, stop=True))
            R = lambda i: rows[0:1, i * SW:(i + 1) * SW]
            s.w(V, tm1, tm2)
            s.op(V, lambda e, rbA=rbA: e.tensor_scalar(R(0), rbA[0:1, 0:SW], 1.0 / D, None, ALU.mult))
            tr0 = s.op(V, lambda e, rbB=rbB: e.tensor_scalar(R(1), rbB[0:1, 0:SW], 1.0 / D, LN_EPS, ALU.mult, ALU.add))
            s.w(A, tr0)
            ta1 = s.op(A, lambda e: e.activation(R(2), R(0), AF.Square))
            s.w(V, ta1)
            tvv = s.op(V, lambda e: e.tensor_sub(R(1), R(1), R(2)))
            s.w(A, tvv)
            s.op(A, lambda e: e.activation(R(2), R(1), AF.Ln))
            tsq2 = s.op(A, lambda e: e.activation(R(3), R(2), AF.Exp, bias=0.0, scale=-0.5))
            s.w(V, tsq2)
            s.op(V, lambda e: e.tensor_mul(R(2), R(0), R(3)))
            trB = s.op(V, lambda e: e.tensor_scalar(R(2), R(2), -1.0, None, ALU.mult))
            s.w(TE, trB, tsq2)
            tbA = s.op(TE, lambda e, pbA=pbA: e.matmul(pbA[:, 0:SW], onesf[0:1, :], R(3), start=True, stop=True))
            tbB = s.op(TE, lambda e, pbB=pbB: e.matmul(pbB[:, 0:SW], onesf[0:1, :], R(2), start=True, stop=True))
            return tbA, tbB

        def emit_ln(rbA, rbB, pbA, pbB, gates=()):
            s.w(TE, *gates)
            s.w(A, *gates)
            for k in range(KC):
                s.op(TE, lambda e, k=k, rbA=rbA: e.matmul(
                    rbA[0:1, 0:SW], onesf[:, 0:1], fbufA[:, k * SW:(k + 1) * SW],
                    start=(k == 0), stop=(k == KC - 1)))
            tmq = {}
            for k in range(KC):
                kl = k % 2
                if k >= 2:
                    s.w(A, tmq[k - 2])
                tsq = s.op(A, lambda e, k=k, kl=kl: e.activation(
                    tscr[:, kl * SW:(kl + 1) * SW], fbufA[:, k * SW:(k + 1) * SW], AF.Square))
                s.w(TE, tsq)
                tmq[k] = s.op(TE, lambda e, k=k, kl=kl, rbB=rbB: e.matmul(
                    rbB[0:1, 0:SW], onesf[:, 0:1], tscr[:, kl * SW:(kl + 1) * SW],
                    start=(k == 0), stop=(k == KC - 1)))
            R = lambda i: rows[0:1, i * SW:(i + 1) * SW]
            s.w(V, tmq[KC - 1])
            s.op(V, lambda e, rbA=rbA: e.tensor_scalar(R(0), rbA[0:1, 0:SW], 1.0 / D, None, ALU.mult))
            tr0 = s.op(V, lambda e, rbB=rbB: e.tensor_scalar(R(1), rbB[0:1, 0:SW], 1.0 / D, LN_EPS, ALU.mult, ALU.add))
            s.w(A, tr0)
            ta1 = s.op(A, lambda e: e.activation(R(2), R(0), AF.Square))
            s.w(V, ta1)
            tvv = s.op(V, lambda e: e.tensor_sub(R(1), R(1), R(2)))
            s.w(A, tvv)
            s.op(A, lambda e: e.activation(R(2), R(1), AF.Ln))
            trs = s.op(A, lambda e: e.activation(R(3), R(2), AF.Exp, bias=0.0, scale=-0.5))
            s.w(V, trs)
            s.op(V, lambda e: e.tensor_mul(R(2), R(0), R(3)))
            trB = s.op(V, lambda e: e.tensor_scalar(R(2), R(2), -1.0, None, ALU.mult))
            s.w(TE, trB, trs)
            tbA = s.op(TE, lambda e, pbA=pbA: e.matmul(pbA[:, 0:SW], onesf[0:1, :], R(3), start=True, stop=True))
            tbB = s.op(TE, lambda e, pbB=pbB: e.matmul(pbB[:, 0:SW], onesf[0:1, :], R(2), start=True, stop=True))
            return tbA, tbB

        if not LN_V2:
            emit_ln = emit_ln_v1

        def emit_moe(tgxg, wr_fn):
            gelu_done = None
            gelu_hist = {}
            cons1 = {}
            for j in range(MOE_J):
                sl = j % 2
                if cons1.get(j - 2) is not None:
                    s.w(G, cons1[j - 2])
                tl = s.dma(G, lambda g, j=j, sl=sl: g.dma_start(
                    mw1b[sl].rearrange("p (kc i) -> p kc i", i=128),
                    mw1_in[:, j * 128:(j + 1) * 128].rearrange("(kc p) i -> p kc i", p=128)),
                    "wf", flip=True)
                pm = pf[j % 2]
                s.w(TE, tl, tgxg, gelu_hist.get(j - 2))
                tk = None
                for k in range(KC):
                    tk = s.op(TE, lambda e, k=k, sl=sl, pm=pm: e.matmul(
                        pm[:, 0:SW], mw1b[sl][:, k * 128:(k + 1) * 128],
                        xg[:, k * SW:(k + 1) * SW], start=(k == 0), stop=(k == KC - 1)))
                cons1[j] = tk
                s.w(A, tk, gelu_done)
                th0 = s.op(A, lambda e, pm=pm: e.activation(tscr[:, 0:SW], pm[:, 0:SW], AF.Identity, bias=0.0, scale=1.0))
                th1 = s.op(A, lambda e, pm=pm: e.activation(sq[:, :], pm[:, 0:SW], AF.Square))
                s.w(V, th1, th0)
                s.op(V, lambda e: e.tensor_scalar(sq[:, :], sq[:, :], GC1, 1.0, ALU.mult, ALU.add))
                tg3 = s.op(V, lambda e: e.tensor_mul(sq[:, :], sq[:, :], tscr[:, 0:SW]))
                s.w(A, tg3)
                tg4 = s.op(A, lambda e: e.activation(sq[:, :], sq[:, :], AF.Sigmoid, bias=0.0, scale=GC2))
                s.w(V, tg4)
                gelu_done = s.op(V, lambda e, j=j: e.tensor_mul(h1m[:, j * SW:(j + 1) * SW], sq[:, :], tscr[:, 0:SW]))
                gelu_hist[j] = gelu_done
            cons2 = {}
            tds = {}
            last = None
            for mo in range(KC):
                sl = mo % 2
                if cons2.get(mo - 2) is not None:
                    s.w(G, cons2[mo - 2])
                tl = s.dma(G, lambda g, mo=mo, sl=sl: g.dma_start(
                    mw2b[sl].rearrange("p (j i) -> p j i", i=128),
                    mw2_in[:, mo * 128:(mo + 1) * 128].rearrange("(j p) i -> p j i", p=128)),
                    "wf", flip=True)
                pm = pf[2 + mo % 2]
                s.w(TE, tl, gelu_done, tds.get(mo - 2))
                tk = None
                for j in range(MOE_J):
                    tk = s.op(TE, lambda e, j=j, sl=sl, pm=pm: e.matmul(
                        pm[:, 0:SW], mw2b[sl][:, j * 128:(j + 1) * 128],
                        h1m[:, j * SW:(j + 1) * SW], start=(j == 0), stop=(j == MOE_J - 1)))
                cons2[mo] = tk
                s.w(V, tk, tewb)
                td = s.op(V, lambda e, pm=pm: e.tensor_mul(tscr[:, SW:2 * SW], pm[:, 0:SW], ewbs[:, :]))
                tds[mo] = td
                last = wr_fn(mo, td, tscr[:, SW:2 * SW])
                s.w(V, last)
            return last

        tewb = build_ewbs(None)
        s.w(SY, tcons, tewb)
        tx = None
        for k in range(KC):
            tx = s.dma(SY, lambda e, k=k: e.dma_start(fbufA[:, k * SW:(k + 1) * SW], xmoe[k]), "xf")
        tbA, tbB = emit_ln(pf[4], pf[5], pf[5], pf[6], gates=(tx, tewb))
        tgxg = None
        for k in range(KC):
            kl = k % 2
            s.w(V, tbA, tbB, tgxg)
            s.op(V, lambda e, k=k, kl=kl: e.tensor_mul(tscr[:, kl * SW:(kl + 1) * SW], fbufA[:, k * SW:(k + 1) * SW], pf[5][:, 0:SW]))
            tt2 = s.op(V, lambda e, kl=kl: e.tensor_add(tscr[:, kl * SW:(kl + 1) * SW], tscr[:, kl * SW:(kl + 1) * SW], pf[6][:, 0:SW]))
            s.w(A, tt2)
            tgxg = s.op(A, lambda e, k=k, kl=kl: e.activation(xg[:, k * SW:(k + 1) * SW], tscr[:, kl * SW:(kl + 1) * SW], AF.Identity, bias=0.0, scale=1.0))

        def wr_moe1(mo, td, src):
            s.w(SY, td)
            return s.dma(SY, lambda e, mo=mo, src=src: e.dma_start(
                ccm1_in[:, mo * SW:(mo + 1) * SW], src), "hw", flip=True)
        tm1w = emit_moe(tgxg, wr_moe1)
        s.drain_stores()
        s.w(G, *s.chtok("hw"))
        if nocc:
            for r in range(NCORE):
                tccm1 = s.dma(G, lambda g, r=r: g.dma_start(
                    ccm1_out[r * 128:(r + 1) * 128, :], ccm1_in), "wq", flip=(r == 0))
        else:
            tccm1 = s.collective("AllGather", ALU.bypass, [ccm1_in], [ccm1_out])
        s.barrier()

        amul = {}
        thwA = {}
        tx1s = {}
        stt_prev = None
        tbA_prev = None
        for sw in range(NSW):
            txl = None
            for k in range(KC):
                s.w(SY, amul.get((sw - 1) * KC + k), tx1s.get((sw - 1) * KC + k))
                txl = s.dma(SY, lambda e, k=k, sw=sw: e.dma_start(
                    fbufA[:, k * SW:(k + 1) * SW], xshard[k][:, sw * SW:(sw + 1) * SW]),
                    "xf", flip=(k == 0))
            tvmask = None
            for k in range(KC):
                kl = k % 2
                if k >= 2:
                    s.w(SY, tvmask)
                else:
                    s.w(SY, tccm1, stt_prev, tbA_prev)
                tgl = s.dma(SY, lambda e, k=k, sw=sw, kl=kl: e.dma_start(
                    tscr[:, kl * SW:(kl + 1) * SW],
                    ccm1_out[sw * 128:(sw + 1) * 128, k * SW:(k + 1) * SW]),
                    "ld", flip=True)
                s.w(V, txl, tgl, tvmask)
                tvmask = s.op(V, lambda e, k=k, kl=kl: e.scalar_tensor_tensor(
                    fbufA[:, k * SW:(k + 1) * SW], tscr[:, kl * SW:(kl + 1) * SW],
                    m0[:, 0:1], fbufA[:, k * SW:(k + 1) * SW], ALU.mult, ALU.add))
                s.w(SY, tvmask)
                tx1s[sw * KC + k] = s.dma(SY, lambda e, k=k, sw=sw: e.dma_start(
                    st_x1[k][:, sw * SW:(sw + 1) * SW], fbufA[:, k * SW:(k + 1) * SW]),
                    "st", flip=True)
            stt_prev = tvmask
            tbA, tbB = emit_ln(pf[0], pf[1], pf[1], pf[2],
                               gates=(tvmask, amul.get((sw - 1) * KC + KC - 1)))
            tbA_prev = tbA
            for k in range(KC):
                kl = k % 2
                s.w(V, tbA, tbB, thwA.get(sw * KC + k - 2))
                amul[sw * KC + k] = s.op(V, lambda e, k=k, kl=kl: e.tensor_mul(
                    hsl[:, kl * SW:(kl + 1) * SW],
                    fbufA[:, k * SW:(k + 1) * SW], pf[1][:, 0:SW]))
                tt2 = s.op(V, lambda e, kl=kl: e.tensor_add(hsl[:, kl * SW:(kl + 1) * SW],
                                                            hsl[:, kl * SW:(kl + 1) * SW], pf[2][:, 0:SW]))
                s.w(SY, tt2)
                thwA[sw * KC + k] = s.dma(SY, lambda e, k=k, sw=sw, kl=kl: e.dma_start(
                    st_a[k][:, sw * SW:(sw + 1) * SW], hsl[:, kl * SW:(kl + 1) * SW]), "hw", flip=True)
        s.barrier()

        for l in range(L):
            ob = obs[:, l * 128:(l + 1) * 128]
            s.w(G, *s.chtok("st"), *s.chtok("hw"), *s.chtok("ou"))
            trv = s.dma(G, lambda g, l=l: g.dma_start(rvf[:, :], rowv[0:1, l * D:(l + 1) * D]), "wq")
            s.op(G, lambda g: g.memset(sbe[64:128, :], 0.0))
            s.op(G, lambda g: g.memset(sbo[0:64, :], 0.0))
            for sl_ in range(2):
                for i_ in range(16):
                    s.op(G, lambda g, sl_=sl_, i_=i_: g.memset(
                        vts[:, sl_ * 1040 + i_ * 65 + 64:sl_ * 1040 + i_ * 65 + 65], 1.0))
                    s.op(G, lambda g, sl_=sl_, i_=i_: g.memset(
                        vts1[:, sl_ * 1040 + i_ * 65 + 64:sl_ * 1040 + i_ * 65 + 65], 1.0))
            trvb0 = s.token(G)
            s.w(A, trv)
            trvb = s.op(A, lambda e: e.activation(rvb[:, :], rvf[:, :], AF.Identity, bias=0.0, scale=1.0))
            twq = None
            for wi, (dst, src) in enumerate(((wqsb, wq_in), (wksb, wk_in),
                                             (wvsb, wv_in), (wosb, wo_in))):
                for k in range(KC):
                    twq = s.dma(G, lambda g, dst=dst, src=src, k=k, l=l: g.dma_start(
                        dst[:, k * 1024:(k + 1) * 1024], src[l][k * 128:(k + 1) * 128, :]),
                        "wq", flip=(k == 0 and wi == 0))

            tsz = s.op(G, lambda g: g.memset(srun[:, :], 0.0))
            s.w(SY, tsz)
            tsnap = s.dma(SY, lambda e: e.dma_start(snap_d[0], srun[:, 0:520]), "sn", flip=True)
            tsadd = None
            tdm_last = None
            tevK = {}
            tfvK = {}
            tdmK = {}
            tvwK = {}
            kxdone = {}
            tvev = None
            ttr_done = None
            for sw in range(NSW):
                sl = sw % 2
                txb = None
                for k in range(KC):
                    if k == 0:
                        s.w(G, *(kxdone.get(sw - 2) or []))
                    txb = s.dma(G, lambda g, k=k, sw=sw, sl=sl: g.dma_start(
                        xbw[sl][:, k * SW:(k + 1) * SW], st_a[k][:, sw * SW:(sw + 1) * SW]),
                        "xb", flip=(k == 0))
                tev = {-2: (None, tevK.get((sw - 1) * KC + KC - 2)),
                       -1: (None, tevK.get((sw - 1) * KC + KC - 1))}
                for mc in range(KC):
                    pm = pf[mc % 2]
                    g2 = tev.get(mc - 2)
                    if mc == 0:
                        s.w(TE, tsadd)
                    s.w(TE, txb, twq, g2[1] if g2 else None)
                    tk = None
                    for k in range(KC):
                        tk = s.op(TE, lambda e, mc=mc, k=k, sl=sl, pm=pm: e.matmul(
                            pm[:, 0:SW], wksb[:, k * 1024 + mc * 128:k * 1024 + mc * 128 + 128],
                            xbw[sl][:, k * SW:(k + 1) * SW], start=(k == 0), stop=(k == KC - 1)))
                    s.w(A, tk)
                    te1 = s.op(A, lambda e, mc=mc, pm=pm, l=l: e.activation(
                        ksb[:, mc * SW:(mc + 1) * SW], pm[:, 0:SW], AF.Identity,
                        bias=pvcol(l, "bk", mc), scale=1.0))
                    te2 = s.op(A, lambda e, mc=mc, pm=pm, l=l: e.activation(
                        k2sb[:, mc * SW:(mc + 1) * SW], pm[:, 0:SW], AF.Square,
                        bias=pvcol(l, "bk", mc), scale=1.0))
                    tev[mc] = (te1, te2)
                    tevK[sw * KC + mc] = te2
                tfv = {-2: tfvK.get((sw - 1) * NP + NP - 2),
                       -1: tfvK.get((sw - 1) * NP + NP - 1)}
                for p in range(NP):
                    pm = pf[4 + p % 2]
                    if p == 0:
                        s.w(A, *s.chtok("st"))
                    s.w(TE, tev[p][0], tev[p][1], tfv.get(p - 2))
                    s.op(TE, lambda e, p=p, pm=pm, ob=ob: e.matmul(
                        pm[:, 0:SW], ob, ksb[:, p * SW:(p + 1) * SW], start=True, stop=False))
                    tk = s.op(TE, lambda e, p=p, pm=pm: e.matmul(
                        pm[:, 0:SW], cbt[:, :], k2sb[:, p * SW:(p + 1) * SW], start=False, stop=True))
                    s.w(A, tk)
                    tfv[p] = s.op(A, lambda e, p=p, pm=pm: e.activation(
                        phkb[:, p * SW:(p + 1) * SW], pm[:, 0:SW], AF.Exp,
                        bias=biasm[:, 0:1], scale=1.0))
                    tfvK[sw * NP + p] = tfv[p]
                    s.w(SY, tfv[p])
                    s.dma(SY, lambda e, p=p, sw=sw: e.dma_start(
                        phk_d[p][:, sw * SW:(sw + 1) * SW], phkb[:, p * SW:(p + 1) * SW]), "st")
                tvts = {}
                for c in range(NCHS):
                    cg = sw * NCHS + c
                    vsl = c % 2
                    s.w(TE, twq, tvev, trvb)
                    tk = None
                    for k in range(KC):
                        for half in range(2):
                            tk = s.op(TE, lambda e, c=c, k=k, half=half, sl=sl: e.matmul(
                                pf[2 + half][:, 0:SW],
                                xbw[sl][:, k * SW + c * CH:k * SW + c * CH + CH],
                                wvsb[:, k * 1024 + half * SW:k * 1024 + (half + 1) * SW],
                                start=(k == 0), stop=False))
                    for half in range(2):
                        tk = s.op(TE, lambda e, half=half: e.matmul(
                            pf[2 + half][:, 0:SW], onesbr[:, :],
                            rvb[0:1, half * SW:(half + 1) * SW],
                            start=False, stop=True))
                    s.w(A, tk, tvwK.get(cg - 2))
                    tva = s.op(A, lambda e, vsl=vsl: e.activation(
                        vstage[:, vsl * 1024:vsl * 1024 + SW], pf[2][:, 0:SW], AF.Identity, bias=0.0, scale=1.0))
                    tvev = s.op(A, lambda e, vsl=vsl: e.activation(
                        vstage[:, vsl * 1024 + SW:vsl * 1024 + 1024], pf[3][:, 0:SW], AF.Identity, bias=0.0, scale=1.0))
                    s.w(SY, tva, tvev)
                    tvwK[cg] = s.dma(SY, lambda e, cg=cg, vsl=vsl: e.dma_start(
                        vraw_d[cg * CH:(cg + 1) * CH, :], vstage[:, vsl * 1024:(vsl + 1) * 1024]), "st", flip=True)
                    s.w(V, tva, tvev, tdmK.get(cg - 2))
                    tvts[c] = s.op(V, lambda e, vsl=vsl: e.tensor_copy(
                        vts[:, vsl * 1040:(vsl + 1) * 1040].rearrange("p (i c) -> p i c", c=65)[:, :, 0:64],
                        vstage[:, vsl * 1024:(vsl + 1) * 1024].rearrange("p (i c) -> p i c", c=64)))
                    ttr = None
                    s.w(TE, tfv[NP - 1], ttr_done)
                    for p in range(NP):
                        ttr = s.op(TE, lambda e, p=p, c=c: e.transpose(
                            pbt[:, p * 128:(p + 1) * 128],
                            phkb[:, p * SW + c * CH:p * SW + (c + 1) * CH], identb[:, :]))
                    tpe = None
                    s.w(V, ttr, tdmK.get(cg - 1))
                    for p in range(NP):
                        tpe = s.op(V, lambda e, p=p: e.tensor_copy(
                            phkt[:, p * 128:(p + 1) * 128], pbt[:, p * 128:(p + 1) * 128]))
                    ttr_done = tpe
                    s.w(TE, tev[KC - 1][1], tpe, tvts[c], tsadd, trvb0)
                    tdmc = None
                    for p in range(NP):
                        bank = pf[0] if p < 4 else pf[1]
                        off = (p % 4) * 65
                        for hh in range(2):
                            tp = None if hh == 0 else (0, 64)
                            orow = slice(64 * hh, 64 * hh + 64)
                            tdmc = s.op(TE, lambda e, p=p, hh=hh, bank=bank, off=off, orow=orow, tp=tp, vsl=vsl: e.matmul(
                                bank[orow, off:off + 65],
                                phkt[:, p * 128 + 64 * hh:p * 128 + 64 * hh + 64],
                                vts[:, vsl * 1040 + (p * 2 + hh) * 65:vsl * 1040 + (p * 2 + hh) * 65 + 65],
                                start=True, stop=True, tile_position=tp))
                    tdmK[cg] = tdmc
                    tdm_last = tdmc
                    cur = cg % 2
                    nxt = (cg + 1) % 2
                    s.w(V, tdmc, tsnap)
                    s.op(V, lambda e, cur=cur, nxt=nxt: e.tensor_add(
                        srun[:, nxt * 520:nxt * 520 + 260], srun[:, cur * 520:cur * 520 + 260],
                        pf[0][:, 0:260]))
                    tsadd = s.op(V, lambda e, cur=cur, nxt=nxt: e.tensor_add(
                        srun[:, nxt * 520 + 260:nxt * 520 + 520], srun[:, cur * 520 + 260:cur * 520 + 520],
                        pf[1][:, 0:260]))
                    if cg < 31:
                        s.w(SY, tsadd)
                        tsnap = s.dma(SY, lambda e, cg=cg, nxt=nxt: e.dma_start(
                            snap_d[cg + 1], srun[:, nxt * 520:(nxt + 1) * 520]), "sn", flip=True)
                kxdone[sw] = [tevK[sw * KC + KC - 1], tvev]
            s.w(SY, tsadd)
            tst_w = s.dma(SY, lambda e: e.dma_start(ccst_in, srun[:, 0:520]), "ou")
            s.w(G, tst_w)
            if nocc:
                for r in range(NCORE):
                    tcst = s.dma(G, lambda g, r=r: g.dma_start(
                        ccst_out[r * 128:(r + 1) * 128, :], ccst_in), "wq", flip=(r == 0))
            else:
                tcst = s.collective("AllGather", ALU.bypass, [ccst_in], [ccst_out])
            sst = {}

            def emit_sinit():
                s.w(SY, tcst)
                tsi = None
                tsis = {}
                for r in range(NCORE):
                    rl = r % 2
                    if r >= 2:
                        s.w(SY, tsis[r - 2])
                    tbl = s.dma(SY, lambda e, r=r, rl=rl: e.dma_start(
                        snapb[:, rl * 520:(rl + 1) * 520], ccst_out[r * 128:(r + 1) * 128, :]),
                        "ld", flip=True)
                    s.w(V, tbl)
                    if r == 0:
                        tsi = s.op(V, lambda e, rl=rl: e.tensor_scalar(
                            sinit[:, :], snapb[:, rl * 520:(rl + 1) * 520], scf[:, 0:1], None, ALU.mult))
                    else:
                        tsi = s.op(V, lambda e, r=r, rl=rl: e.scalar_tensor_tensor(
                            sinit[:, :], snapb[:, rl * 520:(rl + 1) * 520], scf[:, r:r + 1],
                            sinit[:, :], ALU.mult, ALU.add))
                    tsis[r] = tsi
                sst["tsi"] = tsis[NCORE - 1]
                sst["guard"] = tsis[NCORE - 2]

            s.w(G, tsadd, tdm_last, *s.chtok("st"))
            s.op(G, lambda g: g.memset(tscr[:, 0:256], 1.0))
            s.op(G, lambda g: g.affine_select(mw[:, 0:256], tscr[:, 0:256],
                                              pattern=[[1, 256]], compare_op=ALU.is_ge,
                                              fill=0.0, base=0, channel_multiplier=-1))
            tmkw = s.op(G, lambda g: g.memset(phz[0:64, :], 0.0))
            gdh = {}
            vq = [vts, vts1]
            prev_core = [tsadd, tvev, tfvK[(NSW - 1) * NP + NP - 1]]
            twoprev = None
            lnqA = lnqB = None
            amulQ = {}
            thwQ = {}
            qdone = {-2: kxdone[NSW - 2][0], -1: kxdone[NSW - 1][0]}
            tpe_last = ttr_done
            for sw in range(NSW):
                sl = sw % 2
                txb = None
                for k in range(KC):
                    if k == 0:
                        s.w(G, qdone.get(sw - 2))
                        if sw < 2:
                            s.w(G, *(kxdone.get(NSW - 2 + sw) or []))
                    txb = s.dma(G, lambda g, k=k, sw=sw, sl=sl: g.dma_start(
                        xbw[sl][:, k * SW:(k + 1) * SW], st_a[k][:, sw * SW:(sw + 1) * SW]),
                        "xb", flip=(k == 0))
                txf = None
                for k in range(KC):
                    s.w(SY, amulQ.get((sw - 1) * KC + k))
                    txf = s.dma(SY, lambda e, k=k, sw=sw: e.dma_start(
                        fbufA[:, k * SW:(k + 1) * SW], st_a[k][:, sw * SW:(sw + 1) * SW]),
                        "xf", flip=(k == 0))
                tphk = None
                for p in range(NP):
                    if p == 0:
                        if sw == 0:
                            s.w(SY, tpe_last, *s.chtok("st"))
                        s.w(SY, gdh.get((sw - 1) * 2 + 1))
                    tphk = s.dma(SY, lambda e, p=p, sw=sw: e.dma_start(
                        phkb[:, p * SW:(p + 1) * SW], phk_d[p][:, sw * SW:(sw + 1) * SW]),
                        "ph", flip=(p == 0))
                if sw == 0:
                    emit_sinit()
                tev = {}
                for mc in range(KC):
                    pm = pf[mc % 2]
                    g2 = tev.get(mc - 2)
                    if mc == 0:
                        s.w(TE, *prev_core, twoprev)
                    s.w(TE, txb, g2[1] if g2 else None)
                    tk = None
                    for k in range(KC):
                        tk = s.op(TE, lambda e, mc=mc, k=k, sl=sl, pm=pm: e.matmul(
                            pm[:, 0:SW], wqsb[:, k * 1024 + mc * 128:k * 1024 + mc * 128 + 128],
                            xbw[sl][:, k * SW:(k + 1) * SW], start=(k == 0), stop=(k == KC - 1)))
                    s.w(A, tk)
                    te1 = s.op(A, lambda e, mc=mc, pm=pm, l=l: e.activation(
                        ksb[:, mc * SW:(mc + 1) * SW], pm[:, 0:SW], AF.Identity,
                        bias=pvcol(l, "bq", mc), scale=1.0))
                    te2 = s.op(A, lambda e, mc=mc, pm=pm, l=l: e.activation(
                        k2sb[:, mc * SW:(mc + 1) * SW], pm[:, 0:SW], AF.Square,
                        bias=pvcol(l, "bq", mc), scale=1.0))
                    tev[mc] = (te1, te2)
                teps = {}
                for p in range(NP):
                    pm = pf[4 + p % 2]
                    s.w(TE, tev[p][1], teps.get(p - 2))
                    th = s.op(TE, lambda e, p=p, pm=pm: e.matmul(
                        pm[0:2, 0:SW], hm2[:, :], k2sb[:, p * SW:(p + 1) * SW], start=True, stop=True))
                    s.w(A, th)
                    teps[p] = s.op(A, lambda e, p=p, pm=pm: e.activation(
                        eps2[:, p * SW:(p + 1) * SW], pm[0:2, 0:SW], AF.Exp,
                        bias=biasm[0:2, 1:2], scale=1.0))
                tphq = {}
                tzs = {}
                for p in range(NP):
                    pm = pf[2 + p % 2]
                    s.w(TE, tev[p][0], tphq.get(p - 2))
                    tk = s.op(TE, lambda e, p=p, pm=pm, ob=ob: e.matmul(
                        pm[:, 0:SW], ob, ksb[:, p * SW:(p + 1) * SW], start=True, stop=True))
                    s.w(A, tk)
                    tphq[p] = s.op(A, lambda e, p=p, pm=pm: e.activation(
                        phqb[:, p * SW:(p + 1) * SW], pm[:, 0:SW], AF.Exp,
                        bias=biasm[:, 0:1], scale=1.0))
                    s.w(SY, tphq[p])
                    tzs[p] = s.dma(SY, lambda e, p=p: e.dma_start(
                        phz[64:128, p * SW:(p + 1) * SW], phqb[64:128, p * SW:(p + 1) * SW]),
                        "hw", flip=True)
                ev = {0: [], 1: []}
                for g in range(2):
                    gg2 = sw * 2 + g
                    gl = g
                    s.w(SY, gdh.get(gg2 - 2),
                        sst["guard"] if gg2 == 0 else (sst["tsi"] if gg2 == 1 else None))
                    for kb in range(2):
                        cg = gg2 * 2 + kb
                        s.dma(SY, lambda e, cg=cg, gl=gl, kb=kb: e.dma_start(
                            vq[gl][:, kb * 1040:(kb + 1) * 1040].rearrange("p (i c) -> p i c", c=65)[:, :, 0:64],
                            vraw_d[cg * CH:(cg + 1) * CH, :].rearrange("t (i c) -> t i c", c=64)),
                            "ld", flip=(kb == 0))
                    tsl_ = s.dma(SY, lambda e, gg2=gg2, gl=gl: e.dma_start(
                        snapb[:, gl * 520:(gl + 1) * 520], snap_d[2 * gg2]), "ld")
                    s.w(V, tsl_, gdh.get(gg2 - 1), sst["tsi"])
                    s.op(V, lambda e, gl=gl: e.tensor_add(
                        srq[:, :], snapb[:, gl * 520:(gl + 1) * 520], sinit[:, :]))
                    tce = s.op(V, lambda e: e.tensor_copy(sbe[0:64, :], srq[0:64, :]))
                    tco = s.op(V, lambda e: e.tensor_copy(sbo[64:128, :], srq[64:128, :]))

                    def emit_ph2(dv):
                        (p, pi, co, bN, bB, bD, tb) = dv
                        rq = rows[0:1, 1024 + pi * 512:1024 + (pi + 1) * 512]
                        s.w(A, tb[0], tb[1], lnqA, lnqB)
                        s.op(A, lambda e, rq=rq, bB=bB: e.activation(
                            rq[0:1, 0:256], bB[0:1, 256:512], AF.Ln))
                        s.op(A, lambda e, rq=rq, bD=bD: e.activation(
                            rq[0:1, 256:512], bD[0:1, 256:512], AF.Ln))
                        trd = s.op(A, lambda e, rq=rq: e.activation(
                            rq[0:1, 0:512], rq[0:1, 0:512], AF.Exp, bias=0.0, scale=-1.0))
                        s.w(TE, trd)
                        trb0 = s.op(TE, lambda e, rq=rq, bN=bN: e.matmul(
                            bN[0:64, 256:512], onesf[0:1, 0:64],
                            rq[0:1, 0:256], start=True, stop=True))
                        trb1 = s.op(TE, lambda e, rq=rq, bN=bN: e.matmul(
                            bN[64:128, 256:512], onesf[0:1, 0:64],
                            rq[0:1, 256:512], start=True, stop=True,
                            tile_position=(0, 64)))
                        s.w(A, trb0, trb1)
                        trde = s.op(A, lambda e, pi=pi, bN=bN: e.activation(
                            rdbs2[pi][:, :], bN[:, 256:512], AF.Identity, bias=0.0, scale=1.0))
                        s.w(V, trde)
                        tout = s.op(V, lambda e, p=p, co=co, pi=pi, bN=bN: e.tensor_tensor(
                            asw[:, p * SW + co:p * SW + co + 256], bN[:, 0:256],
                            rdbs2[pi][:, :], ALU.mult))
                        ev[pi] = [tout, trde]
                        return tout

                    pend = None
                    for p in range(NP):
                        pi = p % 2
                        co = g * 256
                        bN = pf[pi]
                        bA = pf[2 + pi]
                        bB = pf[4 + pi]
                        bD = bA
                        s.w(TE, tphq[NP - 1], teps[NP - 1], tce, tco, tphk, trvb0, tmkw,
                            tzs[p], *ev[pi])
                        tat = {}
                        tat[0] = s.op(TE, lambda e, p=p, co=co, bA=bA: e.matmul(
                            bA[:, 0:256],
                            phkb[0:64, p * SW + co:p * SW + co + CH],
                            phqb[0:64, p * SW + co:p * SW + co + 256],
                            start=True, stop=True))
                        tat[2] = s.op(TE, lambda e, p=p, co=co, bB=bB: e.matmul(
                            bB[:, 0:CH],
                            phkb[0:64, p * SW + co + CH:p * SW + co + 2 * CH],
                            phqb[0:64, p * SW + co + CH:p * SW + co + 2 * CH],
                            start=True, stop=True))
                        tat[1] = s.op(TE, lambda e, p=p, co=co, bA=bA: e.matmul(
                            bA[:, 256:512],
                            phkb[:, p * SW + co:p * SW + co + CH],
                            phz[:, p * SW + co:p * SW + co + 256],
                            start=True, stop=True))
                        tat[3] = s.op(TE, lambda e, p=p, co=co, bB=bB: e.matmul(
                            bB[:, CH:2 * CH],
                            phkb[:, p * SW + co + CH:p * SW + co + 2 * CH],
                            phz[:, p * SW + co + CH:p * SW + co + 2 * CH],
                            start=True, stop=True))
                        tmk = {}
                        s.w(V, tat[0], tat[1], tat[2], tat[3])
                        for hh in range(2):
                            tmk[hh] = s.op(V, lambda e, pi=pi, hh=hh, bA=bA: e.tensor_tensor(
                                atk0(pi, hh), bA[:, hh * 256:(hh + 1) * 256], mw[:, 0:256], ALU.mult))
                            tmk[2 + hh] = s.op(V, lambda e, pi=pi, hh=hh, bB=bB: e.tensor_tensor(
                                atk1(pi, hh), bB[:, hh * CH:(hh + 1) * CH], mw[:, 0:CH], ALU.mult))
                        tb = {}
                        s.w(TE, tmk[0], tmk[1], tmk[2], tmk[3])
                        for hh in range(2):
                            tp = None if hh == 0 else (0, 64)
                            orow = slice(64 * hh, 64 * hh + 64)
                            sb = sbe if hh == 0 else sbo
                            s.op(TE, lambda e, p=p, co=co, tp=tp, orow=orow, sb=sb, bN=bN: e.matmul(
                                bN[orow, 0:256], sb[:, p * 65:p * 65 + 64],
                                phqb[:, p * SW + co:p * SW + co + 256],
                                start=True, stop=False, tile_position=tp))
                            s.op(TE, lambda e, p=p, hh=hh, pi=pi, gl=gl, tp=tp, orow=orow, bN=bN: e.matmul(
                                bN[orow, CH:256],
                                vq[gl][:, 1040 + (p * 2 + hh) * 65:1040 + (p * 2 + hh) * 65 + 64],
                                atk1(pi, hh), start=False, stop=False, tile_position=tp))
                            s.op(TE, lambda e, p=p, hh=hh, pi=pi, gl=gl, tp=tp, orow=orow, bN=bN: e.matmul(
                                bN[orow, 0:256],
                                vq[gl][:, (p * 2 + hh) * 65:(p * 2 + hh) * 65 + 64],
                                atk0(pi, hh), start=False, stop=True, tile_position=tp))
                            dn = bB[0:1, 256:512] if hh == 0 else bD[0:1, 256:512]
                            dn1 = bB[0:1, 384:512] if hh == 0 else bD[0:1, 384:512]
                            s.op(TE, lambda e, p=p, co=co, sb=sb, dn=dn: e.matmul(
                                dn, sb[:, p * 65 + 64:p * 65 + 65],
                                phqb[:, p * SW + co:p * SW + co + 256],
                                start=True, stop=False))
                            s.op(TE, lambda e, p=p, hh=hh, pi=pi, gl=gl, dn1=dn1: e.matmul(
                                dn1,
                                vq[gl][:, 1040 + (p * 2 + hh) * 65 + 64:1040 + (p * 2 + hh) * 65 + 65],
                                atk1(pi, hh), start=False, stop=False))
                            s.op(TE, lambda e, p=p, hh=hh, pi=pi, gl=gl, dn=dn: e.matmul(
                                dn,
                                vq[gl][:, (p * 2 + hh) * 65 + 64:(p * 2 + hh) * 65 + 65],
                                atk0(pi, hh), start=False, stop=False))
                            tb[hh] = s.op(TE, lambda e, p=p, hh=hh, co=co, dn=dn: e.matmul(
                                dn, identb[0:2, hh:hh + 1],
                                eps2[:, p * SW + co:p * SW + co + 256], start=False, stop=True))
                        if pend is not None:
                            emit_ph2(pend)
                        pend = (p, pi, co, bN, bB, bD, tb)
                    gdh[gg2] = emit_ph2(pend)
                two = {}
                for mo in range(KC):
                    pm = pf[6] if mo % 2 == 0 else pf[0]
                    if mo == 0:
                        s.w(TE, amulQ.get((sw - 1) * KC + KC - 1))
                    s.w(TE, *ev[0], *ev[1], two.get(mo - 2))
                    tk = None
                    for k in range(KC):
                        tk = s.op(TE, lambda e, mo=mo, k=k, pm=pm: e.matmul(
                            pm[:, 0:SW], wosb[:, k * 1024 + mo * 128:k * 1024 + mo * 128 + 128],
                            asw[:, k * SW:(k + 1) * SW], start=(k == 0), stop=(k == KC - 1)))
                    s.w(A, tk, two.get(mo - 2))
                    ta = s.op(A, lambda e, mo=mo, pm=pm, l=l: e.activation(
                        tscr[:, (mo % 2) * SW:(mo % 2 + 1) * SW], pm[:, 0:SW], AF.Identity,
                        bias=pvcol(l, "bo", mo), scale=1.0))
                    s.w(V, ta, txf)
                    two[mo] = s.op(V, lambda e, mo=mo: e.tensor_add(
                        fbufA[:, mo * SW:(mo + 1) * SW], fbufA[:, mo * SW:(mo + 1) * SW],
                        tscr[:, (mo % 2) * SW:(mo % 2 + 1) * SW]))
                tbA, tbB = emit_ln(pf[6], pbtF, pf[6], pbtF, gates=(two[KC - 1],))
                thw = {-2: thwQ.get((sw - 1) * KC + KC - 2),
                       -1: thwQ.get((sw - 1) * KC + KC - 1)}
                for k in range(KC):
                    kl = k % 2
                    s.w(V, tbA, tbB, thw.get(k - 2))
                    amulQ[sw * KC + k] = s.op(V, lambda e, k=k, kl=kl: e.tensor_mul(
                        tscr[:, kl * SW:(kl + 1) * SW],
                        fbufA[:, k * SW:(k + 1) * SW], pf[6][:, 0:SW]))
                    tt2 = s.op(V, lambda e, kl=kl: e.tensor_add(tscr[:, kl * SW:(kl + 1) * SW],
                                                                tscr[:, kl * SW:(kl + 1) * SW], pbtF[:, 0:SW]))
                    s.w(A, tt2)
                    th = s.op(A, lambda e, k=k, kl=kl, l=l: e.activation(
                        hsl[:, kl * SW:(kl + 1) * SW], tscr[:, kl * SW:(kl + 1) * SW], AF.Identity,
                        bias=pvcol(l, "ln1b", k), scale=pvcol(l, "ln1g", k)))
                    s.w(SY, th)
                    thw[k] = s.dma(SY, lambda e, k=k, sw=sw, kl=kl: e.dma_start(
                        st_h[k][:, sw * SW:(sw + 1) * SW], hsl[:, kl * SW:(kl + 1) * SW]), "hw", flip=True)
                    thwQ[sw * KC + k] = thw[k]
                lnqA, lnqB = tbA, tbB
                twoprev = two[KC - 1]
                prev_core = ev[0] + ev[1]
                qdone[sw] = tev[KC - 1][1]
            s.barrier()

            tW1s = {}
            for j in range(MJ):
                tW1s[j] = s.dma(G, lambda g, j=j, l=l: g.dma_start(
                    w1sb[:, j * 1024:(j + 1) * 1024].rearrange("p (kc i) -> p kc i", i=128),
                    w1_in[l][:, j * 128:(j + 1) * 128].rearrange("(kc p) i -> p kc i", p=128)),
                    "wf", flip=True)
            tW2s = {}
            for mo in range(KC):
                tW2s[mo] = s.dma(G, lambda g, mo=mo, l=l: g.dma_start(
                    w2sb[:, mo * 4096:(mo + 1) * 4096].rearrange("p (j i) -> p j i", i=128),
                    w2_in[l][:, mo * 128:(mo + 1) * 128].rearrange("(j p) i -> p j i", p=128)),
                    "wf", flip=True)
            treF = {}
            tyaddF = {}
            thwF = {}
            fblast = {}
            for sw in range(NSW):
                thb = None
                for k in range(KC):
                    if k == 0:
                        s.w(G, treF.get((sw - 1) * MJ + MJ - 1))
                    thb = s.dma(G, lambda g, k=k, sw=sw: g.dma_start(
                        hbf[:, k * SW:(k + 1) * SW], st_h[k][:, sw * SW:(sw + 1) * SW]),
                        "xb", flip=(k == 0))
                txf = None
                for k in range(KC):
                    s.w(SY, fblast.get((sw - 1) * KC + k))
                    txf = s.dma(SY, lambda e, k=k, sw=sw: e.dma_start(
                        fbufA[:, k * SW:(k + 1) * SW], st_h[k][:, sw * SW:(sw + 1) * SW]),
                        "xf", flip=(k == 0))
                tyadd = {-2: tyaddF.get((sw - 1) * KC + KC - 2),
                         -1: tyaddF.get((sw - 1) * KC + KC - 1)}
                tre = {-2: treF.get((sw - 1) * MJ + MJ - 2),
                       -1: treF.get((sw - 1) * MJ + MJ - 1)}
                for j in range(MJ):
                    pm = pf[j % 2]
                    s.w(TE, tW1s[j], thb, tre.get(j - 2), tyadd.get(-1))
                    tk = None
                    for k in range(KC):
                        tk = s.op(TE, lambda e, j=j, k=k, pm=pm: e.matmul(
                            pm[:, 0:TG], w1sb[:, j * 1024 + k * 128:j * 1024 + k * 128 + 128],
                            hbf[:, k * SW:k * SW + TG], start=(k == 0), stop=(k == KC - 1)))
                    s.w(A, tk)
                    tre[j] = s.op(A, lambda e, j=j, pm=pm, l=l: e.activation(
                        h1f[:, j * TG:(j + 1) * TG], pm[:, 0:TG], AF.Relu,
                        bias=pvcol(l, "bf1", j), scale=1.0))
                    treF[sw * MJ + j] = tre[j]
                for mo in range(KC):
                    pm = pf[2 + mo % 2]
                    s.w(TE, tW2s[mo], tre[MJ - 1], tyadd.get(mo - 2))
                    if mo < 2:
                        s.w(A, *(thwF.get((sw - 1) * KC + KC - 2 + mo) or []))
                    tk = None
                    for j in range(MJ):
                        tk = s.op(TE, lambda e, mo=mo, j=j, pm=pm: e.matmul(
                            pm[:, 0:TG], w2sb[:, mo * 4096 + j * 128:mo * 4096 + j * 128 + 128],
                            h1f[:, j * TG:(j + 1) * TG], start=(j == 0), stop=(j == MJ - 1)))
                    s.w(A, tk, tyadd.get(mo - 2))
                    ta = s.op(A, lambda e, mo=mo, pm=pm, l=l: e.activation(
                        tscr[:, (mo % 2) * SW:(mo % 2) * SW + TG],
                        pm[:, 0:TG], AF.Identity, bias=pvcol(l, "bf2", mo), scale=1.0))
                    s.w(V, ta, txf)
                    tyadd[mo] = s.op(V, lambda e, mo=mo: e.tensor_add(
                        fbufA[:, mo * SW:mo * SW + TG],
                        fbufA[:, mo * SW:mo * SW + TG],
                        tscr[:, (mo % 2) * SW:(mo % 2) * SW + TG]))
                    tyaddF[sw * KC + mo] = tyadd[mo]
                gates = [tyadd[mo] for mo in range(KC)]
                tbA, tbB = emit_ln(pf[4], pf[5], pf[4], pf[5], gates=gates)
                thw = {-2: thwF.get((sw - 1) * KC + KC - 2),
                       -1: thwF.get((sw - 1) * KC + KC - 1)}
                for k in range(KC):
                    kl = k % 2
                    s.w(V, tbA, tbB, *(thw.get(k - 2) or []))
                    t1k = s.op(V, lambda e, k=k, kl=kl: e.tensor_mul(tscr[:, kl * SW:(kl + 1) * SW],
                                                                    fbufA[:, k * SW:(k + 1) * SW], pf[4][:, 0:SW]))
                    tt2 = s.op(V, lambda e, kl=kl: e.tensor_add(tscr[:, kl * SW:(kl + 1) * SW],
                                                                tscr[:, kl * SW:(kl + 1) * SW], pf[5][:, 0:SW]))
                    s.w(A, tt2)
                    th = s.op(A, lambda e, k=k, kl=kl, l=l: e.activation(
                        hsl[:, kl * SW:(kl + 1) * SW], tscr[:, kl * SW:(kl + 1) * SW], AF.Identity,
                        bias=pvcol(l, "ln2b", k), scale=pvcol(l, "ln2g", k)))
                    if l < L - 1:
                        s.w(SY, th)
                        thw[k] = [s.dma(SY, lambda e, k=k, sw=sw, kl=kl: e.dma_start(
                            st_a[k][:, sw * SW:(sw + 1) * SW], hsl[:, kl * SW:(kl + 1) * SW]),
                            "hw", flip=True)]
                        thwF[sw * KC + k] = thw[k]
                        fblast[sw * KC + k] = t1k
                    else:
                        s.w(SY, t1k)
                        tx1l = s.dma(SY, lambda e, k=k, sw=sw: e.dma_start(
                            fbufA[:, k * SW:(k + 1) * SW], st_x1[k][:, sw * SW:(sw + 1) * SW]),
                            "ld", flip=True)
                        s.w(V, th, tx1l)
                        tx2 = s.op(V, lambda e, k=k, kl=kl: e.tensor_add(
                            tscr[:, kl * SW:(kl + 1) * SW], hsl[:, kl * SW:(kl + 1) * SW],
                            fbufA[:, k * SW:(k + 1) * SW]))
                        s.w(SY, tx2)
                        tyd = s.dma(SY, lambda e, k=k, sw=sw, kl=kl: e.dma_start(
                            yshard[k][:, sw * SW:(sw + 1) * SW], tscr[:, kl * SW:(kl + 1) * SW]),
                            "ou", flip=True)
                        tvm = s.op(V, lambda e, kl=kl: e.tensor_scalar(
                            hsl[:, kl * SW:(kl + 1) * SW], tscr[:, kl * SW:(kl + 1) * SW],
                            m0[:, 0:1], None, ALU.mult))
                        s.w(SY, tvm)
                        tcxk = s.dma(SY, lambda e, k=k, sw=sw, kl=kl: e.dma_start(
                            ccx2_in[sw * SW + (k // 2) * 128:sw * SW + (k // 2) * 128 + 128,
                                    (k % 2) * SW:(k % 2 + 1) * SW],
                            hsl[:, kl * SW:(kl + 1) * SW]), "hw", flip=True)
                        thw[k] = [tyd, tcxk]
                        thwF[sw * KC + k] = thw[k]
                        fblast[sw * KC + k] = tx2
            s.barrier()

        s.w(G, *s.chtok("hw"))
        if nocc:
            tccx = s.dma(G, lambda g: g.dma_start(ccx2_out, ccx2_in[0:MTOK, :]), "wq", flip=True)
        else:
            tccx = s.collective("ReduceScatter", ALU.add, [ccx2_in], [ccx2_out])

        tewb = build_ewbs(s.token(V))
        s.w(SY, tccx)
        tx5 = None
        for k in range(KC):
            tx5 = s.dma(SY, lambda e, k=k: e.dma_start(
                fbufA[:, k * SW:(k + 1) * SW],
                ccx2_out[(k // 2) * 128:(k // 2) * 128 + 128, (k % 2) * SW:(k % 2 + 1) * SW]),
                "xf", flip=(k == 0))
        tbA, tbB = emit_ln(pf[4], pf[5], pf[5], pf[6], gates=(tx5, tewb))
        tgxg = None
        for k in range(KC):
            kl = k % 2
            s.w(V, tbA, tbB, tgxg)
            s.op(V, lambda e, k=k, kl=kl: e.tensor_mul(hsl[:, kl * SW:(kl + 1) * SW],
                                                       fbufA[:, k * SW:(k + 1) * SW], pf[5][:, 0:SW]))
            tt2 = s.op(V, lambda e, kl=kl: e.tensor_add(hsl[:, kl * SW:(kl + 1) * SW],
                                                        hsl[:, kl * SW:(kl + 1) * SW], pf[6][:, 0:SW]))
            s.w(A, tt2)
            tgxg = s.op(A, lambda e, k=k, kl=kl: e.activation(
                xg[:, k * SW:(k + 1) * SW], hsl[:, kl * SW:(kl + 1) * SW], AF.Identity, bias=0.0, scale=1.0))

        ydma = {}

        def wr_moe3(mo, td, src):
            s.w(V, td, ydma.get(mo - 2))
            ty = s.op(V, lambda e, mo=mo, src=src: e.tensor_add(
                hsl[:, (mo % 2) * SW:(mo % 2 + 1) * SW],
                fbufA[:, mo * SW:(mo + 1) * SW], src))
            s.w(SY, ty)
            ydma[mo] = s.dma(SY, lambda e, mo=mo: e.dma_start(
                ymoe3[mo], hsl[:, (mo % 2) * SW:(mo % 2 + 1) * SW]), "hw", flip=True)
            return ydma[mo]
        emit_moe(tgxg, wr_moe3)

        s.drain_stores()
        s.op(SY, lambda e: e.nop())
        s.replay(block)
    return nc


_NC_CACHE = {}


def kernel(L0_attn_x, L0_attn_Wq, L0_attn_bq, L0_attn_Wk, L0_attn_bk,
           L0_attn_Wv, L0_attn_bv, L0_attn_Wo, L0_attn_bo, L0_attn_omega,
           L0_attn_ln1_g, L0_attn_ln1_b, L0_attn_ln2_g, L0_attn_ln2_b,
           L0_attn_W1, L0_attn_bf1, L0_attn_W2, L0_attn_bf2,
           L0_ffn_scores, L0_ffn_expert_weights, L0_ffn_top_experts,
           L0_ffn_w1, L0_ffn_w2):
    x = np.ascontiguousarray(np.asarray(L0_attn_x, np.float32))
    B, S, _ = x.shape

    def fm(a):
        return np.ascontiguousarray(np.asarray(a, np.float32).T.reshape(KC, 128, -1))

    def cols(v):
        return np.asarray(v, np.float32).reshape(8, 128).T

    pvec = np.zeros((128, 312), np.float32)
    for l in range(L):
        base = l * 104
        pvec[:, base + 0:base + 8] = cols(L0_attn_bq[l])
        pvec[:, base + 8:base + 16] = cols(L0_attn_bk[l])
        pvec[:, base + 16:base + 24] = cols(L0_attn_bv[l])
        pvec[:, base + 24:base + 32] = cols(L0_attn_bo[l])
        pvec[:, base + 32:base + 40] = cols(L0_attn_ln1_g[l])
        pvec[:, base + 40:base + 48] = cols(L0_attn_ln1_b[l])
        pvec[:, base + 48:base + 56] = cols(L0_attn_ln2_g[l])
        pvec[:, base + 56:base + 64] = cols(L0_attn_ln2_b[l])
        pvec[:, base + 64:base + 72] = cols(L0_attn_bf2[l])
        pvec[:, base + 72:base + 104] = np.asarray(L0_attn_bf1[l], np.float32).reshape(32, 128).T
    rowv = np.ascontiguousarray(np.asarray(L0_attn_bv, np.float32).reshape(1, L * D))
    ew = np.asarray(L0_ffn_expert_weights, np.float32)

    if "nc" not in _NC_CACHE:
        _NC_CACHE["nc"] = build()
    nc = _NC_CACHE["nc"]

    in_maps = []
    for i in range(NCORE):
        b, half = i // 2, i % 2
        m0v = np.full((128, 1), 1.0 if i == 0 else 0.0, np.float32)
        sc = np.zeros((128, 8), np.float32)
        if i % 2 == 1:
            sc[:, i - 1] = 1.0
        in_maps.append({
            "xshard": fm(x[b, half * T:(half + 1) * T, :]),
            "xmoe": fm(x[0, i * MTOK:(i + 1) * MTOK, :]),
            "ewT": np.ascontiguousarray(ew[i * MTOK:(i + 1) * MTOK, :].T),
            "pvec": pvec, "rowv": rowv,
            "omg_in": np.asarray(L0_attn_omega, np.float32),
            "wq_in": np.asarray(L0_attn_Wq, np.float32),
            "wk_in": np.asarray(L0_attn_Wk, np.float32),
            "wv_in": np.asarray(L0_attn_Wv, np.float32),
            "wo_in": np.asarray(L0_attn_Wo, np.float32),
            "w1_in": np.asarray(L0_attn_W1, np.float32),
            "w2_in": np.asarray(L0_attn_W2, np.float32),
            "mw1_in": np.asarray(L0_ffn_w1, np.float32),
            "mw2_in": np.asarray(L0_ffn_w2, np.float32),
            "mask0": m0v, "scoef": sc,
        })
    res = run_bass_kernel_spmd(nc, in_maps, core_ids=list(range(NCORE)))
    out = np.empty((B, S, D), np.float32)
    for i in range(NCORE):
        b, half = i // 2, i % 2
        ys = np.asarray(res.results[i]["yshard"])
        out[b, half * T:(half + 1) * T, :] = ys.transpose(2, 0, 1).reshape(T, D)
    for i in range(NCORE):
        ym = np.asarray(res.results[i]["ymoe3"])
        out[0, i * MTOK:(i + 1) * MTOK, :] = ym.transpose(2, 0, 1).reshape(MTOK, D)
    return out


# version 27
# speedup vs baseline: 1.0096x; 1.0096x over previous
import numpy as np
import concourse.bass as bass
import concourse.mybir as mybir
from concourse.bass_utils import run_bass_kernel_spmd

F32 = mybir.dt.float32
BF16 = mybir.dt.bfloat16
AF = mybir.ActivationFunctionType
ALU = mybir.AluOpType

NCORE = 8
D = 1024
KC = 8
T = 4096
NSW = 8
SW = 512
CH = 128
NCHS = 4
NP = 8
DH = 64
M = 32
DFF = 4096
MJ = 32
MOE_I = 8192
MOE_J = 64
MTOK = 512
L = 3
LN_EPS = 1e-5
ATTN_EPS = 1e-6
SC = float(DH) ** -0.25
CSQ = 0.0625
LOGM = float(np.log((2.0 * M) ** -0.5))
LOGEPS = float(np.log(ATTN_EPS))
GC1 = 0.044715
GC2 = float(2.0 * np.sqrt(2.0 / np.pi))
RG = [[0, 1, 2, 3, 4, 5, 6, 7]]
TG = 512
LN_V2 = False

ENGINES = ("tensor", "vector", "scalar", "gpsimd", "sync")
G, V, A, TE, SY = "gpsimd", "vector", "scalar", "tensor", "sync"


class Sched:
    AUTO_CHAIN = ("vector", "scalar", "gpsimd")

    def __init__(self, nc, ctx):
        self.nc = nc
        self.ops = {e: [] for e in ENGINES}
        self.cnt = {}
        self.last_waited = {e: {} for e in ENGINES}
        self.sems = {}
        self._ctx = ctx
        for e in ENGINES:
            self.sems[e] = ctx.enter_context(nc.semaphore(f"s_{e}"))
            self.cnt[e] = 0
        self.chan = {}

    def op(self, eng, fn):
        sem = self.sems[eng]
        if eng in self.AUTO_CHAIN and self.cnt[eng] > 0:
            prev = self.cnt[eng]
            if self.last_waited[eng].get(eng, 0) < prev:
                self.last_waited[eng][eng] = prev
                def emitw(e, sem=sem, val=prev):
                    e.wait_ge(sem, val)
                self.ops[eng].append(emitw)
        self.cnt[eng] += 1
        want = self.cnt[eng]
        def emit(e, fn=fn, sem=sem):
            fn(e).then_inc(sem, 1)
        self.ops[eng].append(emit)
        return (eng, want)

    def mkchan(self, name):
        sems = [self._ctx.enter_context(self.nc.semaphore(f"c_{name}{i}"))
                for i in range(2)]
        self.chan[name] = {"sems": sems, "slot": 0}
        for i in range(2):
            self.sems[f"{name}{i}"] = sems[i]
            self.cnt[f"{name}{i}"] = 0

    def dma(self, eng, fn, ch, flip=False):
        c = self.chan[ch]
        if flip:
            c["slot"] ^= 1
        slot = c["slot"]
        key = f"{ch}{slot}"
        sem = c["sems"][slot]
        if flip and self.cnt[key] > 0:
            prev = self.cnt[key]
            if self.last_waited[eng].get(key, 0) < prev:
                self.last_waited[eng][key] = prev
                def emitw(e, sem=sem, val=prev):
                    e.wait_ge(sem, val)
                self.ops[eng].append(emitw)
        self.cnt[key] += 16
        want = self.cnt[key]
        def emit(e, fn=fn, sem=sem):
            fn(e).then_inc(sem, 16)
        self.ops[eng].append(emit)
        return (key, want)

    def chtok(self, ch):
        return [(f"{ch}{i}", self.cnt[f"{ch}{i}"]) for i in range(2)]

    def collective(self, kind, op, ins, outs):
        name = f"cc{len([k for k in self.sems if k.startswith('cc')])}"
        sem = self._ctx.enter_context(self.nc.semaphore(name))
        self.sems[name] = sem
        self.cnt[name] = 1
        def emit(e, sem=sem):
            e.collective_compute(kind, op, replica_groups=RG,
                                 ins=[i.opt() for i in ins],
                                 outs=[o.opt() for o in outs]).then_inc(sem)
        self.ops["gpsimd"].append(emit)
        return (name, 1)

    def w(self, eng, *tokens):
        for t in tokens:
            if t is None:
                continue
            (src, val) = t
            if src == eng:
                continue
            if val <= self.last_waited[eng].get(src, 0):
                continue
            self.last_waited[eng][src] = val
            sem = self.sems[src]
            def emit(e, sem=sem, val=val):
                e.wait_ge(sem, val)
            self.ops[eng].append(emit)

    def token(self, eng):
        return (eng, self.cnt[eng])

    def drain_stores(self):
        self.w(SY, *self.chtok("st"), *self.chtok("ou"),
               *self.chtok("hw"), *self.chtok("sn"))

    def barrier(self):
        self.drain_stores()
        self.op(SY, lambda e: e.nop())
        toks = [self.token(e) for e in ENGINES]
        for e in ENGINES:
            self.w(e, *toks)

    def replay(self, block):
        for e in ENGINES:
            ops = self.ops[e]
            def mk(ops):
                def run(engine):
                    for fn in ops:
                        fn(engine)
                return run
            getattr(block, e)(mk(ops))


def build(nocc=False):
    import contextlib
    nc = bass.Bass("TRN2", target_bir_lowering=False)
    DT = nc.dram_tensor
    xshard = DT("xshard", [KC, 128, T], F32, kind="ExternalInput").ap()
    xmoe = DT("xmoe", [KC, 128, MTOK], F32, kind="ExternalInput").ap()
    ewT = DT("ewT", [2, MTOK], F32, kind="ExternalInput").ap()
    pvec = DT("pvec", [128, 312], F32, kind="ExternalInput").ap()
    rowv = DT("rowv", [1, L * D], F32, kind="ExternalInput").ap()
    omg_in = DT("omg_in", [L, DH, M], F32, kind="ExternalInput").ap()
    wq_in = DT("wq_in", [L, D, D], F32, kind="ExternalInput").ap()
    wk_in = DT("wk_in", [L, D, D], F32, kind="ExternalInput").ap()
    wv_in = DT("wv_in", [L, D, D], F32, kind="ExternalInput").ap()
    wo_in = DT("wo_in", [L, D, D], F32, kind="ExternalInput").ap()
    w1_in = DT("w1_in", [L, D, DFF], F32, kind="ExternalInput").ap()
    w2_in = DT("w2_in", [L, DFF, D], F32, kind="ExternalInput").ap()
    mw1_in = DT("mw1_in", [D, MOE_I], F32, kind="ExternalInput").ap()
    mw2_in = DT("mw2_in", [MOE_I, D], F32, kind="ExternalInput").ap()
    mask0 = DT("mask0", [128, 1], F32, kind="ExternalInput").ap()
    scoef = DT("scoef", [128, 8], F32, kind="ExternalInput").ap()
    yshard = DT("yshard", [KC, 128, T], F32, kind="ExternalOutput").ap()
    ymoe3 = DT("ymoe3", [KC, 128, MTOK], F32, kind="ExternalOutput").ap()
    st_x1 = DT("st_x1", [KC, 128, T], F32).ap()
    st_a = DT("st_a", [KC, 128, T], F32).ap()
    st_h = DT("st_h", [KC, 128, T], F32).ap()
    phk_d = DT("phk_d", [NP, 128, T], BF16).ap()
    vraw_d = DT("vraw_d", [T, D], BF16).ap()
    snap_d = DT("snap_d", [32, 128, 520], F32).ap()
    ccm1_in = DT("ccm1_in", [128, KC * SW], F32).ap()
    ccm1_out = DT("ccm1_out", [NCORE * 128, KC * SW], F32,
                  addr_space=("Local" if nocc else "Shared")).ap()
    ccst_in = DT("ccst_in", [128, 520], F32).ap()
    ccst_out = DT("ccst_out", [NCORE * 128, 520], F32,
                  addr_space=("Local" if nocc else "Shared")).ap()
    ccx2_in = DT("ccx2_in", [T, D], F32).ap()
    ccx2_out = DT("ccx2_out", [MTOK, D], F32).ap()

    ctx = contextlib.ExitStack()
    with ctx:
        Sb = lambda *a: ctx.enter_context(nc.sbuf_tensor(*a))
        Ps = lambda *a: ctx.enter_context(nc.psum_tensor(*a))
        pv = Sb("pv", [128, 312], F32)
        rvb = Sb("rvb", [1, D], BF16)
        omgs = Sb("omgs", [128, L * M], F32)
        obs = Sb("obs", [128, L * 128], BF16)
        cbt = Sb("cbt", [128, 128], BF16)
        hm2 = Sb("hm2", [128, 2], BF16)
        identb = Sb("identb", [128, 128], BF16)
        maskf = Sb("maskf", [128, 128], F32)
        onesf = Sb("onesf", [128, 128], F32)
        onesbr = Sb("onesbr", [1, 128], BF16)
        halfo = Sb("halfo", [2, 1], F32)
        biasm = Sb("biasm", [128, 2], F32)
        m0 = Sb("m0", [128, 1], F32)
        scf = Sb("scf", [128, 8], F32)
        fbufA = Sb("fbufA", [128, KC * SW], F32)
        tscr = Sb("tscr", [128, 2 * SW], F32)
        hsl = Sb("hsl", [128, 2 * SW], F32)
        rows = Sb("rows", [1, 4 * SW], F32)
        wreg = Sb("wreg", [128, 65536], BF16)
        arena = Sb("arena", [128, 20480], BF16)
        srun = arena[:, 0:2080].bitcast(F32)
        sinit = arena[:, 2080:3120].bitcast(F32)
        srq = arena[:, 3120:4160].bitcast(F32)
        snapb = arena[:, 4160:6240].bitcast(F32)
        sbe = arena[:, 6240:6760]
        sbo = arena[:, 6760:7280]
        vts = arena[:, 7280:9360]
        phkt = arena[:, 9360:10384]
        eps2 = arena[0:2, 10384:14480]
        h1f = arena[:, 0:16384]
        hbf = arena[:, 16384:20480]
        ATMB = 14480
        atk0 = lambda pi, hh: arena[:, ATMB + pi * 768 + hh * 256:ATMB + pi * 768 + (hh + 1) * 256]
        atk1 = lambda pi, hh: arena[:, ATMB + pi * 768 + 512 + hh * 128:ATMB + pi * 768 + 512 + (hh + 1) * 128]
        vts1 = arena[:, 16528:18608]
        rdbs2 = [arena[:, 16016:16528].bitcast(F32),
                 arena[:, 19632:20144].bitcast(F32)]
        mw = arena[:, 9360:10384].bitcast(F32)
        sq = arena[:, 14480:15504].bitcast(F32)
        ewbs = arena[:, 15504:16528].bitcast(F32)
        ewts = arena[0:2, 16528:17552].bitcast(F32)
        idf = fbufA[:, 0:128]
        rvf = fbufA[0:1, SW:SW + D]

        wqsb = wreg[:, 0:8192]
        wksb = wreg[:, 8192:16384]
        wvsb = wreg[:, 16384:24576]
        wosb = wreg[:, 24576:32768]
        xbw = [wreg[:, 32768 + i * 4096:32768 + (i + 1) * 4096] for i in range(2)]
        ksb = wreg[:, 40960:45056]
        k2sb = wreg[:, 45056:49152]
        phkb = wreg[:, 49152:53248]
        phqb = wreg[:, 53248:57344]
        asw = wreg[:, 57344:61440]
        vstage = wreg[:, 61440:63488]
        phz = wreg[:, 61440:65536]
        w1sb = wreg[:, 0:32768]
        w2sb = wreg[:, 32768:65536]
        mw1b = [wreg[:, i * 1024:(i + 1) * 1024] for i in range(2)]
        mw2b = [wreg[:, 4096 + i * 8192:4096 + (i + 1) * 8192] for i in range(2)]
        h1m = wreg[:, 20480:53248]
        xg = wreg[:, 53248:57344]

        pf = [Ps(f"pf{i}", [128, 512], F32) for i in range(7)]
        pbt = Ps("pbt", [128, 1024], BF16)
        pbtF = pbt[:, 0:1024].bitcast(F32)

        s = Sched(nc, ctx)
        for chn in ("wq", "wf", "xb", "xf", "st", "ld", "ou", "ph", "sn", "hw"):
            s.mkchan(chn)
        block = ctx.enter_context(nc.Block())

        def pvcol(l, name, k):
            off = {"bq": 0, "bk": 8, "bv": 16, "bo": 24, "ln1g": 32, "ln1b": 40,
                   "ln2g": 48, "ln2b": 56, "bf2": 64, "bf1": 72}[name]
            cidx = l * 104 + off + k
            return pv[:, cidx:cidx + 1]

        def do_cc(kind, op, cin, cout, nrow):
            if nocc:
                return s.dma(G, lambda g: g.dma_start(cout[0:nrow, :], cin), "ou")
            return s.collective(kind, op, [cin], [cout])

        s.dma(G, lambda g: g.dma_start(pv[:, :], pvec), "wq")
        s.dma(G, lambda g: g.dma_start(m0[:, :], mask0), "wq")
        s.dma(G, lambda g: g.dma_start(scf[:, :], scoef), "wq")
        tldw = None
        for l in range(L):
            s.dma(G, lambda g, l=l: g.dma_start(omgs[0:64, l * M:(l + 1) * M], omg_in[l]), "wq")
            tldw = s.dma(G, lambda g, l=l: g.dma_start(omgs[64:128, l * M:(l + 1) * M], omg_in[l]), "wq")
        s.op(G, lambda g: g.memset(onesf[:, :], 1.0))
        s.op(G, lambda g: g.memset(onesbr[:, :], 1.0))
        s.op(G, lambda g: g.memset(halfo[:, :], 0.5))
        s.op(G, lambda g: g.memset(biasm[:, 0:1], LOGM))
        s.op(G, lambda g: g.memset(biasm[:, 1:2], LOGEPS))
        s.op(G, lambda g: g.memset(cbt[:, :], 0.0))
        s.op(G, lambda g: g.memset(hm2[:, :], 0.0))
        s.op(G, lambda g: g.memset(obs[:, :], 0.0))
        s.op(G, lambda g: g.memset(sbe[:, :], 0.0))
        s.op(G, lambda g: g.memset(sbo[:, :], 0.0))
        for sl in range(2):
            for i in range(16):
                s.op(G, lambda g, sl=sl, i=i: g.memset(
                    vts[:, sl * 1040 + i * 65 + 64:sl * 1040 + i * 65 + 65], 1.0))
        s.op(G, lambda g: g.affine_select(idf[:, :], onesf[:, :], pattern=[[1, 128]],
                                          compare_op=ALU.is_equal, fill=0.0,
                                          base=0, channel_multiplier=-1))
        tgm = s.op(G, lambda g: g.affine_select(maskf[:, :], onesf[:, :], pattern=[[1, 128]],
                                               compare_op=ALU.is_ge, fill=0.0,
                                               base=0, channel_multiplier=-1))
        s.w(A, tgm, tldw)
        s.op(A, lambda e: e.activation(identb[:, :], idf[:, :], AF.Identity, bias=0.0, scale=1.0))
        s.op(A, lambda e: e.activation(cbt[0:64, 0:64], onesf[0:64, 0:64], AF.Identity, bias=0.0, scale=-CSQ))
        s.op(A, lambda e: e.activation(cbt[64:128, 64:128], onesf[64:128, 0:64], AF.Identity, bias=0.0, scale=-CSQ))
        s.op(A, lambda e: e.activation(hm2[0:64, 0:1], onesf[0:64, 0:1], AF.Identity, bias=0.0, scale=CSQ))
        s.op(A, lambda e: e.activation(hm2[64:128, 1:2], onesf[64:128, 0:1], AF.Identity, bias=0.0, scale=CSQ))
        tcons = None
        for l in range(L):
            ob = obs[:, l * 128:(l + 1) * 128]
            om0 = omgs[0:64, l * M:(l + 1) * M]
            om1 = omgs[64:128, l * M:(l + 1) * M]
            s.op(A, lambda e, ob=ob, om0=om0: e.activation(ob[0:64, 0:32], om0, AF.Identity, bias=0.0, scale=SC))
            s.op(A, lambda e, ob=ob, om0=om0: e.activation(ob[0:64, 32:64], om0, AF.Identity, bias=0.0, scale=-SC))
            s.op(A, lambda e, ob=ob, om1=om1: e.activation(ob[64:128, 64:96], om1, AF.Identity, bias=0.0, scale=SC))
            tcons = s.op(A, lambda e, ob=ob, om1=om1: e.activation(ob[64:128, 96:128], om1, AF.Identity, bias=0.0, scale=-SC))

        def build_ewbs(gate):
            tld = s.dma(G, lambda g: g.dma_start(ewts[:, :], ewT), "wq", flip=True)
            s.w(TE, tld, gate, tcons)
            t1 = s.op(TE, lambda e: e.matmul(pf[4][0:1, 0:MTOK], halfo[:, :], ewts[:, :], start=True, stop=True))
            s.w(V, t1)
            t2 = s.op(V, lambda e: e.tensor_copy(rows[0:1, 0:MTOK], pf[4][0:1, 0:MTOK]))
            s.w(TE, t2)
            t3 = s.op(TE, lambda e: e.matmul(pf[6][:, 0:MTOK], onesf[0:1, :], rows[0:1, 0:MTOK], start=True, stop=True))
            s.w(A, t3)
            return s.op(A, lambda e: e.activation(ewbs[:, :], pf[6][:, 0:MTOK], AF.Identity, bias=0.0, scale=1.0))

        def emit_ln_v1(rbA, rbB, pbA, pbB, gates=()):
            s.w(V, *gates)
            s.w(A, *gates)
            tv = s.op(V, lambda e: e.tensor_add(tscr[:, 0:SW], fbufA[:, 0:SW], fbufA[:, SW:2 * SW]))
            for k in range(2, KC):
                tv = s.op(V, lambda e, k=k: e.tensor_add(tscr[:, 0:SW], tscr[:, 0:SW], fbufA[:, k * SW:(k + 1) * SW]))
            ta = s.op(A, lambda e: e.activation(sq[:, :], fbufA[:, 0:SW], AF.Square))
            s.w(V, ta)
            tq = s.op(V, lambda e: e.tensor_copy(tscr[:, SW:2 * SW], sq[:, :]))
            for k in range(1, KC):
                s.w(A, tq)
                ta = s.op(A, lambda e, k=k: e.activation(sq[:, :], fbufA[:, k * SW:(k + 1) * SW], AF.Square))
                s.w(V, ta)
                tq = s.op(V, lambda e: e.tensor_add(tscr[:, SW:2 * SW], tscr[:, SW:2 * SW], sq[:, :]))
            s.w(TE, tq, tv)
            tm1 = s.op(TE, lambda e, rbA=rbA: e.matmul(rbA[0:1, 0:SW], onesf[:, 0:1], tscr[:, 0:SW], start=True, stop=True))
            tm2 = s.op(TE, lambda e, rbB=rbB: e.matmul(rbB[0:1, 0:SW], onesf[:, 0:1], tscr[:, SW:2 * SW], start=True, stop=True))
            R = lambda i: rows[0:1, i * SW:(i + 1) * SW]
            s.w(V, tm1, tm2)
            s.op(V, lambda e, rbA=rbA: e.tensor_scalar(R(0), rbA[0:1, 0:SW], 1.0 / D, None, ALU.mult))
            tr0 = s.op(V, lambda e, rbB=rbB: e.tensor_scalar(R(1), rbB[0:1, 0:SW], 1.0 / D, LN_EPS, ALU.mult, ALU.add))
            s.w(A, tr0)
            ta1 = s.op(A, lambda e: e.activation(R(2), R(0), AF.Square))
            s.w(V, ta1)
            tvv = s.op(V, lambda e: e.tensor_sub(R(1), R(1), R(2)))
            s.w(A, tvv)
            s.op(A, lambda e: e.activation(R(2), R(1), AF.Ln))
            tsq2 = s.op(A, lambda e: e.activation(R(3), R(2), AF.Exp, bias=0.0, scale=-0.5))
            s.w(V, tsq2)
            s.op(V, lambda e: e.tensor_mul(R(2), R(0), R(3)))
            trB = s.op(V, lambda e: e.tensor_scalar(R(2), R(2), -1.0, None, ALU.mult))
            s.w(TE, trB, tsq2)
            tbA = s.op(TE, lambda e, pbA=pbA: e.matmul(pbA[:, 0:SW], onesf[0:1, :], R(3), start=True, stop=True))
            tbB = s.op(TE, lambda e, pbB=pbB: e.matmul(pbB[:, 0:SW], onesf[0:1, :], R(2), start=True, stop=True))
            return tbA, tbB

        def emit_ln(rbA, rbB, pbA, pbB, gates=()):
            s.w(TE, *gates)
            s.w(A, *gates)
            for k in range(KC):
                s.op(TE, lambda e, k=k, rbA=rbA: e.matmul(
                    rbA[0:1, 0:SW], onesf[:, 0:1], fbufA[:, k * SW:(k + 1) * SW],
                    start=(k == 0), stop=(k == KC - 1)))
            tmq = {}
            for k in range(KC):
                kl = k % 2
                if k >= 2:
                    s.w(A, tmq[k - 2])
                tsq = s.op(A, lambda e, k=k, kl=kl: e.activation(
                    tscr[:, kl * SW:(kl + 1) * SW], fbufA[:, k * SW:(k + 1) * SW], AF.Square))
                s.w(TE, tsq)
                tmq[k] = s.op(TE, lambda e, k=k, kl=kl, rbB=rbB: e.matmul(
                    rbB[0:1, 0:SW], onesf[:, 0:1], tscr[:, kl * SW:(kl + 1) * SW],
                    start=(k == 0), stop=(k == KC - 1)))
            R = lambda i: rows[0:1, i * SW:(i + 1) * SW]
            s.w(V, tmq[KC - 1])
            s.op(V, lambda e, rbA=rbA: e.tensor_scalar(R(0), rbA[0:1, 0:SW], 1.0 / D, None, ALU.mult))
            tr0 = s.op(V, lambda e, rbB=rbB: e.tensor_scalar(R(1), rbB[0:1, 0:SW], 1.0 / D, LN_EPS, ALU.mult, ALU.add))
            s.w(A, tr0)
            ta1 = s.op(A, lambda e: e.activation(R(2), R(0), AF.Square))
            s.w(V, ta1)
            tvv = s.op(V, lambda e: e.tensor_sub(R(1), R(1), R(2)))
            s.w(A, tvv)
            s.op(A, lambda e: e.activation(R(2), R(1), AF.Ln))
            trs = s.op(A, lambda e: e.activation(R(3), R(2), AF.Exp, bias=0.0, scale=-0.5))
            s.w(V, trs)
            s.op(V, lambda e: e.tensor_mul(R(2), R(0), R(3)))
            trB = s.op(V, lambda e: e.tensor_scalar(R(2), R(2), -1.0, None, ALU.mult))
            s.w(TE, trB, trs)
            tbA = s.op(TE, lambda e, pbA=pbA: e.matmul(pbA[:, 0:SW], onesf[0:1, :], R(3), start=True, stop=True))
            tbB = s.op(TE, lambda e, pbB=pbB: e.matmul(pbB[:, 0:SW], onesf[0:1, :], R(2), start=True, stop=True))
            return tbA, tbB

        if not LN_V2:
            emit_ln = emit_ln_v1

        def emit_ln_sums(gates=()):
            s.w(V, *gates)
            s.w(A, *gates)
            tv = s.op(V, lambda e: e.tensor_add(tscr[:, 0:SW], fbufA[:, 0:SW], fbufA[:, SW:2 * SW]))
            for k in range(2, KC):
                tv = s.op(V, lambda e, k=k: e.tensor_add(tscr[:, 0:SW], tscr[:, 0:SW], fbufA[:, k * SW:(k + 1) * SW]))
            hq = hsl[:, 0:SW]
            ta = s.op(A, lambda e, hq=hq: e.activation(hq, fbufA[:, 0:SW], AF.Square))
            s.w(V, ta)
            tq = s.op(V, lambda e, hq=hq: e.tensor_copy(tscr[:, SW:2 * SW], hq))
            for k in range(1, KC):
                s.w(A, tq)
                ta = s.op(A, lambda e, k=k, hq=hq: e.activation(hq, fbufA[:, k * SW:(k + 1) * SW], AF.Square))
                s.w(V, ta)
                tq = s.op(V, lambda e, hq=hq: e.tensor_add(tscr[:, SW:2 * SW], tscr[:, SW:2 * SW], hq))

            def phaseB(rbA, rbB, pbA, pbB):
                s.w(TE, tq, tv)
                tm1 = s.op(TE, lambda e, rbA=rbA: e.matmul(rbA[0:1, 0:SW], onesf[:, 0:1], tscr[:, 0:SW], start=True, stop=True))
                tm2 = s.op(TE, lambda e, rbB=rbB: e.matmul(rbB[0:1, 0:SW], onesf[:, 0:1], tscr[:, SW:2 * SW], start=True, stop=True))
                R = lambda i: rows[0:1, i * SW:(i + 1) * SW]
                s.w(V, tm1, tm2)
                s.op(V, lambda e, rbA=rbA: e.tensor_scalar(R(0), rbA[0:1, 0:SW], 1.0 / D, None, ALU.mult))
                tr0 = s.op(V, lambda e, rbB=rbB: e.tensor_scalar(R(1), rbB[0:1, 0:SW], 1.0 / D, LN_EPS, ALU.mult, ALU.add))
                s.w(A, tr0)
                ta1 = s.op(A, lambda e: e.activation(R(2), R(0), AF.Square))
                s.w(V, ta1)
                tvv = s.op(V, lambda e: e.tensor_sub(R(1), R(1), R(2)))
                s.w(A, tvv)
                s.op(A, lambda e: e.activation(R(2), R(1), AF.Ln))
                tsq2 = s.op(A, lambda e: e.activation(R(3), R(2), AF.Exp, bias=0.0, scale=-0.5))
                s.w(V, tsq2)
                s.op(V, lambda e: e.tensor_mul(R(2), R(0), R(3)))
                trB = s.op(V, lambda e: e.tensor_scalar(R(2), R(2), -1.0, None, ALU.mult))
                s.w(TE, trB, tsq2)
                tbA = s.op(TE, lambda e, pbA=pbA: e.matmul(pbA[:, 0:SW], onesf[0:1, :], R(3), start=True, stop=True))
                tbB = s.op(TE, lambda e, pbB=pbB: e.matmul(pbB[:, 0:SW], onesf[0:1, :], R(2), start=True, stop=True))
                return tbA, tbB
            return phaseB

        def emit_moe(tgxg, wr_fn):
            gelu_done = None
            gelu_hist = {}
            cons1 = {}
            for j in range(MOE_J):
                sl = j % 2
                if cons1.get(j - 2) is not None:
                    s.w(G, cons1[j - 2])
                tl = s.dma(G, lambda g, j=j, sl=sl: g.dma_start(
                    mw1b[sl].rearrange("p (kc i) -> p kc i", i=128),
                    mw1_in[:, j * 128:(j + 1) * 128].rearrange("(kc p) i -> p kc i", p=128)),
                    "wf", flip=True)
                pm = pf[j % 2]
                s.w(TE, tl, tgxg, gelu_hist.get(j - 2))
                tk = None
                for k in range(KC):
                    tk = s.op(TE, lambda e, k=k, sl=sl, pm=pm: e.matmul(
                        pm[:, 0:SW], mw1b[sl][:, k * 128:(k + 1) * 128],
                        xg[:, k * SW:(k + 1) * SW], start=(k == 0), stop=(k == KC - 1)))
                cons1[j] = tk
                s.w(A, tk, gelu_done)
                th0 = s.op(A, lambda e, pm=pm: e.activation(tscr[:, 0:SW], pm[:, 0:SW], AF.Identity, bias=0.0, scale=1.0))
                th1 = s.op(A, lambda e, pm=pm: e.activation(sq[:, :], pm[:, 0:SW], AF.Square))
                s.w(V, th1, th0)
                s.op(V, lambda e: e.tensor_scalar(sq[:, :], sq[:, :], GC1, 1.0, ALU.mult, ALU.add))
                tg3 = s.op(V, lambda e: e.tensor_mul(sq[:, :], sq[:, :], tscr[:, 0:SW]))
                s.w(A, tg3)
                tg4 = s.op(A, lambda e: e.activation(sq[:, :], sq[:, :], AF.Sigmoid, bias=0.0, scale=GC2))
                s.w(V, tg4)
                gelu_done = s.op(V, lambda e, j=j: e.tensor_mul(h1m[:, j * SW:(j + 1) * SW], sq[:, :], tscr[:, 0:SW]))
                gelu_hist[j] = gelu_done
            cons2 = {}
            tds = {}
            last = None
            for mo in range(KC):
                sl = mo % 2
                if cons2.get(mo - 2) is not None:
                    s.w(G, cons2[mo - 2])
                tl = s.dma(G, lambda g, mo=mo, sl=sl: g.dma_start(
                    mw2b[sl].rearrange("p (j i) -> p j i", i=128),
                    mw2_in[:, mo * 128:(mo + 1) * 128].rearrange("(j p) i -> p j i", p=128)),
                    "wf", flip=True)
                pm = pf[2 + mo % 2]
                s.w(TE, tl, gelu_done, tds.get(mo - 2))
                tk = None
                for j in range(MOE_J):
                    tk = s.op(TE, lambda e, j=j, sl=sl, pm=pm: e.matmul(
                        pm[:, 0:SW], mw2b[sl][:, j * 128:(j + 1) * 128],
                        h1m[:, j * SW:(j + 1) * SW], start=(j == 0), stop=(j == MOE_J - 1)))
                cons2[mo] = tk
                s.w(V, tk, tewb)
                td = s.op(V, lambda e, pm=pm: e.tensor_mul(tscr[:, SW:2 * SW], pm[:, 0:SW], ewbs[:, :]))
                tds[mo] = td
                last = wr_fn(mo, td, tscr[:, SW:2 * SW])
                s.w(V, last)
            return last

        tewb = build_ewbs(None)
        s.w(SY, tcons, tewb)
        tx = None
        for k in range(KC):
            tx = s.dma(SY, lambda e, k=k: e.dma_start(fbufA[:, k * SW:(k + 1) * SW], xmoe[k]), "xf")
        tbA, tbB = emit_ln(pf[4], pf[5], pf[5], pf[6], gates=(tx, tewb))
        tgxg = None
        for k in range(KC):
            kl = k % 2
            s.w(V, tbA, tbB, tgxg)
            s.op(V, lambda e, k=k, kl=kl: e.tensor_mul(tscr[:, kl * SW:(kl + 1) * SW], fbufA[:, k * SW:(k + 1) * SW], pf[5][:, 0:SW]))
            tt2 = s.op(V, lambda e, kl=kl: e.tensor_add(tscr[:, kl * SW:(kl + 1) * SW], tscr[:, kl * SW:(kl + 1) * SW], pf[6][:, 0:SW]))
            s.w(A, tt2)
            tgxg = s.op(A, lambda e, k=k, kl=kl: e.activation(xg[:, k * SW:(k + 1) * SW], tscr[:, kl * SW:(kl + 1) * SW], AF.Identity, bias=0.0, scale=1.0))

        def wr_moe1(mo, td, src):
            s.w(SY, td)
            return s.dma(SY, lambda e, mo=mo, src=src: e.dma_start(
                ccm1_in[:, mo * SW:(mo + 1) * SW], src), "hw", flip=True)
        tm1w = emit_moe(tgxg, wr_moe1)
        s.drain_stores()
        s.w(G, *s.chtok("hw"))
        if nocc:
            for r in range(NCORE):
                tccm1 = s.dma(G, lambda g, r=r: g.dma_start(
                    ccm1_out[r * 128:(r + 1) * 128, :], ccm1_in), "wq", flip=(r == 0))
        else:
            tccm1 = s.collective("AllGather", ALU.bypass, [ccm1_in], [ccm1_out])
        s.barrier()

        amul = {}
        thwA = {}
        tx1s = {}
        stt_prev = None
        tbA_prev = None
        for sw in range(NSW):
            txl = None
            for k in range(KC):
                s.w(SY, amul.get((sw - 1) * KC + k), tx1s.get((sw - 1) * KC + k))
                txl = s.dma(SY, lambda e, k=k, sw=sw: e.dma_start(
                    fbufA[:, k * SW:(k + 1) * SW], xshard[k][:, sw * SW:(sw + 1) * SW]),
                    "xf", flip=(k == 0))
            tvmask = None
            for k in range(KC):
                kl = k % 2
                if k >= 2:
                    s.w(SY, tvmask)
                else:
                    s.w(SY, tccm1, stt_prev, tbA_prev)
                tgl = s.dma(SY, lambda e, k=k, sw=sw, kl=kl: e.dma_start(
                    tscr[:, kl * SW:(kl + 1) * SW],
                    ccm1_out[sw * 128:(sw + 1) * 128, k * SW:(k + 1) * SW]),
                    "ld", flip=True)
                s.w(V, txl, tgl, tvmask)
                tvmask = s.op(V, lambda e, k=k, kl=kl: e.scalar_tensor_tensor(
                    fbufA[:, k * SW:(k + 1) * SW], tscr[:, kl * SW:(kl + 1) * SW],
                    m0[:, 0:1], fbufA[:, k * SW:(k + 1) * SW], ALU.mult, ALU.add))
                s.w(SY, tvmask)
                tx1s[sw * KC + k] = s.dma(SY, lambda e, k=k, sw=sw: e.dma_start(
                    st_x1[k][:, sw * SW:(sw + 1) * SW], fbufA[:, k * SW:(k + 1) * SW]),
                    "st", flip=True)
            stt_prev = tvmask
            tbA, tbB = emit_ln(pf[0], pf[1], pf[1], pf[2],
                               gates=(tvmask, amul.get((sw - 1) * KC + KC - 1)))
            tbA_prev = tbA
            for k in range(KC):
                kl = k % 2
                s.w(V, tbA, tbB, thwA.get(sw * KC + k - 2))
                amul[sw * KC + k] = s.op(V, lambda e, k=k, kl=kl: e.tensor_mul(
                    hsl[:, kl * SW:(kl + 1) * SW],
                    fbufA[:, k * SW:(k + 1) * SW], pf[1][:, 0:SW]))
                tt2 = s.op(V, lambda e, kl=kl: e.tensor_add(hsl[:, kl * SW:(kl + 1) * SW],
                                                            hsl[:, kl * SW:(kl + 1) * SW], pf[2][:, 0:SW]))
                s.w(SY, tt2)
                thwA[sw * KC + k] = s.dma(SY, lambda e, k=k, sw=sw, kl=kl: e.dma_start(
                    st_a[k][:, sw * SW:(sw + 1) * SW], hsl[:, kl * SW:(kl + 1) * SW]), "hw", flip=True)
        s.barrier()

        for l in range(L):
            ob = obs[:, l * 128:(l + 1) * 128]
            s.w(G, *s.chtok("st"), *s.chtok("hw"), *s.chtok("ou"))
            trv = s.dma(G, lambda g, l=l: g.dma_start(rvf[:, :], rowv[0:1, l * D:(l + 1) * D]), "wq")
            s.op(G, lambda g: g.memset(sbe[64:128, :], 0.0))
            s.op(G, lambda g: g.memset(sbo[0:64, :], 0.0))
            for sl_ in range(2):
                for i_ in range(16):
                    s.op(G, lambda g, sl_=sl_, i_=i_: g.memset(
                        vts[:, sl_ * 1040 + i_ * 65 + 64:sl_ * 1040 + i_ * 65 + 65], 1.0))
                    s.op(G, lambda g, sl_=sl_, i_=i_: g.memset(
                        vts1[:, sl_ * 1040 + i_ * 65 + 64:sl_ * 1040 + i_ * 65 + 65], 1.0))
            trvb0 = s.token(G)
            s.w(A, trv)
            trvb = s.op(A, lambda e: e.activation(rvb[:, :], rvf[:, :], AF.Identity, bias=0.0, scale=1.0))
            twq = None
            for wi, (dst, src) in enumerate(((wqsb, wq_in), (wksb, wk_in),
                                             (wvsb, wv_in), (wosb, wo_in))):
                for k in range(KC):
                    twq = s.dma(G, lambda g, dst=dst, src=src, k=k, l=l: g.dma_start(
                        dst[:, k * 1024:(k + 1) * 1024], src[l][k * 128:(k + 1) * 128, :]),
                        "wq", flip=(k == 0 and wi == 0))

            tsz = s.op(G, lambda g: g.memset(srun[:, :], 0.0))
            s.w(SY, tsz)
            tsnap = s.dma(SY, lambda e: e.dma_start(snap_d[0], srun[:, 0:520]), "sn", flip=True)
            tsadd = None
            tdm_last = None
            tevK = {}
            tfvK = {}
            tdmK = {}
            tvwK = {}
            kxdone = {}
            tvev = None
            ttr_done = None
            for sw in range(NSW):
                sl = sw % 2
                txb = None
                for k in range(KC):
                    if k == 0:
                        s.w(G, *(kxdone.get(sw - 2) or []))
                    txb = s.dma(G, lambda g, k=k, sw=sw, sl=sl: g.dma_start(
                        xbw[sl][:, k * SW:(k + 1) * SW], st_a[k][:, sw * SW:(sw + 1) * SW]),
                        "xb", flip=(k == 0))
                tev = {-2: (None, tevK.get((sw - 1) * KC + KC - 2)),
                       -1: (None, tevK.get((sw - 1) * KC + KC - 1))}
                for mc in range(KC):
                    pm = pf[mc % 2]
                    g2 = tev.get(mc - 2)
                    if mc == 0:
                        s.w(TE, tsadd)
                    s.w(TE, txb, twq, g2[1] if g2 else None)
                    tk = None
                    for k in range(KC):
                        tk = s.op(TE, lambda e, mc=mc, k=k, sl=sl, pm=pm: e.matmul(
                            pm[:, 0:SW], wksb[:, k * 1024 + mc * 128:k * 1024 + mc * 128 + 128],
                            xbw[sl][:, k * SW:(k + 1) * SW], start=(k == 0), stop=(k == KC - 1)))
                    s.w(A, tk)
                    te1 = s.op(A, lambda e, mc=mc, pm=pm, l=l: e.activation(
                        ksb[:, mc * SW:(mc + 1) * SW], pm[:, 0:SW], AF.Identity,
                        bias=pvcol(l, "bk", mc), scale=1.0))
                    te2 = s.op(A, lambda e, mc=mc, pm=pm, l=l: e.activation(
                        k2sb[:, mc * SW:(mc + 1) * SW], pm[:, 0:SW], AF.Square,
                        bias=pvcol(l, "bk", mc), scale=1.0))
                    tev[mc] = (te1, te2)
                    tevK[sw * KC + mc] = te2
                tfv = {-2: tfvK.get((sw - 1) * NP + NP - 2),
                       -1: tfvK.get((sw - 1) * NP + NP - 1)}
                for p in range(NP):
                    pm = pf[4 + p % 2]
                    if p == 0:
                        s.w(A, *s.chtok("st"))
                    s.w(TE, tev[p][0], tev[p][1], tfv.get(p - 2))
                    s.op(TE, lambda e, p=p, pm=pm, ob=ob: e.matmul(
                        pm[:, 0:SW], ob, ksb[:, p * SW:(p + 1) * SW], start=True, stop=False))
                    tk = s.op(TE, lambda e, p=p, pm=pm: e.matmul(
                        pm[:, 0:SW], cbt[:, :], k2sb[:, p * SW:(p + 1) * SW], start=False, stop=True))
                    s.w(A, tk)
                    tfv[p] = s.op(A, lambda e, p=p, pm=pm: e.activation(
                        phkb[:, p * SW:(p + 1) * SW], pm[:, 0:SW], AF.Exp,
                        bias=biasm[:, 0:1], scale=1.0))
                    tfvK[sw * NP + p] = tfv[p]
                    s.w(SY, tfv[p])
                    s.dma(SY, lambda e, p=p, sw=sw: e.dma_start(
                        phk_d[p][:, sw * SW:(sw + 1) * SW], phkb[:, p * SW:(p + 1) * SW]), "st")
                tvts = {}
                for c in range(NCHS):
                    cg = sw * NCHS + c
                    vsl = c % 2
                    s.w(TE, twq, tvev, trvb)
                    tk = None
                    for k in range(KC):
                        for half in range(2):
                            tk = s.op(TE, lambda e, c=c, k=k, half=half, sl=sl: e.matmul(
                                pf[2 + half][:, 0:SW],
                                xbw[sl][:, k * SW + c * CH:k * SW + c * CH + CH],
                                wvsb[:, k * 1024 + half * SW:k * 1024 + (half + 1) * SW],
                                start=(k == 0), stop=False))
                    for half in range(2):
                        tk = s.op(TE, lambda e, half=half: e.matmul(
                            pf[2 + half][:, 0:SW], onesbr[:, :],
                            rvb[0:1, half * SW:(half + 1) * SW],
                            start=False, stop=True))
                    s.w(A, tk, tvwK.get(cg - 2))
                    tva = s.op(A, lambda e, vsl=vsl: e.activation(
                        vstage[:, vsl * 1024:vsl * 1024 + SW], pf[2][:, 0:SW], AF.Identity, bias=0.0, scale=1.0))
                    tvev = s.op(A, lambda e, vsl=vsl: e.activation(
                        vstage[:, vsl * 1024 + SW:vsl * 1024 + 1024], pf[3][:, 0:SW], AF.Identity, bias=0.0, scale=1.0))
                    s.w(SY, tva, tvev)
                    tvwK[cg] = s.dma(SY, lambda e, cg=cg, vsl=vsl: e.dma_start(
                        vraw_d[cg * CH:(cg + 1) * CH, :], vstage[:, vsl * 1024:(vsl + 1) * 1024]), "st", flip=True)
                    s.w(V, tva, tvev, tdmK.get(cg - 2))
                    tvts[c] = s.op(V, lambda e, vsl=vsl: e.tensor_copy(
                        vts[:, vsl * 1040:(vsl + 1) * 1040].rearrange("p (i c) -> p i c", c=65)[:, :, 0:64],
                        vstage[:, vsl * 1024:(vsl + 1) * 1024].rearrange("p (i c) -> p i c", c=64)))
                    ttr = None
                    s.w(TE, tfv[NP - 1], ttr_done)
                    for p in range(NP):
                        ttr = s.op(TE, lambda e, p=p, c=c: e.transpose(
                            pbt[:, p * 128:(p + 1) * 128],
                            phkb[:, p * SW + c * CH:p * SW + (c + 1) * CH], identb[:, :]))
                    tpe = None
                    s.w(V, ttr, tdmK.get(cg - 1))
                    for p in range(NP):
                        tpe = s.op(V, lambda e, p=p: e.tensor_copy(
                            phkt[:, p * 128:(p + 1) * 128], pbt[:, p * 128:(p + 1) * 128]))
                    ttr_done = tpe
                    s.w(TE, tev[KC - 1][1], tpe, tvts[c], tsadd, trvb0)
                    tdmc = None
                    for p in range(NP):
                        bank = pf[0] if p < 4 else pf[1]
                        off = (p % 4) * 65
                        for hh in range(2):
                            tp = None if hh == 0 else (0, 64)
                            orow = slice(64 * hh, 64 * hh + 64)
                            tdmc = s.op(TE, lambda e, p=p, hh=hh, bank=bank, off=off, orow=orow, tp=tp, vsl=vsl: e.matmul(
                                bank[orow, off:off + 65],
                                phkt[:, p * 128 + 64 * hh:p * 128 + 64 * hh + 64],
                                vts[:, vsl * 1040 + (p * 2 + hh) * 65:vsl * 1040 + (p * 2 + hh) * 65 + 65],
                                start=True, stop=True, tile_position=tp))
                    tdmK[cg] = tdmc
                    tdm_last = tdmc
                    cur = cg % 2
                    nxt = (cg + 1) % 2
                    s.w(V, tdmc, tsnap)
                    s.op(V, lambda e, cur=cur, nxt=nxt: e.tensor_add(
                        srun[:, nxt * 520:nxt * 520 + 260], srun[:, cur * 520:cur * 520 + 260],
                        pf[0][:, 0:260]))
                    tsadd = s.op(V, lambda e, cur=cur, nxt=nxt: e.tensor_add(
                        srun[:, nxt * 520 + 260:nxt * 520 + 520], srun[:, cur * 520 + 260:cur * 520 + 520],
                        pf[1][:, 0:260]))
                    if cg < 31:
                        s.w(SY, tsadd)
                        tsnap = s.dma(SY, lambda e, cg=cg, nxt=nxt: e.dma_start(
                            snap_d[cg + 1], srun[:, nxt * 520:(nxt + 1) * 520]), "sn", flip=True)
                kxdone[sw] = [tevK[sw * KC + KC - 1], tvev]
            s.w(SY, tsadd)
            tst_w = s.dma(SY, lambda e: e.dma_start(ccst_in, srun[:, 0:520]), "ou")
            s.w(G, tst_w)
            if nocc:
                for r in range(NCORE):
                    tcst = s.dma(G, lambda g, r=r: g.dma_start(
                        ccst_out[r * 128:(r + 1) * 128, :], ccst_in), "wq", flip=(r == 0))
            else:
                tcst = s.collective("AllGather", ALU.bypass, [ccst_in], [ccst_out])
            sst = {}

            def emit_sinit():
                s.w(SY, tcst)
                tsi = None
                tsis = {}
                for r in range(NCORE):
                    rl = r % 2
                    if r >= 2:
                        s.w(SY, tsis[r - 2])
                    tbl = s.dma(SY, lambda e, r=r, rl=rl: e.dma_start(
                        snapb[:, rl * 520:(rl + 1) * 520], ccst_out[r * 128:(r + 1) * 128, :]),
                        "ld", flip=True)
                    s.w(V, tbl)
                    if r == 0:
                        tsi = s.op(V, lambda e, rl=rl: e.tensor_scalar(
                            sinit[:, :], snapb[:, rl * 520:(rl + 1) * 520], scf[:, 0:1], None, ALU.mult))
                    else:
                        tsi = s.op(V, lambda e, r=r, rl=rl: e.scalar_tensor_tensor(
                            sinit[:, :], snapb[:, rl * 520:(rl + 1) * 520], scf[:, r:r + 1],
                            sinit[:, :], ALU.mult, ALU.add))
                    tsis[r] = tsi
                sst["tsi"] = tsis[NCORE - 1]
                sst["guard"] = tsis[NCORE - 2]

            s.w(G, tsadd, tdm_last, *s.chtok("st"))
            s.op(G, lambda g: g.memset(tscr[:, 0:256], 1.0))
            s.op(G, lambda g: g.affine_select(mw[:, 0:256], tscr[:, 0:256],
                                              pattern=[[1, 256]], compare_op=ALU.is_ge,
                                              fill=0.0, base=0, channel_multiplier=-1))
            tmkw = s.op(G, lambda g: g.memset(phz[0:64, :], 0.0))
            gdh = {}
            vq = [vts, vts1]
            prev_core = [tsadd, tvev, tfvK[(NSW - 1) * NP + NP - 1]]
            twoprev = None
            lnq = {}
            pend_ln1 = [None]
            amulQ = {}
            thwQ = {}
            qdone = {-2: kxdone[NSW - 2][0], -1: kxdone[NSW - 1][0]}
            tpe_last = ttr_done
            for sw in range(NSW):
                sl = sw % 2
                txb = None
                for k in range(KC):
                    if k == 0:
                        s.w(G, qdone.get(sw - 2))
                        if sw < 2:
                            s.w(G, *(kxdone.get(NSW - 2 + sw) or []))
                    txb = s.dma(G, lambda g, k=k, sw=sw, sl=sl: g.dma_start(
                        xbw[sl][:, k * SW:(k + 1) * SW], st_a[k][:, sw * SW:(sw + 1) * SW]),
                        "xb", flip=(k == 0))
                tphk = None
                for p in range(NP):
                    if p == 0:
                        if sw == 0:
                            s.w(SY, tpe_last, *s.chtok("st"))
                        s.w(SY, gdh.get((sw - 1) * 2 + 1))
                    tphk = s.dma(SY, lambda e, p=p, sw=sw: e.dma_start(
                        phkb[:, p * SW:(p + 1) * SW], phk_d[p][:, sw * SW:(sw + 1) * SW]),
                        "ph", flip=(p == 0))
                if sw == 0:
                    emit_sinit()
                tev = {}
                for mc in range(KC):
                    pm = pf[mc % 2]
                    g2 = tev.get(mc - 2)
                    if mc == 0:
                        s.w(TE, *prev_core, twoprev)
                    s.w(TE, txb, g2[1] if g2 else None)
                    tk = None
                    for k in range(KC):
                        tk = s.op(TE, lambda e, mc=mc, k=k, sl=sl, pm=pm: e.matmul(
                            pm[:, 0:SW], wqsb[:, k * 1024 + mc * 128:k * 1024 + mc * 128 + 128],
                            xbw[sl][:, k * SW:(k + 1) * SW], start=(k == 0), stop=(k == KC - 1)))
                    s.w(A, tk)
                    te1 = s.op(A, lambda e, mc=mc, pm=pm, l=l: e.activation(
                        ksb[:, mc * SW:(mc + 1) * SW], pm[:, 0:SW], AF.Identity,
                        bias=pvcol(l, "bq", mc), scale=1.0))
                    te2 = s.op(A, lambda e, mc=mc, pm=pm, l=l: e.activation(
                        k2sb[:, mc * SW:(mc + 1) * SW], pm[:, 0:SW], AF.Square,
                        bias=pvcol(l, "bq", mc), scale=1.0))
                    tev[mc] = (te1, te2)
                if pend_ln1[0] is not None:
                    pend_ln1[0]()
                    pend_ln1[0] = None
                txf = None
                for k in range(KC):
                    s.w(SY, amulQ.get((sw - 1) * KC + k))
                    txf = s.dma(SY, lambda e, k=k, sw=sw: e.dma_start(
                        fbufA[:, k * SW:(k + 1) * SW], st_a[k][:, sw * SW:(sw + 1) * SW]),
                        "xf", flip=(k == 0))
                teps = {}
                for p in range(NP):
                    pm = pf[4 + p % 2]
                    s.w(TE, tev[p][1], teps.get(p - 2))
                    th = s.op(TE, lambda e, p=p, pm=pm: e.matmul(
                        pm[0:2, 0:SW], hm2[:, :], k2sb[:, p * SW:(p + 1) * SW], start=True, stop=True))
                    s.w(A, th)
                    teps[p] = s.op(A, lambda e, p=p, pm=pm: e.activation(
                        eps2[:, p * SW:(p + 1) * SW], pm[0:2, 0:SW], AF.Exp,
                        bias=biasm[0:2, 1:2], scale=1.0))
                tphq = {}
                tzs = {}
                for p in range(NP):
                    pm = pf[2 + p % 2]
                    s.w(TE, tev[p][0], tphq.get(p - 2))
                    tk = s.op(TE, lambda e, p=p, pm=pm, ob=ob: e.matmul(
                        pm[:, 0:SW], ob, ksb[:, p * SW:(p + 1) * SW], start=True, stop=True))
                    s.w(A, tk)
                    tphq[p] = s.op(A, lambda e, p=p, pm=pm: e.activation(
                        phqb[:, p * SW:(p + 1) * SW], pm[:, 0:SW], AF.Exp,
                        bias=biasm[:, 0:1], scale=1.0))
                    s.w(SY, tphq[p])
                    tzs[p] = s.dma(SY, lambda e, p=p: e.dma_start(
                        phz[64:128, p * SW:(p + 1) * SW], phqb[64:128, p * SW:(p + 1) * SW]),
                        "hw", flip=True)
                ev = {0: [], 1: []}
                for g in range(2):
                    gg2 = sw * 2 + g
                    gl = g
                    s.w(SY, gdh.get(gg2 - 2),
                        sst["guard"] if gg2 == 0 else (sst["tsi"] if gg2 == 1 else None))
                    for kb in range(2):
                        cg = gg2 * 2 + kb
                        s.dma(SY, lambda e, cg=cg, gl=gl, kb=kb: e.dma_start(
                            vq[gl][:, kb * 1040:(kb + 1) * 1040].rearrange("p (i c) -> p i c", c=65)[:, :, 0:64],
                            vraw_d[cg * CH:(cg + 1) * CH, :].rearrange("t (i c) -> t i c", c=64)),
                            "ld", flip=(kb == 0))
                    tsl_ = s.dma(SY, lambda e, gg2=gg2, gl=gl: e.dma_start(
                        snapb[:, gl * 520:(gl + 1) * 520], snap_d[2 * gg2]), "ld")
                    s.w(V, tsl_, gdh.get(gg2 - 1), sst["tsi"])
                    s.op(V, lambda e, gl=gl: e.tensor_add(
                        srq[:, :], snapb[:, gl * 520:(gl + 1) * 520], sinit[:, :]))
                    tce = s.op(V, lambda e: e.tensor_copy(sbe[0:64, :], srq[0:64, :]))
                    tco = s.op(V, lambda e: e.tensor_copy(sbo[64:128, :], srq[64:128, :]))

                    def emit_ph2(dv):
                        (p, pi, co, bN, bB, bD, tb) = dv
                        rq = rows[0:1, 1024 + pi * 512:1024 + (pi + 1) * 512]
                        s.w(A, tb[0], tb[1], lnq.get("A"), lnq.get("B"))
                        s.op(A, lambda e, rq=rq, bB=bB: e.activation(
                            rq[0:1, 0:256], bB[0:1, 256:512], AF.Ln))
                        s.op(A, lambda e, rq=rq, bD=bD: e.activation(
                            rq[0:1, 256:512], bD[0:1, 256:512], AF.Ln))
                        trd = s.op(A, lambda e, rq=rq: e.activation(
                            rq[0:1, 0:512], rq[0:1, 0:512], AF.Exp, bias=0.0, scale=-1.0))
                        s.w(TE, trd)
                        trb0 = s.op(TE, lambda e, rq=rq, bN=bN: e.matmul(
                            bN[0:64, 256:512], onesf[0:1, 0:64],
                            rq[0:1, 0:256], start=True, stop=True))
                        trb1 = s.op(TE, lambda e, rq=rq, bN=bN: e.matmul(
                            bN[64:128, 256:512], onesf[0:1, 0:64],
                            rq[0:1, 256:512], start=True, stop=True,
                            tile_position=(0, 64)))
                        s.w(A, trb0, trb1)
                        trde = s.op(A, lambda e, pi=pi, bN=bN: e.activation(
                            rdbs2[pi][:, :], bN[:, 256:512], AF.Identity, bias=0.0, scale=1.0))
                        s.w(V, trde)
                        tout = s.op(V, lambda e, p=p, co=co, pi=pi, bN=bN: e.tensor_tensor(
                            asw[:, p * SW + co:p * SW + co + 256], bN[:, 0:256],
                            rdbs2[pi][:, :], ALU.mult))
                        ev[pi] = [tout, trde]
                        return tout

                    pend = None
                    for p in range(NP):
                        pi = p % 2
                        co = g * 256
                        bN = pf[pi]
                        bA = pf[2 + pi]
                        bB = pf[4 + pi]
                        bD = bA
                        s.w(TE, tphq[NP - 1], teps[NP - 1], tce, tco, tphk, trvb0, tmkw,
                            tzs[p], *ev[pi])
                        tat = {}
                        tat[0] = s.op(TE, lambda e, p=p, co=co, bA=bA: e.matmul(
                            bA[:, 0:256],
                            phkb[0:64, p * SW + co:p * SW + co + CH],
                            phqb[0:64, p * SW + co:p * SW + co + 256],
                            start=True, stop=True))
                        tat[2] = s.op(TE, lambda e, p=p, co=co, bB=bB: e.matmul(
                            bB[:, 0:CH],
                            phkb[0:64, p * SW + co + CH:p * SW + co + 2 * CH],
                            phqb[0:64, p * SW + co + CH:p * SW + co + 2 * CH],
                            start=True, stop=True))
                        tat[1] = s.op(TE, lambda e, p=p, co=co, bA=bA: e.matmul(
                            bA[:, 256:512],
                            phkb[:, p * SW + co:p * SW + co + CH],
                            phz[:, p * SW + co:p * SW + co + 256],
                            start=True, stop=True))
                        tat[3] = s.op(TE, lambda e, p=p, co=co, bB=bB: e.matmul(
                            bB[:, CH:2 * CH],
                            phkb[:, p * SW + co + CH:p * SW + co + 2 * CH],
                            phz[:, p * SW + co + CH:p * SW + co + 2 * CH],
                            start=True, stop=True))
                        tmk = {}
                        s.w(V, tat[0], tat[1], tat[2], tat[3])
                        for hh in range(2):
                            tmk[hh] = s.op(V, lambda e, pi=pi, hh=hh, bA=bA: e.tensor_tensor(
                                atk0(pi, hh), bA[:, hh * 256:(hh + 1) * 256], mw[:, 0:256], ALU.mult))
                            tmk[2 + hh] = s.op(V, lambda e, pi=pi, hh=hh, bB=bB: e.tensor_tensor(
                                atk1(pi, hh), bB[:, hh * CH:(hh + 1) * CH], mw[:, 0:CH], ALU.mult))
                        tb = {}
                        s.w(TE, tmk[0], tmk[1], tmk[2], tmk[3])
                        for hh in range(2):
                            tp = None if hh == 0 else (0, 64)
                            orow = slice(64 * hh, 64 * hh + 64)
                            sb = sbe if hh == 0 else sbo
                            s.op(TE, lambda e, p=p, co=co, tp=tp, orow=orow, sb=sb, bN=bN: e.matmul(
                                bN[orow, 0:256], sb[:, p * 65:p * 65 + 64],
                                phqb[:, p * SW + co:p * SW + co + 256],
                                start=True, stop=False, tile_position=tp))
                            s.op(TE, lambda e, p=p, hh=hh, pi=pi, gl=gl, tp=tp, orow=orow, bN=bN: e.matmul(
                                bN[orow, CH:256],
                                vq[gl][:, 1040 + (p * 2 + hh) * 65:1040 + (p * 2 + hh) * 65 + 64],
                                atk1(pi, hh), start=False, stop=False, tile_position=tp))
                            s.op(TE, lambda e, p=p, hh=hh, pi=pi, gl=gl, tp=tp, orow=orow, bN=bN: e.matmul(
                                bN[orow, 0:256],
                                vq[gl][:, (p * 2 + hh) * 65:(p * 2 + hh) * 65 + 64],
                                atk0(pi, hh), start=False, stop=True, tile_position=tp))
                            dn = bB[0:1, 256:512] if hh == 0 else bD[0:1, 256:512]
                            dn1 = bB[0:1, 384:512] if hh == 0 else bD[0:1, 384:512]
                            s.op(TE, lambda e, p=p, co=co, sb=sb, dn=dn: e.matmul(
                                dn, sb[:, p * 65 + 64:p * 65 + 65],
                                phqb[:, p * SW + co:p * SW + co + 256],
                                start=True, stop=False))
                            s.op(TE, lambda e, p=p, hh=hh, pi=pi, gl=gl, dn1=dn1: e.matmul(
                                dn1,
                                vq[gl][:, 1040 + (p * 2 + hh) * 65 + 64:1040 + (p * 2 + hh) * 65 + 65],
                                atk1(pi, hh), start=False, stop=False))
                            s.op(TE, lambda e, p=p, hh=hh, pi=pi, gl=gl, dn=dn: e.matmul(
                                dn,
                                vq[gl][:, (p * 2 + hh) * 65 + 64:(p * 2 + hh) * 65 + 65],
                                atk0(pi, hh), start=False, stop=False))
                            tb[hh] = s.op(TE, lambda e, p=p, hh=hh, co=co, dn=dn: e.matmul(
                                dn, identb[0:2, hh:hh + 1],
                                eps2[:, p * SW + co:p * SW + co + 256], start=False, stop=True))
                        if pend is not None:
                            emit_ph2(pend)
                        pend = (p, pi, co, bN, bB, bD, tb)
                    gdh[gg2] = emit_ph2(pend)
                two = {}
                for mo in range(KC):
                    pm = pf[6] if mo % 2 == 0 else pf[0]
                    if mo < 2:
                        s.w(A, thwQ.get((sw - 1) * KC + KC - 2 + mo))
                    if mo == 0:
                        s.w(TE, amulQ.get((sw - 1) * KC + KC - 1))
                    s.w(TE, *ev[0], *ev[1], two.get(mo - 2))
                    tk = None
                    for k in range(KC):
                        tk = s.op(TE, lambda e, mo=mo, k=k, pm=pm: e.matmul(
                            pm[:, 0:SW], wosb[:, k * 1024 + mo * 128:k * 1024 + mo * 128 + 128],
                            asw[:, k * SW:(k + 1) * SW], start=(k == 0), stop=(k == KC - 1)))
                    s.w(A, tk, two.get(mo - 2))
                    ta = s.op(A, lambda e, mo=mo, pm=pm, l=l: e.activation(
                        tscr[:, (mo % 2) * SW:(mo % 2 + 1) * SW], pm[:, 0:SW], AF.Identity,
                        bias=pvcol(l, "bo", mo), scale=1.0))
                    s.w(V, ta, txf)
                    two[mo] = s.op(V, lambda e, mo=mo: e.tensor_add(
                        fbufA[:, mo * SW:(mo + 1) * SW], fbufA[:, mo * SW:(mo + 1) * SW],
                        tscr[:, (mo % 2) * SW:(mo % 2 + 1) * SW]))
                phB = emit_ln_sums(gates=(two[KC - 1], thwQ.get((sw - 1) * KC + KC - 2)))

                def mk_ln1(sw, phB, l=l):
                    def flush():
                        tbA, tbB = phB(pf[6], pbtF, pf[6], pbtF)
                        thw = {-2: thwQ.get((sw - 1) * KC + KC - 2),
                               -1: thwQ.get((sw - 1) * KC + KC - 1)}
                        for k in range(KC):
                            kl = k % 2
                            s.w(V, tbA, tbB, thw.get(k - 2))
                            amulQ[sw * KC + k] = s.op(V, lambda e, k=k, kl=kl: e.tensor_mul(
                                tscr[:, kl * SW:(kl + 1) * SW],
                                fbufA[:, k * SW:(k + 1) * SW], pf[6][:, 0:SW]))
                            tt2 = s.op(V, lambda e, kl=kl: e.tensor_add(
                                tscr[:, kl * SW:(kl + 1) * SW],
                                tscr[:, kl * SW:(kl + 1) * SW], pbtF[:, 0:SW]))
                            s.w(A, tt2)
                            th = s.op(A, lambda e, k=k, kl=kl, l=l: e.activation(
                                hsl[:, kl * SW:(kl + 1) * SW], tscr[:, kl * SW:(kl + 1) * SW], AF.Identity,
                                bias=pvcol(l, "ln1b", k), scale=pvcol(l, "ln1g", k)))
                            s.w(SY, th)
                            thw[k] = s.dma(SY, lambda e, k=k, sw=sw, kl=kl: e.dma_start(
                                st_h[k][:, sw * SW:(sw + 1) * SW], hsl[:, kl * SW:(kl + 1) * SW]), "hw", flip=True)
                            thwQ[sw * KC + k] = thw[k]
                        lnq["A"], lnq["B"] = tbA, tbB
                    return flush
                pend_ln1[0] = mk_ln1(sw, phB)
                twoprev = two[KC - 1]
                prev_core = ev[0] + ev[1]
                qdone[sw] = tev[KC - 1][1]
            if pend_ln1[0] is not None:
                pend_ln1[0]()
                pend_ln1[0] = None
            s.barrier()

            tW1s = {}
            for j in range(MJ):
                tW1s[j] = s.dma(G, lambda g, j=j, l=l: g.dma_start(
                    w1sb[:, j * 1024:(j + 1) * 1024].rearrange("p (kc i) -> p kc i", i=128),
                    w1_in[l][:, j * 128:(j + 1) * 128].rearrange("(kc p) i -> p kc i", p=128)),
                    "wf", flip=True)
            tW2s = {}
            for mo in range(KC):
                tW2s[mo] = s.dma(G, lambda g, mo=mo, l=l: g.dma_start(
                    w2sb[:, mo * 4096:(mo + 1) * 4096].rearrange("p (j i) -> p j i", i=128),
                    w2_in[l][:, mo * 128:(mo + 1) * 128].rearrange("(j p) i -> p j i", p=128)),
                    "wf", flip=True)
            treF = {}
            tyaddF = {}
            thwF = {}
            fblast = {}
            pend_ln2 = [None]
            for sw in range(NSW):
                thb = None
                for k in range(KC):
                    if k == 0:
                        s.w(G, treF.get((sw - 1) * MJ + MJ - 1))
                    thb = s.dma(G, lambda g, k=k, sw=sw: g.dma_start(
                        hbf[:, k * SW:(k + 1) * SW], st_h[k][:, sw * SW:(sw + 1) * SW]),
                        "xb", flip=(k == 0))
                tyadd = {-2: tyaddF.get((sw - 1) * KC + KC - 2),
                         -1: tyaddF.get((sw - 1) * KC + KC - 1)}
                tre = {-2: treF.get((sw - 1) * MJ + MJ - 2),
                       -1: treF.get((sw - 1) * MJ + MJ - 1)}
                txf = None
                for j in range(MJ):
                    if j == 2:
                        if pend_ln2[0] is not None:
                            pend_ln2[0]()
                            pend_ln2[0] = None
                        for k in range(KC):
                            s.w(SY, fblast.get((sw - 1) * KC + k))
                            txf = s.dma(SY, lambda e, k=k, sw=sw: e.dma_start(
                                fbufA[:, k * SW:(k + 1) * SW], st_h[k][:, sw * SW:(sw + 1) * SW]),
                                "xf", flip=(k == 0))
                    pm = pf[j % 2]
                    s.w(TE, tW1s[j], thb, tre.get(j - 2), tyadd.get(-1))
                    tk = None
                    for k in range(KC):
                        tk = s.op(TE, lambda e, j=j, k=k, pm=pm: e.matmul(
                            pm[:, 0:TG], w1sb[:, j * 1024 + k * 128:j * 1024 + k * 128 + 128],
                            hbf[:, k * SW:k * SW + TG], start=(k == 0), stop=(k == KC - 1)))
                    s.w(A, tk)
                    tre[j] = s.op(A, lambda e, j=j, pm=pm, l=l: e.activation(
                        h1f[:, j * TG:(j + 1) * TG], pm[:, 0:TG], AF.Relu,
                        bias=pvcol(l, "bf1", j), scale=1.0))
                    treF[sw * MJ + j] = tre[j]
                for mo in range(KC):
                    pm = pf[2 + mo % 2]
                    s.w(TE, tW2s[mo], tre[MJ - 1], tyadd.get(mo - 2))
                    if mo < 2:
                        s.w(A, *(thwF.get((sw - 1) * KC + KC - 2 + mo) or []))
                    tk = None
                    for j in range(MJ):
                        tk = s.op(TE, lambda e, mo=mo, j=j, pm=pm: e.matmul(
                            pm[:, 0:TG], w2sb[:, mo * 4096 + j * 128:mo * 4096 + j * 128 + 128],
                            h1f[:, j * TG:(j + 1) * TG], start=(j == 0), stop=(j == MJ - 1)))
                    s.w(A, tk, tyadd.get(mo - 2))
                    ta = s.op(A, lambda e, mo=mo, pm=pm, l=l: e.activation(
                        tscr[:, (mo % 2) * SW:(mo % 2) * SW + TG],
                        pm[:, 0:TG], AF.Identity, bias=pvcol(l, "bf2", mo), scale=1.0))
                    s.w(V, ta, txf)
                    tyadd[mo] = s.op(V, lambda e, mo=mo: e.tensor_add(
                        fbufA[:, mo * SW:mo * SW + TG],
                        fbufA[:, mo * SW:mo * SW + TG],
                        tscr[:, (mo % 2) * SW:(mo % 2) * SW + TG]))
                    tyaddF[sw * KC + mo] = tyadd[mo]
                gates = [tyadd[mo] for mo in range(KC)] + (thwF.get((sw - 1) * KC + KC - 2) or [])
                phB2 = emit_ln_sums(gates=gates)

                def mk_ln2(sw, phB2, l=l):
                    def flush():
                        tbA, tbB = phB2(pf[4], pf[5], pf[4], pf[5])
                        thw = {-2: thwF.get((sw - 1) * KC + KC - 2),
                               -1: thwF.get((sw - 1) * KC + KC - 1)}
                        for k in range(KC):
                            kl = k % 2
                            s.w(V, tbA, tbB, *(thw.get(k - 2) or []))
                            t1k = s.op(V, lambda e, k=k, kl=kl: e.tensor_mul(
                                tscr[:, kl * SW:(kl + 1) * SW],
                                fbufA[:, k * SW:(k + 1) * SW], pf[4][:, 0:SW]))
                            tt2 = s.op(V, lambda e, kl=kl: e.tensor_add(
                                tscr[:, kl * SW:(kl + 1) * SW],
                                tscr[:, kl * SW:(kl + 1) * SW], pf[5][:, 0:SW]))
                            s.w(A, tt2)
                            th = s.op(A, lambda e, k=k, kl=kl, l=l: e.activation(
                                hsl[:, kl * SW:(kl + 1) * SW], tscr[:, kl * SW:(kl + 1) * SW], AF.Identity,
                                bias=pvcol(l, "ln2b", k), scale=pvcol(l, "ln2g", k)))
                            if l < L - 1:
                                s.w(SY, th)
                                thw[k] = [s.dma(SY, lambda e, k=k, sw=sw, kl=kl: e.dma_start(
                                    st_a[k][:, sw * SW:(sw + 1) * SW], hsl[:, kl * SW:(kl + 1) * SW]),
                                    "hw", flip=True)]
                                thwF[sw * KC + k] = thw[k]
                                fblast[sw * KC + k] = t1k
                            else:
                                s.w(SY, t1k)
                                tx1l = s.dma(SY, lambda e, k=k, sw=sw: e.dma_start(
                                    fbufA[:, k * SW:(k + 1) * SW], st_x1[k][:, sw * SW:(sw + 1) * SW]),
                                    "ld", flip=True)
                                s.w(V, th, tx1l)
                                tx2 = s.op(V, lambda e, k=k, kl=kl: e.tensor_add(
                                    tscr[:, kl * SW:(kl + 1) * SW], hsl[:, kl * SW:(kl + 1) * SW],
                                    fbufA[:, k * SW:(k + 1) * SW]))
                                s.w(SY, tx2)
                                tyd = s.dma(SY, lambda e, k=k, sw=sw, kl=kl: e.dma_start(
                                    yshard[k][:, sw * SW:(sw + 1) * SW], tscr[:, kl * SW:(kl + 1) * SW]),
                                    "ou", flip=True)
                                tvm = s.op(V, lambda e, kl=kl: e.tensor_scalar(
                                    hsl[:, kl * SW:(kl + 1) * SW], tscr[:, kl * SW:(kl + 1) * SW],
                                    m0[:, 0:1], None, ALU.mult))
                                s.w(SY, tvm)
                                tcxk = s.dma(SY, lambda e, k=k, sw=sw, kl=kl: e.dma_start(
                                    ccx2_in[sw * SW + (k // 2) * 128:sw * SW + (k // 2) * 128 + 128,
                                            (k % 2) * SW:(k % 2 + 1) * SW],
                                    hsl[:, kl * SW:(kl + 1) * SW]), "hw", flip=True)
                                thw[k] = [tyd, tcxk]
                                thwF[sw * KC + k] = thw[k]
                                fblast[sw * KC + k] = tx2
                    return flush
                pend_ln2[0] = mk_ln2(sw, phB2)
            if pend_ln2[0] is not None:
                pend_ln2[0]()
                pend_ln2[0] = None
            s.barrier()

        s.w(G, *s.chtok("hw"))
        if nocc:
            tccx = s.dma(G, lambda g: g.dma_start(ccx2_out, ccx2_in[0:MTOK, :]), "wq", flip=True)
        else:
            tccx = s.collective("ReduceScatter", ALU.add, [ccx2_in], [ccx2_out])

        tewb = build_ewbs(s.token(V))
        s.w(SY, tccx)
        tx5 = None
        for k in range(KC):
            tx5 = s.dma(SY, lambda e, k=k: e.dma_start(
                fbufA[:, k * SW:(k + 1) * SW],
                ccx2_out[(k // 2) * 128:(k // 2) * 128 + 128, (k % 2) * SW:(k % 2 + 1) * SW]),
                "xf", flip=(k == 0))
        tbA, tbB = emit_ln(pf[4], pf[5], pf[5], pf[6], gates=(tx5, tewb))
        tgxg = None
        for k in range(KC):
            kl = k % 2
            s.w(V, tbA, tbB, tgxg)
            s.op(V, lambda e, k=k, kl=kl: e.tensor_mul(hsl[:, kl * SW:(kl + 1) * SW],
                                                       fbufA[:, k * SW:(k + 1) * SW], pf[5][:, 0:SW]))
            tt2 = s.op(V, lambda e, kl=kl: e.tensor_add(hsl[:, kl * SW:(kl + 1) * SW],
                                                        hsl[:, kl * SW:(kl + 1) * SW], pf[6][:, 0:SW]))
            s.w(A, tt2)
            tgxg = s.op(A, lambda e, k=k, kl=kl: e.activation(
                xg[:, k * SW:(k + 1) * SW], hsl[:, kl * SW:(kl + 1) * SW], AF.Identity, bias=0.0, scale=1.0))

        ydma = {}

        def wr_moe3(mo, td, src):
            s.w(V, td, ydma.get(mo - 2))
            ty = s.op(V, lambda e, mo=mo, src=src: e.tensor_add(
                hsl[:, (mo % 2) * SW:(mo % 2 + 1) * SW],
                fbufA[:, mo * SW:(mo + 1) * SW], src))
            s.w(SY, ty)
            ydma[mo] = s.dma(SY, lambda e, mo=mo: e.dma_start(
                ymoe3[mo], hsl[:, (mo % 2) * SW:(mo % 2 + 1) * SW]), "hw", flip=True)
            return ydma[mo]
        emit_moe(tgxg, wr_moe3)

        s.drain_stores()
        s.op(SY, lambda e: e.nop())
        s.replay(block)
    return nc


_NC_CACHE = {}


def kernel(L0_attn_x, L0_attn_Wq, L0_attn_bq, L0_attn_Wk, L0_attn_bk,
           L0_attn_Wv, L0_attn_bv, L0_attn_Wo, L0_attn_bo, L0_attn_omega,
           L0_attn_ln1_g, L0_attn_ln1_b, L0_attn_ln2_g, L0_attn_ln2_b,
           L0_attn_W1, L0_attn_bf1, L0_attn_W2, L0_attn_bf2,
           L0_ffn_scores, L0_ffn_expert_weights, L0_ffn_top_experts,
           L0_ffn_w1, L0_ffn_w2):
    x = np.ascontiguousarray(np.asarray(L0_attn_x, np.float32))
    B, S, _ = x.shape

    def fm(a):
        return np.ascontiguousarray(np.asarray(a, np.float32).T.reshape(KC, 128, -1))

    def cols(v):
        return np.asarray(v, np.float32).reshape(8, 128).T

    pvec = np.zeros((128, 312), np.float32)
    for l in range(L):
        base = l * 104
        pvec[:, base + 0:base + 8] = cols(L0_attn_bq[l])
        pvec[:, base + 8:base + 16] = cols(L0_attn_bk[l])
        pvec[:, base + 16:base + 24] = cols(L0_attn_bv[l])
        pvec[:, base + 24:base + 32] = cols(L0_attn_bo[l])
        pvec[:, base + 32:base + 40] = cols(L0_attn_ln1_g[l])
        pvec[:, base + 40:base + 48] = cols(L0_attn_ln1_b[l])
        pvec[:, base + 48:base + 56] = cols(L0_attn_ln2_g[l])
        pvec[:, base + 56:base + 64] = cols(L0_attn_ln2_b[l])
        pvec[:, base + 64:base + 72] = cols(L0_attn_bf2[l])
        pvec[:, base + 72:base + 104] = np.asarray(L0_attn_bf1[l], np.float32).reshape(32, 128).T
    rowv = np.ascontiguousarray(np.asarray(L0_attn_bv, np.float32).reshape(1, L * D))
    ew = np.asarray(L0_ffn_expert_weights, np.float32)

    if "nc" not in _NC_CACHE:
        _NC_CACHE["nc"] = build()
    nc = _NC_CACHE["nc"]

    in_maps = []
    for i in range(NCORE):
        b, half = i // 2, i % 2
        m0v = np.full((128, 1), 1.0 if i == 0 else 0.0, np.float32)
        sc = np.zeros((128, 8), np.float32)
        if i % 2 == 1:
            sc[:, i - 1] = 1.0
        in_maps.append({
            "xshard": fm(x[b, half * T:(half + 1) * T, :]),
            "xmoe": fm(x[0, i * MTOK:(i + 1) * MTOK, :]),
            "ewT": np.ascontiguousarray(ew[i * MTOK:(i + 1) * MTOK, :].T),
            "pvec": pvec, "rowv": rowv,
            "omg_in": np.asarray(L0_attn_omega, np.float32),
            "wq_in": np.asarray(L0_attn_Wq, np.float32),
            "wk_in": np.asarray(L0_attn_Wk, np.float32),
            "wv_in": np.asarray(L0_attn_Wv, np.float32),
            "wo_in": np.asarray(L0_attn_Wo, np.float32),
            "w1_in": np.asarray(L0_attn_W1, np.float32),
            "w2_in": np.asarray(L0_attn_W2, np.float32),
            "mw1_in": np.asarray(L0_ffn_w1, np.float32),
            "mw2_in": np.asarray(L0_ffn_w2, np.float32),
            "mask0": m0v, "scoef": sc,
        })
    res = run_bass_kernel_spmd(nc, in_maps, core_ids=list(range(NCORE)))
    out = np.empty((B, S, D), np.float32)
    for i in range(NCORE):
        b, half = i // 2, i % 2
        ys = np.asarray(res.results[i]["yshard"])
        out[b, half * T:(half + 1) * T, :] = ys.transpose(2, 0, 1).reshape(T, D)
    for i in range(NCORE):
        ym = np.asarray(res.results[i]["ymoe3"])
        out[0, i * MTOK:(i + 1) * MTOK, :] = ym.transpose(2, 0, 1).reshape(MTOK, D)
    return out


# version 28
# speedup vs baseline: 1.0241x; 1.0143x over previous
import numpy as np
import concourse.bass as bass
import concourse.mybir as mybir
from concourse.bass_utils import run_bass_kernel_spmd

F32 = mybir.dt.float32
BF16 = mybir.dt.bfloat16
AF = mybir.ActivationFunctionType
ALU = mybir.AluOpType

NCORE = 8
D = 1024
KC = 8
T = 4096
NSW = 8
SW = 512
CH = 128
NCHS = 4
NP = 8
DH = 64
M = 32
DFF = 4096
MJ = 32
MOE_I = 8192
MOE_J = 64
MTOK = 512
L = 3
LN_EPS = 1e-5
ATTN_EPS = 1e-6
SC = float(DH) ** -0.25
CSQ = 0.0625
LOGM = float(np.log((2.0 * M) ** -0.5))
LOGEPS = float(np.log(ATTN_EPS))
GC1 = 0.044715
GC2 = float(2.0 * np.sqrt(2.0 / np.pi))
RG = [[0, 1, 2, 3, 4, 5, 6, 7]]
TG = 512
LN_V2 = False

ENGINES = ("tensor", "vector", "scalar", "gpsimd", "sync")
G, V, A, TE, SY = "gpsimd", "vector", "scalar", "tensor", "sync"


class Sched:
    AUTO_CHAIN = ("vector", "scalar", "gpsimd")

    def __init__(self, nc, ctx):
        self.nc = nc
        self.ops = {e: [] for e in ENGINES}
        self.cnt = {}
        self.last_waited = {e: {} for e in ENGINES}
        self.sems = {}
        self._ctx = ctx
        for e in ENGINES:
            self.sems[e] = ctx.enter_context(nc.semaphore(f"s_{e}"))
            self.cnt[e] = 0
        self.chan = {}

    def op(self, eng, fn):
        sem = self.sems[eng]
        if eng in self.AUTO_CHAIN and self.cnt[eng] > 0:
            prev = self.cnt[eng]
            if self.last_waited[eng].get(eng, 0) < prev:
                self.last_waited[eng][eng] = prev
                def emitw(e, sem=sem, val=prev):
                    e.wait_ge(sem, val)
                self.ops[eng].append(emitw)
        self.cnt[eng] += 1
        want = self.cnt[eng]
        def emit(e, fn=fn, sem=sem):
            fn(e).then_inc(sem, 1)
        self.ops[eng].append(emit)
        return (eng, want)

    def mkchan(self, name):
        sems = [self._ctx.enter_context(self.nc.semaphore(f"c_{name}{i}"))
                for i in range(2)]
        self.chan[name] = {"sems": sems, "slot": 0}
        for i in range(2):
            self.sems[f"{name}{i}"] = sems[i]
            self.cnt[f"{name}{i}"] = 0

    def dma(self, eng, fn, ch, flip=False):
        c = self.chan[ch]
        if flip:
            c["slot"] ^= 1
        slot = c["slot"]
        key = f"{ch}{slot}"
        sem = c["sems"][slot]
        if flip and self.cnt[key] > 0:
            prev = self.cnt[key]
            if self.last_waited[eng].get(key, 0) < prev:
                self.last_waited[eng][key] = prev
                def emitw(e, sem=sem, val=prev):
                    e.wait_ge(sem, val)
                self.ops[eng].append(emitw)
        self.cnt[key] += 16
        want = self.cnt[key]
        def emit(e, fn=fn, sem=sem):
            fn(e).then_inc(sem, 16)
        self.ops[eng].append(emit)
        return (key, want)

    def chtok(self, ch):
        return [(f"{ch}{i}", self.cnt[f"{ch}{i}"]) for i in range(2)]

    def collective(self, kind, op, ins, outs):
        name = f"cc{len([k for k in self.sems if k.startswith('cc')])}"
        sem = self._ctx.enter_context(self.nc.semaphore(name))
        self.sems[name] = sem
        self.cnt[name] = 1
        def emit(e, sem=sem):
            e.collective_compute(kind, op, replica_groups=RG,
                                 ins=[i.opt() for i in ins],
                                 outs=[o.opt() for o in outs]).then_inc(sem)
        self.ops["gpsimd"].append(emit)
        return (name, 1)

    def w(self, eng, *tokens):
        for t in tokens:
            if t is None:
                continue
            (src, val) = t
            if src == eng:
                continue
            if val <= self.last_waited[eng].get(src, 0):
                continue
            self.last_waited[eng][src] = val
            sem = self.sems[src]
            def emit(e, sem=sem, val=val):
                e.wait_ge(sem, val)
            self.ops[eng].append(emit)

    def token(self, eng):
        return (eng, self.cnt[eng])

    def drain_stores(self):
        self.w(SY, *self.chtok("st"), *self.chtok("ou"),
               *self.chtok("hw"), *self.chtok("sn"))

    def barrier(self):
        self.drain_stores()
        self.op(SY, lambda e: e.nop())
        toks = [self.token(e) for e in ENGINES]
        for e in ENGINES:
            self.w(e, *toks)

    def replay(self, block):
        for e in ENGINES:
            ops = self.ops[e]
            def mk(ops):
                def run(engine):
                    for fn in ops:
                        fn(engine)
                return run
            getattr(block, e)(mk(ops))


def build(nocc=False):
    import contextlib
    nc = bass.Bass("TRN2", target_bir_lowering=False)
    DT = nc.dram_tensor
    xshard = DT("xshard", [KC, 128, T], F32, kind="ExternalInput").ap()
    xmoe = DT("xmoe", [KC, 128, MTOK], F32, kind="ExternalInput").ap()
    ewT = DT("ewT", [2, MTOK], F32, kind="ExternalInput").ap()
    pvec = DT("pvec", [128, 312], F32, kind="ExternalInput").ap()
    rowv = DT("rowv", [1, L * D], F32, kind="ExternalInput").ap()
    omg_in = DT("omg_in", [L, DH, M], F32, kind="ExternalInput").ap()
    wq_in = DT("wq_in", [L, D, D], F32, kind="ExternalInput").ap()
    wk_in = DT("wk_in", [L, D, D], F32, kind="ExternalInput").ap()
    wv_in = DT("wv_in", [L, D, D], F32, kind="ExternalInput").ap()
    wo_in = DT("wo_in", [L, D, D], F32, kind="ExternalInput").ap()
    w1_in = DT("w1_in", [L, D, DFF], F32, kind="ExternalInput").ap()
    w2_in = DT("w2_in", [L, DFF, D], F32, kind="ExternalInput").ap()
    mw1_in = DT("mw1_in", [D, MOE_I], F32, kind="ExternalInput").ap()
    mw2_in = DT("mw2_in", [MOE_I, D], F32, kind="ExternalInput").ap()
    mask0 = DT("mask0", [128, 1], F32, kind="ExternalInput").ap()
    scoef = DT("scoef", [128, 8], F32, kind="ExternalInput").ap()
    yshard = DT("yshard", [KC, 128, T], F32, kind="ExternalOutput").ap()
    ymoe3 = DT("ymoe3", [KC, 128, MTOK], F32, kind="ExternalOutput").ap()
    st_x1 = DT("st_x1", [KC, 128, T], F32).ap()
    st_a = DT("st_a", [KC, 128, T], F32).ap()
    st_h = DT("st_h", [KC, 128, T], F32).ap()
    phk_d = DT("phk_d", [NP, 128, T], BF16).ap()
    vraw_d = DT("vraw_d", [T, D], BF16).ap()
    snap_d = DT("snap_d", [32, 128, 520], F32).ap()
    ccm1_in = DT("ccm1_in", [128, KC * SW], F32).ap()
    ccm1_out = DT("ccm1_out", [NCORE * 128, KC * SW], F32,
                  addr_space=("Local" if nocc else "Shared")).ap()
    ccst_in = DT("ccst_in", [128, 520], F32).ap()
    ccst_out = DT("ccst_out", [NCORE * 128, 520], F32,
                  addr_space=("Local" if nocc else "Shared")).ap()
    ccx2_in = DT("ccx2_in", [T, D], F32).ap()
    ccx2_out = DT("ccx2_out", [MTOK, D], F32).ap()

    ctx = contextlib.ExitStack()
    with ctx:
        Sb = lambda *a: ctx.enter_context(nc.sbuf_tensor(*a))
        Ps = lambda *a: ctx.enter_context(nc.psum_tensor(*a))
        pv = Sb("pv", [128, 312], F32)
        rvb = Sb("rvb", [1, D], BF16)
        omgs = Sb("omgs", [128, L * M], F32)
        obs = Sb("obs", [128, L * 128], BF16)
        cbt = Sb("cbt", [128, 128], BF16)
        hm2 = Sb("hm2", [128, 2], BF16)
        identb = Sb("identb", [128, 128], BF16)
        maskf = Sb("maskf", [128, 128], F32)
        onesf = Sb("onesf", [128, 128], F32)
        onesbr = Sb("onesbr", [1, 128], BF16)
        halfo = Sb("halfo", [2, 1], F32)
        biasm = Sb("biasm", [128, 2], F32)
        m0 = Sb("m0", [128, 1], F32)
        scf = Sb("scf", [128, 8], F32)
        fbufA = Sb("fbufA", [128, KC * SW], F32)
        tscr = Sb("tscr", [128, 2 * SW], F32)
        hsl = Sb("hsl", [128, 2 * SW], F32)
        rows = Sb("rows", [1, 4 * SW], F32)
        wreg = Sb("wreg", [128, 65536], BF16)
        arena = Sb("arena", [128, 20480], BF16)
        srun = arena[:, 0:2080].bitcast(F32)
        sinit = arena[:, 2080:3120].bitcast(F32)
        srq = arena[:, 3120:4160].bitcast(F32)
        snapb = arena[:, 4160:6240].bitcast(F32)
        sbe = arena[:, 6240:6760]
        sbo = arena[:, 6760:7280]
        vts = arena[:, 7280:9360]
        phkt = arena[:, 9360:10384]
        eps2 = arena[0:2, 10384:14480]
        h1f = arena[:, 0:16384]
        hbf = arena[:, 16384:20480]
        ATMB = 14480
        atk0 = lambda pi, hh: arena[:, ATMB + pi * 768 + hh * 256:ATMB + pi * 768 + (hh + 1) * 256]
        atk1 = lambda pi, hh: arena[:, ATMB + pi * 768 + 512 + hh * 128:ATMB + pi * 768 + 512 + (hh + 1) * 128]
        vts1 = arena[:, 16528:18608]
        rdbs2 = [arena[:, 16016:16528].bitcast(F32),
                 arena[:, 19632:20144].bitcast(F32)]
        mw = arena[:, 9360:10384].bitcast(F32)
        sq = arena[:, 14480:15504].bitcast(F32)
        ewbs = arena[:, 15504:16528].bitcast(F32)
        ewts = arena[0:2, 16528:17552].bitcast(F32)
        idf = fbufA[:, 0:128]
        rvf = fbufA[0:1, SW:SW + D]

        wqsb = wreg[:, 0:8192]
        wksb = wreg[:, 8192:16384]
        wvsb = wreg[:, 16384:24576]
        wosb = wreg[:, 24576:32768]
        xbw = [wreg[:, 32768 + i * 4096:32768 + (i + 1) * 4096] for i in range(2)]
        ksb = wreg[:, 40960:45056]
        k2sb = wreg[:, 45056:49152]
        phkb = wreg[:, 49152:53248]
        phqb = wreg[:, 53248:57344]
        asw = wreg[:, 57344:61440]
        vstage = wreg[:, 61440:63488]
        phz = wreg[:, 61440:65536]
        w1sb = wreg[:, 0:32768]
        w2sb = wreg[:, 32768:65536]
        mw1b = [wreg[:, i * 1024:(i + 1) * 1024] for i in range(2)]
        mw2b = [wreg[:, 4096 + i * 8192:4096 + (i + 1) * 8192] for i in range(2)]
        h1m = wreg[:, 20480:53248]
        xg = wreg[:, 53248:57344]

        pf = [Ps(f"pf{i}", [128, 512], F32) for i in range(7)]
        pbt = Ps("pbt", [128, 1024], BF16)
        pbtF = pbt[:, 0:1024].bitcast(F32)

        s = Sched(nc, ctx)
        for chn in ("wq", "wf", "xb", "xf", "st", "ld", "ou", "ph", "sn", "hw"):
            s.mkchan(chn)
        block = ctx.enter_context(nc.Block())

        def pvcol(l, name, k):
            off = {"bq": 0, "bk": 8, "bv": 16, "bo": 24, "ln1g": 32, "ln1b": 40,
                   "ln2g": 48, "ln2b": 56, "bf2": 64, "bf1": 72}[name]
            cidx = l * 104 + off + k
            return pv[:, cidx:cidx + 1]

        def do_cc(kind, op, cin, cout, nrow):
            if nocc:
                return s.dma(G, lambda g: g.dma_start(cout[0:nrow, :], cin), "ou")
            return s.collective(kind, op, [cin], [cout])

        s.dma(G, lambda g: g.dma_start(pv[:, :], pvec), "wq")
        s.dma(G, lambda g: g.dma_start(m0[:, :], mask0), "wq")
        s.dma(G, lambda g: g.dma_start(scf[:, :], scoef), "wq")
        tldw = None
        for l in range(L):
            s.dma(G, lambda g, l=l: g.dma_start(omgs[0:64, l * M:(l + 1) * M], omg_in[l]), "wq")
            tldw = s.dma(G, lambda g, l=l: g.dma_start(omgs[64:128, l * M:(l + 1) * M], omg_in[l]), "wq")
        s.op(G, lambda g: g.memset(onesf[:, :], 1.0))
        s.op(G, lambda g: g.memset(onesbr[:, :], 1.0))
        s.op(G, lambda g: g.memset(halfo[:, :], 0.5))
        s.op(G, lambda g: g.memset(biasm[:, 0:1], LOGM))
        s.op(G, lambda g: g.memset(biasm[:, 1:2], LOGEPS))
        s.op(G, lambda g: g.memset(cbt[:, :], 0.0))
        s.op(G, lambda g: g.memset(hm2[:, :], 0.0))
        s.op(G, lambda g: g.memset(obs[:, :], 0.0))
        s.op(G, lambda g: g.memset(sbe[:, :], 0.0))
        s.op(G, lambda g: g.memset(sbo[:, :], 0.0))
        for sl in range(2):
            for i in range(16):
                s.op(G, lambda g, sl=sl, i=i: g.memset(
                    vts[:, sl * 1040 + i * 65 + 64:sl * 1040 + i * 65 + 65], 1.0))
        s.op(G, lambda g: g.affine_select(idf[:, :], onesf[:, :], pattern=[[1, 128]],
                                          compare_op=ALU.is_equal, fill=0.0,
                                          base=0, channel_multiplier=-1))
        tgm = s.op(G, lambda g: g.affine_select(maskf[:, :], onesf[:, :], pattern=[[1, 128]],
                                               compare_op=ALU.is_ge, fill=0.0,
                                               base=0, channel_multiplier=-1))
        s.w(A, tgm, tldw)
        s.op(A, lambda e: e.activation(identb[:, :], idf[:, :], AF.Identity, bias=0.0, scale=1.0))
        s.op(A, lambda e: e.activation(cbt[0:64, 0:64], onesf[0:64, 0:64], AF.Identity, bias=0.0, scale=-CSQ))
        s.op(A, lambda e: e.activation(cbt[64:128, 64:128], onesf[64:128, 0:64], AF.Identity, bias=0.0, scale=-CSQ))
        s.op(A, lambda e: e.activation(hm2[0:64, 0:1], onesf[0:64, 0:1], AF.Identity, bias=0.0, scale=CSQ))
        s.op(A, lambda e: e.activation(hm2[64:128, 1:2], onesf[64:128, 0:1], AF.Identity, bias=0.0, scale=CSQ))
        tcons = None
        for l in range(L):
            ob = obs[:, l * 128:(l + 1) * 128]
            om0 = omgs[0:64, l * M:(l + 1) * M]
            om1 = omgs[64:128, l * M:(l + 1) * M]
            s.op(A, lambda e, ob=ob, om0=om0: e.activation(ob[0:64, 0:32], om0, AF.Identity, bias=0.0, scale=SC))
            s.op(A, lambda e, ob=ob, om0=om0: e.activation(ob[0:64, 32:64], om0, AF.Identity, bias=0.0, scale=-SC))
            s.op(A, lambda e, ob=ob, om1=om1: e.activation(ob[64:128, 64:96], om1, AF.Identity, bias=0.0, scale=SC))
            tcons = s.op(A, lambda e, ob=ob, om1=om1: e.activation(ob[64:128, 96:128], om1, AF.Identity, bias=0.0, scale=-SC))

        def build_ewbs(gate):
            tld = s.dma(G, lambda g: g.dma_start(ewts[:, :], ewT), "wq", flip=True)
            s.w(TE, tld, gate, tcons)
            t1 = s.op(TE, lambda e: e.matmul(pf[4][0:1, 0:MTOK], halfo[:, :], ewts[:, :], start=True, stop=True))
            s.w(V, t1)
            t2 = s.op(V, lambda e: e.tensor_copy(rows[0:1, 0:MTOK], pf[4][0:1, 0:MTOK]))
            s.w(TE, t2)
            t3 = s.op(TE, lambda e: e.matmul(pf[6][:, 0:MTOK], onesf[0:1, :], rows[0:1, 0:MTOK], start=True, stop=True))
            s.w(A, t3)
            return s.op(A, lambda e: e.activation(ewbs[:, :], pf[6][:, 0:MTOK], AF.Identity, bias=0.0, scale=1.0))

        def emit_ln_v1(rbA, rbB, pbA, pbB, gates=()):
            s.w(V, *gates)
            s.w(A, *gates)
            tv = s.op(V, lambda e: e.tensor_add(tscr[:, 0:SW], fbufA[:, 0:SW], fbufA[:, SW:2 * SW]))
            for k in range(2, KC):
                tv = s.op(V, lambda e, k=k: e.tensor_add(tscr[:, 0:SW], tscr[:, 0:SW], fbufA[:, k * SW:(k + 1) * SW]))
            ta = s.op(A, lambda e: e.activation(sq[:, :], fbufA[:, 0:SW], AF.Square))
            s.w(V, ta)
            tq = s.op(V, lambda e: e.tensor_copy(tscr[:, SW:2 * SW], sq[:, :]))
            for k in range(1, KC):
                s.w(A, tq)
                ta = s.op(A, lambda e, k=k: e.activation(sq[:, :], fbufA[:, k * SW:(k + 1) * SW], AF.Square))
                s.w(V, ta)
                tq = s.op(V, lambda e: e.tensor_add(tscr[:, SW:2 * SW], tscr[:, SW:2 * SW], sq[:, :]))
            s.w(TE, tq, tv)
            tm1 = s.op(TE, lambda e, rbA=rbA: e.matmul(rbA[0:1, 0:SW], onesf[:, 0:1], tscr[:, 0:SW], start=True, stop=True))
            tm2 = s.op(TE, lambda e, rbB=rbB: e.matmul(rbB[0:1, 0:SW], onesf[:, 0:1], tscr[:, SW:2 * SW], start=True, stop=True))
            R = lambda i: rows[0:1, i * SW:(i + 1) * SW]
            s.w(V, tm1, tm2)
            s.op(V, lambda e, rbA=rbA: e.tensor_scalar(R(0), rbA[0:1, 0:SW], 1.0 / D, None, ALU.mult))
            tr0 = s.op(V, lambda e, rbB=rbB: e.tensor_scalar(R(1), rbB[0:1, 0:SW], 1.0 / D, LN_EPS, ALU.mult, ALU.add))
            s.w(A, tr0)
            ta1 = s.op(A, lambda e: e.activation(R(2), R(0), AF.Square))
            s.w(V, ta1)
            tvv = s.op(V, lambda e: e.tensor_sub(R(1), R(1), R(2)))
            s.w(A, tvv)
            s.op(A, lambda e: e.activation(R(2), R(1), AF.Ln))
            tsq2 = s.op(A, lambda e: e.activation(R(3), R(2), AF.Exp, bias=0.0, scale=-0.5))
            s.w(V, tsq2)
            s.op(V, lambda e: e.tensor_mul(R(2), R(0), R(3)))
            trB = s.op(V, lambda e: e.tensor_scalar(R(2), R(2), -1.0, None, ALU.mult))
            s.w(TE, trB, tsq2)
            tbA = s.op(TE, lambda e, pbA=pbA: e.matmul(pbA[:, 0:SW], onesf[0:1, :], R(3), start=True, stop=True))
            tbB = s.op(TE, lambda e, pbB=pbB: e.matmul(pbB[:, 0:SW], onesf[0:1, :], R(2), start=True, stop=True))
            return tbA, tbB

        def emit_ln(rbA, rbB, pbA, pbB, gates=()):
            s.w(TE, *gates)
            s.w(A, *gates)
            for k in range(KC):
                s.op(TE, lambda e, k=k, rbA=rbA: e.matmul(
                    rbA[0:1, 0:SW], onesf[:, 0:1], fbufA[:, k * SW:(k + 1) * SW],
                    start=(k == 0), stop=(k == KC - 1)))
            tmq = {}
            for k in range(KC):
                kl = k % 2
                if k >= 2:
                    s.w(A, tmq[k - 2])
                tsq = s.op(A, lambda e, k=k, kl=kl: e.activation(
                    tscr[:, kl * SW:(kl + 1) * SW], fbufA[:, k * SW:(k + 1) * SW], AF.Square))
                s.w(TE, tsq)
                tmq[k] = s.op(TE, lambda e, k=k, kl=kl, rbB=rbB: e.matmul(
                    rbB[0:1, 0:SW], onesf[:, 0:1], tscr[:, kl * SW:(kl + 1) * SW],
                    start=(k == 0), stop=(k == KC - 1)))
            R = lambda i: rows[0:1, i * SW:(i + 1) * SW]
            s.w(V, tmq[KC - 1])
            s.op(V, lambda e, rbA=rbA: e.tensor_scalar(R(0), rbA[0:1, 0:SW], 1.0 / D, None, ALU.mult))
            tr0 = s.op(V, lambda e, rbB=rbB: e.tensor_scalar(R(1), rbB[0:1, 0:SW], 1.0 / D, LN_EPS, ALU.mult, ALU.add))
            s.w(A, tr0)
            ta1 = s.op(A, lambda e: e.activation(R(2), R(0), AF.Square))
            s.w(V, ta1)
            tvv = s.op(V, lambda e: e.tensor_sub(R(1), R(1), R(2)))
            s.w(A, tvv)
            s.op(A, lambda e: e.activation(R(2), R(1), AF.Ln))
            trs = s.op(A, lambda e: e.activation(R(3), R(2), AF.Exp, bias=0.0, scale=-0.5))
            s.w(V, trs)
            s.op(V, lambda e: e.tensor_mul(R(2), R(0), R(3)))
            trB = s.op(V, lambda e: e.tensor_scalar(R(2), R(2), -1.0, None, ALU.mult))
            s.w(TE, trB, trs)
            tbA = s.op(TE, lambda e, pbA=pbA: e.matmul(pbA[:, 0:SW], onesf[0:1, :], R(3), start=True, stop=True))
            tbB = s.op(TE, lambda e, pbB=pbB: e.matmul(pbB[:, 0:SW], onesf[0:1, :], R(2), start=True, stop=True))
            return tbA, tbB

        if not LN_V2:
            emit_ln = emit_ln_v1

        def emit_ln_sums(gates=()):
            s.w(V, *gates)
            s.w(A, *gates)
            tv = s.op(V, lambda e: e.tensor_add(tscr[:, 0:SW], fbufA[:, 0:SW], fbufA[:, SW:2 * SW]))
            for k in range(2, KC):
                tv = s.op(V, lambda e, k=k: e.tensor_add(tscr[:, 0:SW], tscr[:, 0:SW], fbufA[:, k * SW:(k + 1) * SW]))
            hq = hsl[:, 0:SW]
            ta = s.op(A, lambda e, hq=hq: e.activation(hq, fbufA[:, 0:SW], AF.Square))
            s.w(V, ta)
            tq = s.op(V, lambda e, hq=hq: e.tensor_copy(tscr[:, SW:2 * SW], hq))
            for k in range(1, KC):
                s.w(A, tq)
                ta = s.op(A, lambda e, k=k, hq=hq: e.activation(hq, fbufA[:, k * SW:(k + 1) * SW], AF.Square))
                s.w(V, ta)
                tq = s.op(V, lambda e, hq=hq: e.tensor_add(tscr[:, SW:2 * SW], tscr[:, SW:2 * SW], hq))

            def phaseB(rbA, rbB, pbA, pbB):
                s.w(TE, tq, tv)
                tm1 = s.op(TE, lambda e, rbA=rbA: e.matmul(rbA[0:1, 0:SW], onesf[:, 0:1], tscr[:, 0:SW], start=True, stop=True))
                tm2 = s.op(TE, lambda e, rbB=rbB: e.matmul(rbB[0:1, 0:SW], onesf[:, 0:1], tscr[:, SW:2 * SW], start=True, stop=True))
                R = lambda i: rows[0:1, i * SW:(i + 1) * SW]
                s.w(V, tm1, tm2)
                s.op(V, lambda e, rbA=rbA: e.tensor_scalar(R(0), rbA[0:1, 0:SW], 1.0 / D, None, ALU.mult))
                tr0 = s.op(V, lambda e, rbB=rbB: e.tensor_scalar(R(1), rbB[0:1, 0:SW], 1.0 / D, LN_EPS, ALU.mult, ALU.add))
                s.w(A, tr0)
                ta1 = s.op(A, lambda e: e.activation(R(2), R(0), AF.Square))
                s.w(V, ta1)
                tvv = s.op(V, lambda e: e.tensor_sub(R(1), R(1), R(2)))
                s.w(A, tvv)
                s.op(A, lambda e: e.activation(R(2), R(1), AF.Ln))
                tsq2 = s.op(A, lambda e: e.activation(R(3), R(2), AF.Exp, bias=0.0, scale=-0.5))
                s.w(V, tsq2)
                s.op(V, lambda e: e.tensor_mul(R(2), R(0), R(3)))
                trB = s.op(V, lambda e: e.tensor_scalar(R(2), R(2), -1.0, None, ALU.mult))
                s.w(TE, trB, tsq2)
                tbA = s.op(TE, lambda e, pbA=pbA: e.matmul(pbA[:, 0:SW], onesf[0:1, :], R(3), start=True, stop=True))
                tbB = s.op(TE, lambda e, pbB=pbB: e.matmul(pbB[:, 0:SW], onesf[0:1, :], R(2), start=True, stop=True))
                return tbA, tbB
            return phaseB

        def emit_moe(tgxg, wr_fn):
            gelu_done = None
            gelu_hist = {}
            cons1 = {}
            for j in range(MOE_J):
                sl = j % 2
                if cons1.get(j - 2) is not None:
                    s.w(G, cons1[j - 2])
                tl = s.dma(G, lambda g, j=j, sl=sl: g.dma_start(
                    mw1b[sl].rearrange("p (kc i) -> p kc i", i=128),
                    mw1_in[:, j * 128:(j + 1) * 128].rearrange("(kc p) i -> p kc i", p=128)),
                    "wf", flip=True)
                pm = pf[j % 2]
                s.w(TE, tl, tgxg, gelu_hist.get(j - 2))
                tk = None
                for k in range(KC):
                    tk = s.op(TE, lambda e, k=k, sl=sl, pm=pm: e.matmul(
                        pm[:, 0:SW], mw1b[sl][:, k * 128:(k + 1) * 128],
                        xg[:, k * SW:(k + 1) * SW], start=(k == 0), stop=(k == KC - 1)))
                cons1[j] = tk
                s.w(A, tk, gelu_done)
                th0 = s.op(A, lambda e, pm=pm: e.activation(tscr[:, 0:SW], pm[:, 0:SW], AF.Identity, bias=0.0, scale=1.0))
                th1 = s.op(A, lambda e, pm=pm: e.activation(sq[:, :], pm[:, 0:SW], AF.Square))
                s.w(V, th1, th0)
                s.op(V, lambda e: e.tensor_scalar(sq[:, :], sq[:, :], GC1, 1.0, ALU.mult, ALU.add))
                tg3 = s.op(V, lambda e: e.tensor_mul(sq[:, :], sq[:, :], tscr[:, 0:SW]))
                s.w(A, tg3)
                tg4 = s.op(A, lambda e: e.activation(sq[:, :], sq[:, :], AF.Sigmoid, bias=0.0, scale=GC2))
                s.w(V, tg4)
                gelu_done = s.op(V, lambda e, j=j: e.tensor_mul(h1m[:, j * SW:(j + 1) * SW], sq[:, :], tscr[:, 0:SW]))
                gelu_hist[j] = gelu_done
            cons2 = {}
            tds = {}
            last = None
            for mo in range(KC):
                sl = mo % 2
                if cons2.get(mo - 2) is not None:
                    s.w(G, cons2[mo - 2])
                tl = s.dma(G, lambda g, mo=mo, sl=sl: g.dma_start(
                    mw2b[sl].rearrange("p (j i) -> p j i", i=128),
                    mw2_in[:, mo * 128:(mo + 1) * 128].rearrange("(j p) i -> p j i", p=128)),
                    "wf", flip=True)
                pm = pf[2 + mo % 2]
                s.w(TE, tl, gelu_done, tds.get(mo - 2))
                tk = None
                for j in range(MOE_J):
                    tk = s.op(TE, lambda e, j=j, sl=sl, pm=pm: e.matmul(
                        pm[:, 0:SW], mw2b[sl][:, j * 128:(j + 1) * 128],
                        h1m[:, j * SW:(j + 1) * SW], start=(j == 0), stop=(j == MOE_J - 1)))
                cons2[mo] = tk
                s.w(V, tk, tewb)
                td = s.op(V, lambda e, pm=pm: e.tensor_mul(tscr[:, SW:2 * SW], pm[:, 0:SW], ewbs[:, :]))
                tds[mo] = td
                last = wr_fn(mo, td, tscr[:, SW:2 * SW])
                s.w(V, last)
            return last

        tewb = build_ewbs(None)
        s.w(SY, tcons, tewb)
        tx = None
        for k in range(KC):
            tx = s.dma(SY, lambda e, k=k: e.dma_start(fbufA[:, k * SW:(k + 1) * SW], xmoe[k]), "xf")
        tbA, tbB = emit_ln(pf[4], pf[5], pf[5], pf[6], gates=(tx, tewb))
        tgxg = None
        for k in range(KC):
            kl = k % 2
            s.w(V, tbA, tbB, tgxg)
            s.op(V, lambda e, k=k, kl=kl: e.tensor_mul(tscr[:, kl * SW:(kl + 1) * SW], fbufA[:, k * SW:(k + 1) * SW], pf[5][:, 0:SW]))
            tt2 = s.op(V, lambda e, kl=kl: e.tensor_add(tscr[:, kl * SW:(kl + 1) * SW], tscr[:, kl * SW:(kl + 1) * SW], pf[6][:, 0:SW]))
            s.w(A, tt2)
            tgxg = s.op(A, lambda e, k=k, kl=kl: e.activation(xg[:, k * SW:(k + 1) * SW], tscr[:, kl * SW:(kl + 1) * SW], AF.Identity, bias=0.0, scale=1.0))

        def wr_moe1(mo, td, src):
            s.w(SY, td)
            return s.dma(SY, lambda e, mo=mo, src=src: e.dma_start(
                ccm1_in[:, mo * SW:(mo + 1) * SW], src), "hw", flip=True)
        tm1w = emit_moe(tgxg, wr_moe1)
        s.drain_stores()
        s.w(G, *s.chtok("hw"))
        if nocc:
            for r in range(NCORE):
                tccm1 = s.dma(G, lambda g, r=r: g.dma_start(
                    ccm1_out[r * 128:(r + 1) * 128, :], ccm1_in), "wq", flip=(r == 0))
        else:
            tccm1 = s.collective("AllGather", ALU.bypass, [ccm1_in], [ccm1_out])
        s.barrier()

        amul = {}
        thwA = {}
        tx1s = {}
        stt_prev = None
        tbA_prev = None
        for sw in range(NSW):
            txl = None
            for k in range(KC):
                s.w(SY, amul.get((sw - 1) * KC + k), tx1s.get((sw - 1) * KC + k))
                txl = s.dma(SY, lambda e, k=k, sw=sw: e.dma_start(
                    fbufA[:, k * SW:(k + 1) * SW], xshard[k][:, sw * SW:(sw + 1) * SW]),
                    "xf", flip=(k == 0))
            tvmask = None
            for k in range(KC):
                kl = k % 2
                if k >= 2:
                    s.w(SY, tvmask)
                else:
                    s.w(SY, tccm1, stt_prev, tbA_prev)
                tgl = s.dma(SY, lambda e, k=k, sw=sw, kl=kl: e.dma_start(
                    tscr[:, kl * SW:(kl + 1) * SW],
                    ccm1_out[sw * 128:(sw + 1) * 128, k * SW:(k + 1) * SW]),
                    "ld", flip=True)
                s.w(V, txl, tgl, tvmask)
                tvmask = s.op(V, lambda e, k=k, kl=kl: e.scalar_tensor_tensor(
                    fbufA[:, k * SW:(k + 1) * SW], tscr[:, kl * SW:(kl + 1) * SW],
                    m0[:, 0:1], fbufA[:, k * SW:(k + 1) * SW], ALU.mult, ALU.add))
                s.w(SY, tvmask)
                tx1s[sw * KC + k] = s.dma(SY, lambda e, k=k, sw=sw: e.dma_start(
                    st_x1[k][:, sw * SW:(sw + 1) * SW], fbufA[:, k * SW:(k + 1) * SW]),
                    "st", flip=True)
            stt_prev = tvmask
            tbA, tbB = emit_ln(pf[0], pf[1], pf[1], pf[2],
                               gates=(tvmask, amul.get((sw - 1) * KC + KC - 1)))
            tbA_prev = tbA
            for k in range(KC):
                kl = k % 2
                s.w(V, tbA, tbB, thwA.get(sw * KC + k - 2))
                amul[sw * KC + k] = s.op(V, lambda e, k=k, kl=kl: e.tensor_mul(
                    hsl[:, kl * SW:(kl + 1) * SW],
                    fbufA[:, k * SW:(k + 1) * SW], pf[1][:, 0:SW]))
                tt2 = s.op(V, lambda e, kl=kl: e.tensor_add(hsl[:, kl * SW:(kl + 1) * SW],
                                                            hsl[:, kl * SW:(kl + 1) * SW], pf[2][:, 0:SW]))
                s.w(SY, tt2)
                thwA[sw * KC + k] = s.dma(SY, lambda e, k=k, sw=sw, kl=kl: e.dma_start(
                    st_a[k][:, sw * SW:(sw + 1) * SW], hsl[:, kl * SW:(kl + 1) * SW]), "hw", flip=True)
        s.barrier()

        for l in range(L):
            ob = obs[:, l * 128:(l + 1) * 128]
            s.w(G, *s.chtok("st"), *s.chtok("hw"), *s.chtok("ou"))
            trv = s.dma(G, lambda g, l=l: g.dma_start(rvf[:, :], rowv[0:1, l * D:(l + 1) * D]), "wq")
            s.op(G, lambda g: g.memset(sbe[64:128, :], 0.0))
            s.op(G, lambda g: g.memset(sbo[0:64, :], 0.0))
            for sl_ in range(2):
                for i_ in range(16):
                    s.op(G, lambda g, sl_=sl_, i_=i_: g.memset(
                        vts[:, sl_ * 1040 + i_ * 65 + 64:sl_ * 1040 + i_ * 65 + 65], 1.0))
                    s.op(G, lambda g, sl_=sl_, i_=i_: g.memset(
                        vts1[:, sl_ * 1040 + i_ * 65 + 64:sl_ * 1040 + i_ * 65 + 65], 1.0))
            trvb0 = s.token(G)
            s.w(A, trv)
            trvb = s.op(A, lambda e: e.activation(rvb[:, :], rvf[:, :], AF.Identity, bias=0.0, scale=1.0))
            twq = None
            for wi, (dst, src) in enumerate(((wqsb, wq_in), (wksb, wk_in),
                                             (wvsb, wv_in), (wosb, wo_in))):
                for k in range(KC):
                    twq = s.dma(G, lambda g, dst=dst, src=src, k=k, l=l: g.dma_start(
                        dst[:, k * 1024:(k + 1) * 1024], src[l][k * 128:(k + 1) * 128, :]),
                        "wq", flip=(k == 0 and wi == 0))

            tsz = s.op(G, lambda g: g.memset(srun[:, :], 0.0))
            s.w(SY, tsz)
            tsnap = s.dma(SY, lambda e: e.dma_start(snap_d[0], srun[:, 0:520]), "sn", flip=True)
            tsadd = None
            tdm_last = None
            tevK = {}
            tfvK = {}
            tdmK = {}
            tvwK = {}
            kxdone = {}
            tvev = None
            ttr_done = None
            for sw in range(NSW):
                sl = sw % 2
                txb = None
                for k in range(KC):
                    if k == 0:
                        s.w(G, *(kxdone.get(sw - 2) or []))
                    txb = s.dma(G, lambda g, k=k, sw=sw, sl=sl: g.dma_start(
                        xbw[sl][:, k * SW:(k + 1) * SW], st_a[k][:, sw * SW:(sw + 1) * SW]),
                        "xb", flip=(k == 0))
                tev = {-2: (None, tevK.get((sw - 1) * KC + KC - 2)),
                       -1: (None, tevK.get((sw - 1) * KC + KC - 1))}
                for mc in range(KC):
                    pm = pf[mc % 2]
                    g2 = tev.get(mc - 2)
                    if mc == 0:
                        s.w(TE, tsadd)
                    s.w(TE, txb, twq, g2[1] if g2 else None)
                    tk = None
                    for k in range(KC):
                        tk = s.op(TE, lambda e, mc=mc, k=k, sl=sl, pm=pm: e.matmul(
                            pm[:, 0:SW], wksb[:, k * 1024 + mc * 128:k * 1024 + mc * 128 + 128],
                            xbw[sl][:, k * SW:(k + 1) * SW], start=(k == 0), stop=(k == KC - 1)))
                    s.w(A, tk)
                    te1 = s.op(A, lambda e, mc=mc, pm=pm, l=l: e.activation(
                        ksb[:, mc * SW:(mc + 1) * SW], pm[:, 0:SW], AF.Identity,
                        bias=pvcol(l, "bk", mc), scale=1.0))
                    te2 = s.op(A, lambda e, mc=mc, pm=pm, l=l: e.activation(
                        k2sb[:, mc * SW:(mc + 1) * SW], pm[:, 0:SW], AF.Square,
                        bias=pvcol(l, "bk", mc), scale=1.0))
                    tev[mc] = (te1, te2)
                    tevK[sw * KC + mc] = te2
                tfv = {-2: tfvK.get((sw - 1) * NP + NP - 2),
                       -1: tfvK.get((sw - 1) * NP + NP - 1)}
                for p in range(NP):
                    pm = pf[4 + p % 2]
                    if p == 0:
                        s.w(A, *s.chtok("st"))
                    s.w(TE, tev[p][0], tev[p][1], tfv.get(p - 2))
                    s.op(TE, lambda e, p=p, pm=pm, ob=ob: e.matmul(
                        pm[:, 0:SW], ob, ksb[:, p * SW:(p + 1) * SW], start=True, stop=False))
                    tk = s.op(TE, lambda e, p=p, pm=pm: e.matmul(
                        pm[:, 0:SW], cbt[:, :], k2sb[:, p * SW:(p + 1) * SW], start=False, stop=True))
                    s.w(A, tk)
                    tfv[p] = s.op(A, lambda e, p=p, pm=pm: e.activation(
                        phkb[:, p * SW:(p + 1) * SW], pm[:, 0:SW], AF.Exp,
                        bias=biasm[:, 0:1], scale=1.0))
                    tfvK[sw * NP + p] = tfv[p]
                    s.w(SY, tfv[p])
                    s.dma(SY, lambda e, p=p, sw=sw: e.dma_start(
                        phk_d[p][:, sw * SW:(sw + 1) * SW], phkb[:, p * SW:(p + 1) * SW]), "st")
                tvts = {}
                for c in range(NCHS):
                    cg = sw * NCHS + c
                    vsl = c % 2
                    s.w(TE, twq, tvev, trvb)
                    tk = None
                    for k in range(KC):
                        for half in range(2):
                            tk = s.op(TE, lambda e, c=c, k=k, half=half, sl=sl: e.matmul(
                                pf[2 + half][:, 0:SW],
                                xbw[sl][:, k * SW + c * CH:k * SW + c * CH + CH],
                                wvsb[:, k * 1024 + half * SW:k * 1024 + (half + 1) * SW],
                                start=(k == 0), stop=False))
                    for half in range(2):
                        tk = s.op(TE, lambda e, half=half: e.matmul(
                            pf[2 + half][:, 0:SW], onesbr[:, :],
                            rvb[0:1, half * SW:(half + 1) * SW],
                            start=False, stop=True))
                    s.w(A, tk, tvwK.get(cg - 2))
                    tva = s.op(A, lambda e, vsl=vsl: e.activation(
                        vstage[:, vsl * 1024:vsl * 1024 + SW], pf[2][:, 0:SW], AF.Identity, bias=0.0, scale=1.0))
                    tvev = s.op(A, lambda e, vsl=vsl: e.activation(
                        vstage[:, vsl * 1024 + SW:vsl * 1024 + 1024], pf[3][:, 0:SW], AF.Identity, bias=0.0, scale=1.0))
                    s.w(SY, tva, tvev)
                    tvwK[cg] = s.dma(SY, lambda e, cg=cg, vsl=vsl: e.dma_start(
                        vraw_d[cg * CH:(cg + 1) * CH, :], vstage[:, vsl * 1024:(vsl + 1) * 1024]), "st", flip=True)
                    s.w(V, tva, tvev, tdmK.get(cg - 2))
                    tvts[c] = s.op(V, lambda e, vsl=vsl: e.tensor_copy(
                        vts[:, vsl * 1040:(vsl + 1) * 1040].rearrange("p (i c) -> p i c", c=65)[:, :, 0:64],
                        vstage[:, vsl * 1024:(vsl + 1) * 1024].rearrange("p (i c) -> p i c", c=64)))
                    ttr = None
                    s.w(TE, tfv[NP - 1], ttr_done)
                    for p in range(NP):
                        ttr = s.op(TE, lambda e, p=p, c=c: e.transpose(
                            pbt[:, p * 128:(p + 1) * 128],
                            phkb[:, p * SW + c * CH:p * SW + (c + 1) * CH], identb[:, :]))
                    tpe = None
                    s.w(V, ttr, tdmK.get(cg - 1))
                    for p in range(NP):
                        tpe = s.op(V, lambda e, p=p: e.tensor_copy(
                            phkt[:, p * 128:(p + 1) * 128], pbt[:, p * 128:(p + 1) * 128]))
                    ttr_done = tpe
                    s.w(TE, tev[KC - 1][1], tpe, tvts[c], tsadd, trvb0)
                    tdmc = None
                    for p in range(NP):
                        bank = pf[0] if p < 4 else pf[1]
                        off = (p % 4) * 65
                        for hh in range(2):
                            tp = None if hh == 0 else (0, 64)
                            orow = slice(64 * hh, 64 * hh + 64)
                            tdmc = s.op(TE, lambda e, p=p, hh=hh, bank=bank, off=off, orow=orow, tp=tp, vsl=vsl: e.matmul(
                                bank[orow, off:off + 65],
                                phkt[:, p * 128 + 64 * hh:p * 128 + 64 * hh + 64],
                                vts[:, vsl * 1040 + (p * 2 + hh) * 65:vsl * 1040 + (p * 2 + hh) * 65 + 65],
                                start=True, stop=True, tile_position=tp))
                    tdmK[cg] = tdmc
                    tdm_last = tdmc
                    cur = cg % 2
                    nxt = (cg + 1) % 2
                    s.w(V, tdmc, tsnap)
                    s.op(V, lambda e, cur=cur, nxt=nxt: e.tensor_add(
                        srun[:, nxt * 520:nxt * 520 + 260], srun[:, cur * 520:cur * 520 + 260],
                        pf[0][:, 0:260]))
                    tsadd = s.op(V, lambda e, cur=cur, nxt=nxt: e.tensor_add(
                        srun[:, nxt * 520 + 260:nxt * 520 + 520], srun[:, cur * 520 + 260:cur * 520 + 520],
                        pf[1][:, 0:260]))
                    if cg < 31:
                        s.w(SY, tsadd)
                        tsnap = s.dma(SY, lambda e, cg=cg, nxt=nxt: e.dma_start(
                            snap_d[cg + 1], srun[:, nxt * 520:(nxt + 1) * 520]), "sn", flip=True)
                kxdone[sw] = [tevK[sw * KC + KC - 1], tvev]
            s.w(SY, tsadd)
            tst_w = s.dma(SY, lambda e: e.dma_start(ccst_in, srun[:, 0:520]), "ou")
            s.w(G, tst_w)
            if nocc:
                for r in range(NCORE):
                    tcst = s.dma(G, lambda g, r=r: g.dma_start(
                        ccst_out[r * 128:(r + 1) * 128, :], ccst_in), "wq", flip=(r == 0))
            else:
                tcst = s.collective("AllGather", ALU.bypass, [ccst_in], [ccst_out])
            sst = {}

            def emit_sinit():
                s.w(SY, tcst)
                tsi = None
                tsis = {}
                for r in range(NCORE):
                    rl = r % 2
                    if r >= 2:
                        s.w(SY, tsis[r - 2])
                    tbl = s.dma(SY, lambda e, r=r, rl=rl: e.dma_start(
                        snapb[:, rl * 520:(rl + 1) * 520], ccst_out[r * 128:(r + 1) * 128, :]),
                        "ld", flip=True)
                    s.w(V, tbl)
                    if r == 0:
                        tsi = s.op(V, lambda e, rl=rl: e.tensor_scalar(
                            sinit[:, :], snapb[:, rl * 520:(rl + 1) * 520], scf[:, 0:1], None, ALU.mult))
                    else:
                        tsi = s.op(V, lambda e, r=r, rl=rl: e.scalar_tensor_tensor(
                            sinit[:, :], snapb[:, rl * 520:(rl + 1) * 520], scf[:, r:r + 1],
                            sinit[:, :], ALU.mult, ALU.add))
                    tsis[r] = tsi
                sst["tsi"] = tsis[NCORE - 1]
                sst["guard"] = tsis[NCORE - 2]

            s.w(G, tsadd, tdm_last, *s.chtok("st"))
            s.op(G, lambda g: g.memset(tscr[:, 0:256], 1.0))
            s.op(G, lambda g: g.affine_select(mw[:, 0:256], tscr[:, 0:256],
                                              pattern=[[1, 256]], compare_op=ALU.is_ge,
                                              fill=0.0, base=0, channel_multiplier=-1))
            tmkw = s.op(G, lambda g: g.memset(phz[0:64, :], 0.0))
            s.w(G, tevK[(NSW - 1) * KC + KC - 1], tvev)
            tW1pre = {}
            tW2pre = {}
            for j in range(8, 24):
                tW1pre[j] = s.dma(G, lambda g, j=j, l=l: g.dma_start(
                    w1sb[:, j * 1024:(j + 1) * 1024].rearrange("p (kc i) -> p kc i", i=128),
                    w1_in[l][:, j * 128:(j + 1) * 128].rearrange("(kc p) i -> p kc i", p=128)),
                    "wf", flip=True)
            gdh = {}
            vq = [vts, vts1]
            prev_core = [tsadd, tvev, tfvK[(NSW - 1) * NP + NP - 1]]
            twoprev = None
            lnq = {}
            pend_ln1 = [None]
            amulQ = {}
            thwQ = {}
            qdone = {-2: kxdone[NSW - 2][0], -1: kxdone[NSW - 1][0]}
            tpe_last = ttr_done
            for sw in range(NSW):
                sl = sw % 2
                txb = None
                for k in range(KC):
                    if k == 0:
                        s.w(G, qdone.get(sw - 2))
                        if sw < 2:
                            s.w(G, *(kxdone.get(NSW - 2 + sw) or []))
                    txb = s.dma(G, lambda g, k=k, sw=sw, sl=sl: g.dma_start(
                        xbw[sl][:, k * SW:(k + 1) * SW], st_a[k][:, sw * SW:(sw + 1) * SW]),
                        "xb", flip=(k == 0))
                tphk = None
                for p in range(NP):
                    if p == 0:
                        if sw == 0:
                            s.w(SY, tpe_last, *s.chtok("st"))
                        s.w(SY, gdh.get((sw - 1) * 2 + 1))
                    tphk = s.dma(SY, lambda e, p=p, sw=sw: e.dma_start(
                        phkb[:, p * SW:(p + 1) * SW], phk_d[p][:, sw * SW:(sw + 1) * SW]),
                        "ph", flip=(p == 0))
                if sw == 0:
                    emit_sinit()
                tev = {}
                for mc in range(KC):
                    pm = pf[mc % 2]
                    g2 = tev.get(mc - 2)
                    if mc == 0:
                        s.w(TE, *prev_core, twoprev)
                    s.w(TE, txb, g2[1] if g2 else None)
                    tk = None
                    for k in range(KC):
                        tk = s.op(TE, lambda e, mc=mc, k=k, sl=sl, pm=pm: e.matmul(
                            pm[:, 0:SW], wqsb[:, k * 1024 + mc * 128:k * 1024 + mc * 128 + 128],
                            xbw[sl][:, k * SW:(k + 1) * SW], start=(k == 0), stop=(k == KC - 1)))
                    s.w(A, tk)
                    te1 = s.op(A, lambda e, mc=mc, pm=pm, l=l: e.activation(
                        ksb[:, mc * SW:(mc + 1) * SW], pm[:, 0:SW], AF.Identity,
                        bias=pvcol(l, "bq", mc), scale=1.0))
                    te2 = s.op(A, lambda e, mc=mc, pm=pm, l=l: e.activation(
                        k2sb[:, mc * SW:(mc + 1) * SW], pm[:, 0:SW], AF.Square,
                        bias=pvcol(l, "bq", mc), scale=1.0))
                    tev[mc] = (te1, te2)
                if pend_ln1[0] is not None:
                    pend_ln1[0]()
                    pend_ln1[0] = None
                if sw == NSW - 1:
                    s.w(G, tev[KC - 1][1])
                    for j in range(8):
                        tW1pre[j] = s.dma(G, lambda g, j=j, l=l: g.dma_start(
                            w1sb[:, j * 1024:(j + 1) * 1024].rearrange("p (kc i) -> p kc i", i=128),
                            w1_in[l][:, j * 128:(j + 1) * 128].rearrange("(kc p) i -> p kc i", p=128)),
                            "wf", flip=True)
                txf = None
                for k in range(KC):
                    s.w(SY, amulQ.get((sw - 1) * KC + k))
                    txf = s.dma(SY, lambda e, k=k, sw=sw: e.dma_start(
                        fbufA[:, k * SW:(k + 1) * SW], st_a[k][:, sw * SW:(sw + 1) * SW]),
                        "xf", flip=(k == 0))
                teps = {}
                for p in range(NP):
                    pm = pf[4 + p % 2]
                    s.w(TE, tev[p][1], teps.get(p - 2))
                    th = s.op(TE, lambda e, p=p, pm=pm: e.matmul(
                        pm[0:2, 0:SW], hm2[:, :], k2sb[:, p * SW:(p + 1) * SW], start=True, stop=True))
                    s.w(A, th)
                    teps[p] = s.op(A, lambda e, p=p, pm=pm: e.activation(
                        eps2[:, p * SW:(p + 1) * SW], pm[0:2, 0:SW], AF.Exp,
                        bias=biasm[0:2, 1:2], scale=1.0))
                tphq = {}
                tzs = {}
                for p in range(NP):
                    pm = pf[2 + p % 2]
                    s.w(TE, tev[p][0], tphq.get(p - 2))
                    tk = s.op(TE, lambda e, p=p, pm=pm, ob=ob: e.matmul(
                        pm[:, 0:SW], ob, ksb[:, p * SW:(p + 1) * SW], start=True, stop=True))
                    s.w(A, tk)
                    tphq[p] = s.op(A, lambda e, p=p, pm=pm: e.activation(
                        phqb[:, p * SW:(p + 1) * SW], pm[:, 0:SW], AF.Exp,
                        bias=biasm[:, 0:1], scale=1.0))
                    s.w(SY, tphq[p])
                    tzs[p] = s.dma(SY, lambda e, p=p: e.dma_start(
                        phz[64:128, p * SW:(p + 1) * SW], phqb[64:128, p * SW:(p + 1) * SW]),
                        "hw", flip=True)
                if sw == NSW - 1:
                    s.w(G, tphq[NP - 1], teps[NP - 1], tev[KC - 1][1])
                    for mo in range(4):
                        tW2pre[mo] = s.dma(G, lambda g, mo=mo, l=l: g.dma_start(
                            w2sb[:, mo * 4096:(mo + 1) * 4096].rearrange("p (j i) -> p j i", i=128),
                            w2_in[l][:, mo * 128:(mo + 1) * 128].rearrange("(j p) i -> p j i", p=128)),
                            "wf", flip=True)
                ev = {0: [], 1: []}
                for g in range(2):
                    gg2 = sw * 2 + g
                    gl = g
                    s.w(SY, gdh.get(gg2 - 2),
                        sst["guard"] if gg2 == 0 else (sst["tsi"] if gg2 == 1 else None))
                    for kb in range(2):
                        cg = gg2 * 2 + kb
                        s.dma(SY, lambda e, cg=cg, gl=gl, kb=kb: e.dma_start(
                            vq[gl][:, kb * 1040:(kb + 1) * 1040].rearrange("p (i c) -> p i c", c=65)[:, :, 0:64],
                            vraw_d[cg * CH:(cg + 1) * CH, :].rearrange("t (i c) -> t i c", c=64)),
                            "ld", flip=(kb == 0))
                    tsl_ = s.dma(SY, lambda e, gg2=gg2, gl=gl: e.dma_start(
                        snapb[:, gl * 520:(gl + 1) * 520], snap_d[2 * gg2]), "ld")
                    s.w(V, tsl_, gdh.get(gg2 - 1), sst["tsi"])
                    s.op(V, lambda e, gl=gl: e.tensor_add(
                        srq[:, :], snapb[:, gl * 520:(gl + 1) * 520], sinit[:, :]))
                    tce = s.op(V, lambda e: e.tensor_copy(sbe[0:64, :], srq[0:64, :]))
                    tco = s.op(V, lambda e: e.tensor_copy(sbo[64:128, :], srq[64:128, :]))

                    def emit_ph2(dv):
                        (p, pi, co, bN, bB, bD, tb) = dv
                        rq = rows[0:1, 1024 + pi * 512:1024 + (pi + 1) * 512]
                        s.w(A, tb[0], tb[1], lnq.get("A"), lnq.get("B"))
                        s.op(A, lambda e, rq=rq, bB=bB: e.activation(
                            rq[0:1, 0:256], bB[0:1, 256:512], AF.Ln))
                        s.op(A, lambda e, rq=rq, bD=bD: e.activation(
                            rq[0:1, 256:512], bD[0:1, 256:512], AF.Ln))
                        trd = s.op(A, lambda e, rq=rq: e.activation(
                            rq[0:1, 0:512], rq[0:1, 0:512], AF.Exp, bias=0.0, scale=-1.0))
                        s.w(TE, trd)
                        trb0 = s.op(TE, lambda e, rq=rq, bN=bN: e.matmul(
                            bN[0:64, 256:512], onesf[0:1, 0:64],
                            rq[0:1, 0:256], start=True, stop=True))
                        trb1 = s.op(TE, lambda e, rq=rq, bN=bN: e.matmul(
                            bN[64:128, 256:512], onesf[0:1, 0:64],
                            rq[0:1, 256:512], start=True, stop=True,
                            tile_position=(0, 64)))
                        s.w(A, trb0, trb1)
                        trde = s.op(A, lambda e, pi=pi, bN=bN: e.activation(
                            rdbs2[pi][:, :], bN[:, 256:512], AF.Identity, bias=0.0, scale=1.0))
                        s.w(V, trde)
                        tout = s.op(V, lambda e, p=p, co=co, pi=pi, bN=bN: e.tensor_tensor(
                            asw[:, p * SW + co:p * SW + co + 256], bN[:, 0:256],
                            rdbs2[pi][:, :], ALU.mult))
                        ev[pi] = [tout, trde]
                        return tout

                    pend = None
                    for p in range(NP):
                        pi = p % 2
                        co = g * 256
                        bN = pf[pi]
                        bA = pf[2 + pi]
                        bB = pf[4 + pi]
                        bD = bA
                        s.w(TE, tphq[NP - 1], teps[NP - 1], tce, tco, tphk, trvb0, tmkw,
                            tzs[p], *ev[pi])
                        tat = {}
                        tat[0] = s.op(TE, lambda e, p=p, co=co, bA=bA: e.matmul(
                            bA[:, 0:256],
                            phkb[0:64, p * SW + co:p * SW + co + CH],
                            phqb[0:64, p * SW + co:p * SW + co + 256],
                            start=True, stop=True))
                        tat[2] = s.op(TE, lambda e, p=p, co=co, bB=bB: e.matmul(
                            bB[:, 0:CH],
                            phkb[0:64, p * SW + co + CH:p * SW + co + 2 * CH],
                            phqb[0:64, p * SW + co + CH:p * SW + co + 2 * CH],
                            start=True, stop=True))
                        tat[1] = s.op(TE, lambda e, p=p, co=co, bA=bA: e.matmul(
                            bA[:, 256:512],
                            phkb[:, p * SW + co:p * SW + co + CH],
                            phz[:, p * SW + co:p * SW + co + 256],
                            start=True, stop=True))
                        tat[3] = s.op(TE, lambda e, p=p, co=co, bB=bB: e.matmul(
                            bB[:, CH:2 * CH],
                            phkb[:, p * SW + co + CH:p * SW + co + 2 * CH],
                            phz[:, p * SW + co + CH:p * SW + co + 2 * CH],
                            start=True, stop=True))
                        tmk = {}
                        s.w(V, tat[0], tat[1], tat[2], tat[3])
                        for hh in range(2):
                            tmk[hh] = s.op(V, lambda e, pi=pi, hh=hh, bA=bA: e.tensor_tensor(
                                atk0(pi, hh), bA[:, hh * 256:(hh + 1) * 256], mw[:, 0:256], ALU.mult))
                            tmk[2 + hh] = s.op(V, lambda e, pi=pi, hh=hh, bB=bB: e.tensor_tensor(
                                atk1(pi, hh), bB[:, hh * CH:(hh + 1) * CH], mw[:, 0:CH], ALU.mult))
                        tb = {}
                        s.w(TE, tmk[0], tmk[1], tmk[2], tmk[3])
                        for hh in range(2):
                            tp = None if hh == 0 else (0, 64)
                            orow = slice(64 * hh, 64 * hh + 64)
                            sb = sbe if hh == 0 else sbo
                            s.op(TE, lambda e, p=p, co=co, tp=tp, orow=orow, sb=sb, bN=bN: e.matmul(
                                bN[orow, 0:256], sb[:, p * 65:p * 65 + 64],
                                phqb[:, p * SW + co:p * SW + co + 256],
                                start=True, stop=False, tile_position=tp))
                            s.op(TE, lambda e, p=p, hh=hh, pi=pi, gl=gl, tp=tp, orow=orow, bN=bN: e.matmul(
                                bN[orow, CH:256],
                                vq[gl][:, 1040 + (p * 2 + hh) * 65:1040 + (p * 2 + hh) * 65 + 64],
                                atk1(pi, hh), start=False, stop=False, tile_position=tp))
                            s.op(TE, lambda e, p=p, hh=hh, pi=pi, gl=gl, tp=tp, orow=orow, bN=bN: e.matmul(
                                bN[orow, 0:256],
                                vq[gl][:, (p * 2 + hh) * 65:(p * 2 + hh) * 65 + 64],
                                atk0(pi, hh), start=False, stop=True, tile_position=tp))
                            dn = bB[0:1, 256:512] if hh == 0 else bD[0:1, 256:512]
                            dn1 = bB[0:1, 384:512] if hh == 0 else bD[0:1, 384:512]
                            s.op(TE, lambda e, p=p, co=co, sb=sb, dn=dn: e.matmul(
                                dn, sb[:, p * 65 + 64:p * 65 + 65],
                                phqb[:, p * SW + co:p * SW + co + 256],
                                start=True, stop=False))
                            s.op(TE, lambda e, p=p, hh=hh, pi=pi, gl=gl, dn1=dn1: e.matmul(
                                dn1,
                                vq[gl][:, 1040 + (p * 2 + hh) * 65 + 64:1040 + (p * 2 + hh) * 65 + 65],
                                atk1(pi, hh), start=False, stop=False))
                            s.op(TE, lambda e, p=p, hh=hh, pi=pi, gl=gl, dn=dn: e.matmul(
                                dn,
                                vq[gl][:, (p * 2 + hh) * 65 + 64:(p * 2 + hh) * 65 + 65],
                                atk0(pi, hh), start=False, stop=False))
                            tb[hh] = s.op(TE, lambda e, p=p, hh=hh, co=co, dn=dn: e.matmul(
                                dn, identb[0:2, hh:hh + 1],
                                eps2[:, p * SW + co:p * SW + co + 256], start=False, stop=True))
                        if pend is not None:
                            emit_ph2(pend)
                        pend = (p, pi, co, bN, bB, bD, tb)
                    gdh[gg2] = emit_ph2(pend)
                two = {}
                for mo in range(KC):
                    pm = pf[6] if mo % 2 == 0 else pf[0]
                    if mo < 2:
                        s.w(A, thwQ.get((sw - 1) * KC + KC - 2 + mo))
                    if mo == 0:
                        s.w(TE, amulQ.get((sw - 1) * KC + KC - 1))
                    s.w(TE, *ev[0], *ev[1], two.get(mo - 2))
                    tk = None
                    for k in range(KC):
                        tk = s.op(TE, lambda e, mo=mo, k=k, pm=pm: e.matmul(
                            pm[:, 0:SW], wosb[:, k * 1024 + mo * 128:k * 1024 + mo * 128 + 128],
                            asw[:, k * SW:(k + 1) * SW], start=(k == 0), stop=(k == KC - 1)))
                    s.w(A, tk, two.get(mo - 2))
                    ta = s.op(A, lambda e, mo=mo, pm=pm, l=l: e.activation(
                        tscr[:, (mo % 2) * SW:(mo % 2 + 1) * SW], pm[:, 0:SW], AF.Identity,
                        bias=pvcol(l, "bo", mo), scale=1.0))
                    s.w(V, ta, txf)
                    two[mo] = s.op(V, lambda e, mo=mo: e.tensor_add(
                        fbufA[:, mo * SW:(mo + 1) * SW], fbufA[:, mo * SW:(mo + 1) * SW],
                        tscr[:, (mo % 2) * SW:(mo % 2 + 1) * SW]))
                if sw == NSW - 1:
                    s.w(G, *ev[0], *ev[1], two[KC - 1])
                    for j in range(24, MJ):
                        tW1pre[j] = s.dma(G, lambda g, j=j, l=l: g.dma_start(
                            w1sb[:, j * 1024:(j + 1) * 1024].rearrange("p (kc i) -> p kc i", i=128),
                            w1_in[l][:, j * 128:(j + 1) * 128].rearrange("(kc p) i -> p kc i", p=128)),
                            "wf", flip=True)
                    for mo in range(4, KC):
                        tW2pre[mo] = s.dma(G, lambda g, mo=mo, l=l: g.dma_start(
                            w2sb[:, mo * 4096:(mo + 1) * 4096].rearrange("p (j i) -> p j i", i=128),
                            w2_in[l][:, mo * 128:(mo + 1) * 128].rearrange("(j p) i -> p j i", p=128)),
                            "wf", flip=True)
                phB = emit_ln_sums(gates=(two[KC - 1], thwQ.get((sw - 1) * KC + KC - 2)))

                def mk_ln1(sw, phB, l=l):
                    def flush():
                        tbA, tbB = phB(pf[6], pbtF, pf[6], pbtF)
                        thw = {-2: thwQ.get((sw - 1) * KC + KC - 2),
                               -1: thwQ.get((sw - 1) * KC + KC - 1)}
                        for k in range(KC):
                            kl = k % 2
                            s.w(V, tbA, tbB, thw.get(k - 2))
                            amulQ[sw * KC + k] = s.op(V, lambda e, k=k, kl=kl: e.tensor_mul(
                                tscr[:, kl * SW:(kl + 1) * SW],
                                fbufA[:, k * SW:(k + 1) * SW], pf[6][:, 0:SW]))
                            tt2 = s.op(V, lambda e, kl=kl: e.tensor_add(
                                tscr[:, kl * SW:(kl + 1) * SW],
                                tscr[:, kl * SW:(kl + 1) * SW], pbtF[:, 0:SW]))
                            s.w(A, tt2)
                            th = s.op(A, lambda e, k=k, kl=kl, l=l: e.activation(
                                hsl[:, kl * SW:(kl + 1) * SW], tscr[:, kl * SW:(kl + 1) * SW], AF.Identity,
                                bias=pvcol(l, "ln1b", k), scale=pvcol(l, "ln1g", k)))
                            s.w(SY, th)
                            thw[k] = s.dma(SY, lambda e, k=k, sw=sw, kl=kl: e.dma_start(
                                st_h[k][:, sw * SW:(sw + 1) * SW], hsl[:, kl * SW:(kl + 1) * SW]), "hw", flip=True)
                            thwQ[sw * KC + k] = thw[k]
                        lnq["A"], lnq["B"] = tbA, tbB
                    return flush
                pend_ln1[0] = mk_ln1(sw, phB)
                twoprev = two[KC - 1]
                prev_core = ev[0] + ev[1]
                qdone[sw] = tev[KC - 1][1]
            if pend_ln1[0] is not None:
                pend_ln1[0]()
                pend_ln1[0] = None
            s.barrier()

            tW1s = tW1pre
            tW2s = tW2pre
            treF = {}
            tyaddF = {}
            thwF = {}
            fblast = {}
            pend_ln2 = [None]
            for sw in range(NSW):
                thb = None
                for k in range(KC):
                    if k == 0:
                        s.w(G, treF.get((sw - 1) * MJ + MJ - 1))
                    thb = s.dma(G, lambda g, k=k, sw=sw: g.dma_start(
                        hbf[:, k * SW:(k + 1) * SW], st_h[k][:, sw * SW:(sw + 1) * SW]),
                        "xb", flip=(k == 0))
                tyadd = {-2: tyaddF.get((sw - 1) * KC + KC - 2),
                         -1: tyaddF.get((sw - 1) * KC + KC - 1)}
                tre = {-2: treF.get((sw - 1) * MJ + MJ - 2),
                       -1: treF.get((sw - 1) * MJ + MJ - 1)}
                txf = None
                for j in range(MJ):
                    if j == 2:
                        if pend_ln2[0] is not None:
                            pend_ln2[0]()
                            pend_ln2[0] = None
                        for k in range(KC):
                            s.w(SY, fblast.get((sw - 1) * KC + k))
                            txf = s.dma(SY, lambda e, k=k, sw=sw: e.dma_start(
                                fbufA[:, k * SW:(k + 1) * SW], st_h[k][:, sw * SW:(sw + 1) * SW]),
                                "xf", flip=(k == 0))
                    pm = pf[j % 2]
                    s.w(TE, tW1s[j], thb, tre.get(j - 2), tyadd.get(-1))
                    tk = None
                    for k in range(KC):
                        tk = s.op(TE, lambda e, j=j, k=k, pm=pm: e.matmul(
                            pm[:, 0:TG], w1sb[:, j * 1024 + k * 128:j * 1024 + k * 128 + 128],
                            hbf[:, k * SW:k * SW + TG], start=(k == 0), stop=(k == KC - 1)))
                    s.w(A, tk)
                    tre[j] = s.op(A, lambda e, j=j, pm=pm, l=l: e.activation(
                        h1f[:, j * TG:(j + 1) * TG], pm[:, 0:TG], AF.Relu,
                        bias=pvcol(l, "bf1", j), scale=1.0))
                    treF[sw * MJ + j] = tre[j]
                for mo in range(KC):
                    pm = pf[2 + mo % 2]
                    s.w(TE, tW2s[mo], tre[MJ - 1], tyadd.get(mo - 2))
                    if mo < 2:
                        s.w(A, *(thwF.get((sw - 1) * KC + KC - 2 + mo) or []))
                    tk = None
                    for j in range(MJ):
                        tk = s.op(TE, lambda e, mo=mo, j=j, pm=pm: e.matmul(
                            pm[:, 0:TG], w2sb[:, mo * 4096 + j * 128:mo * 4096 + j * 128 + 128],
                            h1f[:, j * TG:(j + 1) * TG], start=(j == 0), stop=(j == MJ - 1)))
                    s.w(A, tk, tyadd.get(mo - 2))
                    ta = s.op(A, lambda e, mo=mo, pm=pm, l=l: e.activation(
                        tscr[:, (mo % 2) * SW:(mo % 2) * SW + TG],
                        pm[:, 0:TG], AF.Identity, bias=pvcol(l, "bf2", mo), scale=1.0))
                    s.w(V, ta, txf)
                    tyadd[mo] = s.op(V, lambda e, mo=mo: e.tensor_add(
                        fbufA[:, mo * SW:mo * SW + TG],
                        fbufA[:, mo * SW:mo * SW + TG],
                        tscr[:, (mo % 2) * SW:(mo % 2) * SW + TG]))
                    tyaddF[sw * KC + mo] = tyadd[mo]
                gates = [tyadd[mo] for mo in range(KC)] + (thwF.get((sw - 1) * KC + KC - 2) or [])
                phB2 = emit_ln_sums(gates=gates)

                def mk_ln2(sw, phB2, l=l):
                    def flush():
                        tbA, tbB = phB2(pf[4], pf[5], pf[4], pf[5])
                        thw = {-2: thwF.get((sw - 1) * KC + KC - 2),
                               -1: thwF.get((sw - 1) * KC + KC - 1)}
                        for k in range(KC):
                            kl = k % 2
                            s.w(V, tbA, tbB, *(thw.get(k - 2) or []))
                            t1k = s.op(V, lambda e, k=k, kl=kl: e.tensor_mul(
                                tscr[:, kl * SW:(kl + 1) * SW],
                                fbufA[:, k * SW:(k + 1) * SW], pf[4][:, 0:SW]))
                            tt2 = s.op(V, lambda e, kl=kl: e.tensor_add(
                                tscr[:, kl * SW:(kl + 1) * SW],
                                tscr[:, kl * SW:(kl + 1) * SW], pf[5][:, 0:SW]))
                            s.w(A, tt2)
                            th = s.op(A, lambda e, k=k, kl=kl, l=l: e.activation(
                                hsl[:, kl * SW:(kl + 1) * SW], tscr[:, kl * SW:(kl + 1) * SW], AF.Identity,
                                bias=pvcol(l, "ln2b", k), scale=pvcol(l, "ln2g", k)))
                            if l < L - 1:
                                s.w(SY, th)
                                thw[k] = [s.dma(SY, lambda e, k=k, sw=sw, kl=kl: e.dma_start(
                                    st_a[k][:, sw * SW:(sw + 1) * SW], hsl[:, kl * SW:(kl + 1) * SW]),
                                    "hw", flip=True)]
                                thwF[sw * KC + k] = thw[k]
                                fblast[sw * KC + k] = t1k
                            else:
                                s.w(SY, t1k)
                                tx1l = s.dma(SY, lambda e, k=k, sw=sw: e.dma_start(
                                    fbufA[:, k * SW:(k + 1) * SW], st_x1[k][:, sw * SW:(sw + 1) * SW]),
                                    "ld", flip=True)
                                s.w(V, th, tx1l)
                                tx2 = s.op(V, lambda e, k=k, kl=kl: e.tensor_add(
                                    tscr[:, kl * SW:(kl + 1) * SW], hsl[:, kl * SW:(kl + 1) * SW],
                                    fbufA[:, k * SW:(k + 1) * SW]))
                                s.w(SY, tx2)
                                tyd = s.dma(SY, lambda e, k=k, sw=sw, kl=kl: e.dma_start(
                                    yshard[k][:, sw * SW:(sw + 1) * SW], tscr[:, kl * SW:(kl + 1) * SW]),
                                    "ou", flip=True)
                                tvm = s.op(V, lambda e, kl=kl: e.tensor_scalar(
                                    hsl[:, kl * SW:(kl + 1) * SW], tscr[:, kl * SW:(kl + 1) * SW],
                                    m0[:, 0:1], None, ALU.mult))
                                s.w(SY, tvm)
                                tcxk = s.dma(SY, lambda e, k=k, sw=sw, kl=kl: e.dma_start(
                                    ccx2_in[sw * SW + (k // 2) * 128:sw * SW + (k // 2) * 128 + 128,
                                            (k % 2) * SW:(k % 2 + 1) * SW],
                                    hsl[:, kl * SW:(kl + 1) * SW]), "hw", flip=True)
                                thw[k] = [tyd, tcxk]
                                thwF[sw * KC + k] = thw[k]
                                fblast[sw * KC + k] = tx2
                    return flush
                pend_ln2[0] = mk_ln2(sw, phB2)
            if pend_ln2[0] is not None:
                pend_ln2[0]()
                pend_ln2[0] = None
            s.barrier()

        s.w(G, *s.chtok("hw"))
        if nocc:
            tccx = s.dma(G, lambda g: g.dma_start(ccx2_out, ccx2_in[0:MTOK, :]), "wq", flip=True)
        else:
            tccx = s.collective("ReduceScatter", ALU.add, [ccx2_in], [ccx2_out])

        tewb = build_ewbs(s.token(V))
        s.w(SY, tccx)
        tx5 = None
        for k in range(KC):
            tx5 = s.dma(SY, lambda e, k=k: e.dma_start(
                fbufA[:, k * SW:(k + 1) * SW],
                ccx2_out[(k // 2) * 128:(k // 2) * 128 + 128, (k % 2) * SW:(k % 2 + 1) * SW]),
                "xf", flip=(k == 0))
        tbA, tbB = emit_ln(pf[4], pf[5], pf[5], pf[6], gates=(tx5, tewb))
        tgxg = None
        for k in range(KC):
            kl = k % 2
            s.w(V, tbA, tbB, tgxg)
            s.op(V, lambda e, k=k, kl=kl: e.tensor_mul(hsl[:, kl * SW:(kl + 1) * SW],
                                                       fbufA[:, k * SW:(k + 1) * SW], pf[5][:, 0:SW]))
            tt2 = s.op(V, lambda e, kl=kl: e.tensor_add(hsl[:, kl * SW:(kl + 1) * SW],
                                                        hsl[:, kl * SW:(kl + 1) * SW], pf[6][:, 0:SW]))
            s.w(A, tt2)
            tgxg = s.op(A, lambda e, k=k, kl=kl: e.activation(
                xg[:, k * SW:(k + 1) * SW], hsl[:, kl * SW:(kl + 1) * SW], AF.Identity, bias=0.0, scale=1.0))

        ydma = {}

        def wr_moe3(mo, td, src):
            s.w(V, td, ydma.get(mo - 2))
            ty = s.op(V, lambda e, mo=mo, src=src: e.tensor_add(
                hsl[:, (mo % 2) * SW:(mo % 2 + 1) * SW],
                fbufA[:, mo * SW:(mo + 1) * SW], src))
            s.w(SY, ty)
            ydma[mo] = s.dma(SY, lambda e, mo=mo: e.dma_start(
                ymoe3[mo], hsl[:, (mo % 2) * SW:(mo % 2 + 1) * SW]), "hw", flip=True)
            return ydma[mo]
        emit_moe(tgxg, wr_moe3)

        s.drain_stores()
        s.op(SY, lambda e: e.nop())
        s.replay(block)
    return nc


_NC_CACHE = {}


def kernel(L0_attn_x, L0_attn_Wq, L0_attn_bq, L0_attn_Wk, L0_attn_bk,
           L0_attn_Wv, L0_attn_bv, L0_attn_Wo, L0_attn_bo, L0_attn_omega,
           L0_attn_ln1_g, L0_attn_ln1_b, L0_attn_ln2_g, L0_attn_ln2_b,
           L0_attn_W1, L0_attn_bf1, L0_attn_W2, L0_attn_bf2,
           L0_ffn_scores, L0_ffn_expert_weights, L0_ffn_top_experts,
           L0_ffn_w1, L0_ffn_w2):
    x = np.ascontiguousarray(np.asarray(L0_attn_x, np.float32))
    B, S, _ = x.shape

    def fm(a):
        return np.ascontiguousarray(np.asarray(a, np.float32).T.reshape(KC, 128, -1))

    def cols(v):
        return np.asarray(v, np.float32).reshape(8, 128).T

    pvec = np.zeros((128, 312), np.float32)
    for l in range(L):
        base = l * 104
        pvec[:, base + 0:base + 8] = cols(L0_attn_bq[l])
        pvec[:, base + 8:base + 16] = cols(L0_attn_bk[l])
        pvec[:, base + 16:base + 24] = cols(L0_attn_bv[l])
        pvec[:, base + 24:base + 32] = cols(L0_attn_bo[l])
        pvec[:, base + 32:base + 40] = cols(L0_attn_ln1_g[l])
        pvec[:, base + 40:base + 48] = cols(L0_attn_ln1_b[l])
        pvec[:, base + 48:base + 56] = cols(L0_attn_ln2_g[l])
        pvec[:, base + 56:base + 64] = cols(L0_attn_ln2_b[l])
        pvec[:, base + 64:base + 72] = cols(L0_attn_bf2[l])
        pvec[:, base + 72:base + 104] = np.asarray(L0_attn_bf1[l], np.float32).reshape(32, 128).T
    rowv = np.ascontiguousarray(np.asarray(L0_attn_bv, np.float32).reshape(1, L * D))
    ew = np.asarray(L0_ffn_expert_weights, np.float32)

    if "nc" not in _NC_CACHE:
        _NC_CACHE["nc"] = build()
    nc = _NC_CACHE["nc"]

    in_maps = []
    for i in range(NCORE):
        b, half = i // 2, i % 2
        m0v = np.full((128, 1), 1.0 if i == 0 else 0.0, np.float32)
        sc = np.zeros((128, 8), np.float32)
        if i % 2 == 1:
            sc[:, i - 1] = 1.0
        in_maps.append({
            "xshard": fm(x[b, half * T:(half + 1) * T, :]),
            "xmoe": fm(x[0, i * MTOK:(i + 1) * MTOK, :]),
            "ewT": np.ascontiguousarray(ew[i * MTOK:(i + 1) * MTOK, :].T),
            "pvec": pvec, "rowv": rowv,
            "omg_in": np.asarray(L0_attn_omega, np.float32),
            "wq_in": np.asarray(L0_attn_Wq, np.float32),
            "wk_in": np.asarray(L0_attn_Wk, np.float32),
            "wv_in": np.asarray(L0_attn_Wv, np.float32),
            "wo_in": np.asarray(L0_attn_Wo, np.float32),
            "w1_in": np.asarray(L0_attn_W1, np.float32),
            "w2_in": np.asarray(L0_attn_W2, np.float32),
            "mw1_in": np.asarray(L0_ffn_w1, np.float32),
            "mw2_in": np.asarray(L0_ffn_w2, np.float32),
            "mask0": m0v, "scoef": sc,
        })
    res = run_bass_kernel_spmd(nc, in_maps, core_ids=list(range(NCORE)))
    out = np.empty((B, S, D), np.float32)
    for i in range(NCORE):
        b, half = i // 2, i % 2
        ys = np.asarray(res.results[i]["yshard"])
        out[b, half * T:(half + 1) * T, :] = ys.transpose(2, 0, 1).reshape(T, D)
    for i in range(NCORE):
        ym = np.asarray(res.results[i]["ymoe3"])
        out[0, i * MTOK:(i + 1) * MTOK, :] = ym.transpose(2, 0, 1).reshape(MTOK, D)
    return out


# version 29
# speedup vs baseline: 1.1038x; 1.0779x over previous
import numpy as np
import concourse.bass as bass
import concourse.mybir as mybir
from concourse.bass_utils import run_bass_kernel_spmd

F32 = mybir.dt.float32
BF16 = mybir.dt.bfloat16
AF = mybir.ActivationFunctionType
ALU = mybir.AluOpType

NCORE = 8
D = 1024
KC = 8
T = 4096
NSW = 8
SW = 512
CH = 128
NCHS = 4
NP = 8
DH = 64
M = 32
DFF = 4096
MJ = 32
MOE_I = 8192
MOE_J = 64
MTOK = 512
L = 3
LN_EPS = 1e-5
ATTN_EPS = 1e-6
SC = float(DH) ** -0.25
CSQ = 0.0625
LOGM = float(np.log((2.0 * M) ** -0.5))
LOGEPS = float(np.log(ATTN_EPS))
GC1 = 0.044715
GC2 = float(2.0 * np.sqrt(2.0 / np.pi))
RG = [[0, 1, 2, 3, 4, 5, 6, 7]]
TG = 512
LN_V2 = False

ENGINES = ("tensor", "vector", "scalar", "gpsimd", "sync")
G, V, A, TE, SY = "gpsimd", "vector", "scalar", "tensor", "sync"


class Sched:
    AUTO_CHAIN = ("vector", "scalar", "gpsimd")

    def __init__(self, nc, ctx):
        self.nc = nc
        self.ops = {e: [] for e in ENGINES}
        self.cnt = {}
        self.last_waited = {e: {} for e in ENGINES}
        self.sems = {}
        self._ctx = ctx
        for e in ENGINES:
            self.sems[e] = ctx.enter_context(nc.semaphore(f"s_{e}"))
            self.cnt[e] = 0
        self.chan = {}

    def op(self, eng, fn):
        sem = self.sems[eng]
        if eng in self.AUTO_CHAIN and self.cnt[eng] > 0:
            prev = self.cnt[eng]
            if self.last_waited[eng].get(eng, 0) < prev:
                self.last_waited[eng][eng] = prev
                def emitw(e, sem=sem, val=prev):
                    e.wait_ge(sem, val)
                self.ops[eng].append(emitw)
        self.cnt[eng] += 1
        want = self.cnt[eng]
        def emit(e, fn=fn, sem=sem):
            fn(e).then_inc(sem, 1)
        self.ops[eng].append(emit)
        return (eng, want)

    def mkchan(self, name):
        sems = [self._ctx.enter_context(self.nc.semaphore(f"c_{name}{i}"))
                for i in range(2)]
        self.chan[name] = {"sems": sems, "slot": 0}
        for i in range(2):
            self.sems[f"{name}{i}"] = sems[i]
            self.cnt[f"{name}{i}"] = 0

    def dma(self, eng, fn, ch, flip=False):
        c = self.chan[ch]
        if flip:
            c["slot"] ^= 1
        slot = c["slot"]
        key = f"{ch}{slot}"
        sem = c["sems"][slot]
        if flip and self.cnt[key] > 0:
            prev = self.cnt[key]
            if self.last_waited[eng].get(key, 0) < prev:
                self.last_waited[eng][key] = prev
                def emitw(e, sem=sem, val=prev):
                    e.wait_ge(sem, val)
                self.ops[eng].append(emitw)
        self.cnt[key] += 16
        want = self.cnt[key]
        def emit(e, fn=fn, sem=sem):
            fn(e).then_inc(sem, 16)
        self.ops[eng].append(emit)
        return (key, want)

    def chtok(self, ch):
        return [(f"{ch}{i}", self.cnt[f"{ch}{i}"]) for i in range(2)]

    def collective(self, kind, op, ins, outs):
        name = f"cc{len([k for k in self.sems if k.startswith('cc')])}"
        sem = self._ctx.enter_context(self.nc.semaphore(name))
        self.sems[name] = sem
        self.cnt[name] = 1
        def emit(e, sem=sem):
            e.collective_compute(kind, op, replica_groups=RG,
                                 ins=[i.opt() for i in ins],
                                 outs=[o.opt() for o in outs]).then_inc(sem)
        self.ops["gpsimd"].append(emit)
        return (name, 1)

    def w(self, eng, *tokens):
        for t in tokens:
            if t is None:
                continue
            (src, val) = t
            if src == eng:
                continue
            if val <= self.last_waited[eng].get(src, 0):
                continue
            self.last_waited[eng][src] = val
            sem = self.sems[src]
            def emit(e, sem=sem, val=val):
                e.wait_ge(sem, val)
            self.ops[eng].append(emit)

    def token(self, eng):
        return (eng, self.cnt[eng])

    def drain_stores(self):
        self.w(SY, *self.chtok("st"), *self.chtok("ou"),
               *self.chtok("hw"), *self.chtok("sn"))

    def barrier(self):
        self.drain_stores()
        self.op(SY, lambda e: e.nop())
        toks = [self.token(e) for e in ENGINES]
        for e in ENGINES:
            self.w(e, *toks)

    def replay(self, block):
        for e in ENGINES:
            ops = self.ops[e]
            def mk(ops):
                def run(engine):
                    for fn in ops:
                        fn(engine)
                return run
            getattr(block, e)(mk(ops))


def build(nocc=False):
    import contextlib
    nc = bass.Bass("TRN2", target_bir_lowering=False)
    DT = nc.dram_tensor
    xshard = DT("xshard", [KC, 128, T], F32, kind="ExternalInput").ap()
    xmoe = DT("xmoe", [KC, 128, MTOK], F32, kind="ExternalInput").ap()
    ewT = DT("ewT", [2, MTOK], F32, kind="ExternalInput").ap()
    pvec = DT("pvec", [128, 312], F32, kind="ExternalInput").ap()
    rowv = DT("rowv", [1, L * D], F32, kind="ExternalInput").ap()
    omg_in = DT("omg_in", [L, DH, M], F32, kind="ExternalInput").ap()
    wq_in = DT("wq_in", [L, D, D], F32, kind="ExternalInput").ap()
    wk_in = DT("wk_in", [L, D, D], F32, kind="ExternalInput").ap()
    wv_in = DT("wv_in", [L, D, D], F32, kind="ExternalInput").ap()
    wo_in = DT("wo_in", [L, D, D], F32, kind="ExternalInput").ap()
    w1_in = DT("w1_in", [L, D, DFF], F32, kind="ExternalInput").ap()
    w2_in = DT("w2_in", [L, DFF, D], F32, kind="ExternalInput").ap()
    mw1_in = DT("mw1_in", [D, MOE_I], F32, kind="ExternalInput").ap()
    mw2_in = DT("mw2_in", [MOE_I, D], F32, kind="ExternalInput").ap()
    mask0 = DT("mask0", [128, 1], F32, kind="ExternalInput").ap()
    scoef = DT("scoef", [128, 8], F32, kind="ExternalInput").ap()
    yshard = DT("yshard", [KC, 128, T], F32, kind="ExternalOutput").ap()
    ymoe3 = DT("ymoe3", [KC, 128, MTOK], F32, kind="ExternalOutput").ap()
    st_x1 = DT("st_x1", [KC, 128, T], F32).ap()
    st_a = DT("st_a", [KC, 128, T], F32).ap()
    st_h = DT("st_h", [KC, 128, T], F32).ap()
    phk_d = DT("phk_d", [NP, 128, T], BF16).ap()
    vraw_d = DT("vraw_d", [T, D], BF16).ap()
    snap_d = DT("snap_d", [32, 128, 520], F32).ap()
    ccm1_in = DT("ccm1_in", [128, KC * SW], F32).ap()
    ccm1_out = DT("ccm1_out", [NCORE * 128, KC * SW], F32,
                  addr_space=("Local" if nocc else "Shared")).ap()
    ccst_in = DT("ccst_in", [128, 520], F32).ap()
    ccst_out = DT("ccst_out", [NCORE * 128, 520], F32,
                  addr_space=("Local" if nocc else "Shared")).ap()
    ccx2_in = DT("ccx2_in", [T, D], F32).ap()
    ccx2_out = DT("ccx2_out", [MTOK, D], F32).ap()

    ctx = contextlib.ExitStack()
    with ctx:
        Sb = lambda *a: ctx.enter_context(nc.sbuf_tensor(*a))
        Ps = lambda *a: ctx.enter_context(nc.psum_tensor(*a))
        pv = Sb("pv", [128, 312], F32)
        rvb = Sb("rvb", [1, D], BF16)
        omgs = Sb("omgs", [128, L * M], F32)
        obs = Sb("obs", [128, L * 128], BF16)
        cbt = Sb("cbt", [128, 128], BF16)
        hm2 = Sb("hm2", [128, 2], BF16)
        identb = Sb("identb", [128, 128], BF16)
        maskf = Sb("maskf", [128, 128], F32)
        onesf = Sb("onesf", [128, 128], F32)
        onesbr = Sb("onesbr", [1, 128], BF16)
        halfo = Sb("halfo", [2, 1], F32)
        biasm = Sb("biasm", [128, 2], F32)
        m0 = Sb("m0", [128, 1], F32)
        scf = Sb("scf", [128, 8], F32)
        fbufA = Sb("fbufA", [128, KC * SW], F32)
        tscr = Sb("tscr", [128, 2 * SW], F32)
        hsl = Sb("hsl", [128, 2 * SW], F32)
        rows = Sb("rows", [1, 4 * SW], F32)
        wreg = Sb("wreg", [128, 65536], BF16)
        arena = Sb("arena", [128, 20480], BF16)
        srun = arena[:, 0:2080].bitcast(F32)
        sinit = arena[:, 2080:3120].bitcast(F32)
        srq = arena[:, 3120:4160].bitcast(F32)
        snapb = arena[:, 4160:6240].bitcast(F32)
        sbe = arena[:, 6240:6760]
        sbo = arena[:, 6760:7280]
        vts = arena[:, 7280:9360]
        phkt = arena[:, 9360:10384]
        eps2 = arena[0:2, 10384:14480]
        h1f = arena[:, 0:16384]
        hbf = arena[:, 16384:20480]
        ATMB = 14480
        atk0 = lambda pi, hh: arena[:, ATMB + pi * 768 + hh * 256:ATMB + pi * 768 + (hh + 1) * 256]
        atk1 = lambda pi, hh: arena[:, ATMB + pi * 768 + 512 + hh * 128:ATMB + pi * 768 + 512 + (hh + 1) * 128]
        vts1 = arena[:, 16528:18608]
        rdbs2 = [arena[:, 16016:16528].bitcast(F32),
                 arena[:, 19632:20144].bitcast(F32)]
        mw = arena[:, 9360:10384].bitcast(F32)
        sq = arena[:, 14480:15504].bitcast(F32)
        ewbs = arena[:, 15504:16528].bitcast(F32)
        ewts = arena[0:2, 16528:17552].bitcast(F32)
        idf = fbufA[:, 0:128]
        rvf = fbufA[0:1, SW:SW + D]

        wqsb = wreg[:, 0:8192]
        wksb = wreg[:, 8192:16384]
        wvsb = wreg[:, 16384:24576]
        wosb = wreg[:, 24576:32768]
        xbw = [wreg[:, 32768 + i * 4096:32768 + (i + 1) * 4096] for i in range(2)]
        ksb = wreg[:, 40960:45056]
        k2sb = wreg[:, 45056:49152]
        phkb = wreg[:, 49152:53248]
        phqb = wreg[:, 53248:57344]
        asw = wreg[:, 57344:61440]
        vstage = wreg[:, 61440:63488]
        phz = wreg[:, 61440:65536]
        w1sb = wreg[:, 0:32768]
        w2sb = wreg[:, 32768:65536]
        mw1b = [wreg[:, i * 1024:(i + 1) * 1024] for i in range(2)]
        mw2b = [wreg[:, 4096 + i * 8192:4096 + (i + 1) * 8192] for i in range(2)]
        h1m = wreg[:, 20480:53248]
        xg = wreg[:, 53248:57344]

        pf = [Ps(f"pf{i}", [128, 512], F32) for i in range(7)]
        pbt = Ps("pbt", [128, 1024], BF16)
        pbtF = pbt[:, 0:1024].bitcast(F32)

        s = Sched(nc, ctx)
        for chn in ("wq", "wf", "xb", "xf", "st", "ld", "ou", "ph", "sn", "hw"):
            s.mkchan(chn)
        block = ctx.enter_context(nc.Block())

        def pvcol(l, name, k):
            off = {"bq": 0, "bk": 8, "bv": 16, "bo": 24, "ln1g": 32, "ln1b": 40,
                   "ln2g": 48, "ln2b": 56, "bf2": 64, "bf1": 72}[name]
            cidx = l * 104 + off + k
            return pv[:, cidx:cidx + 1]

        def do_cc(kind, op, cin, cout, nrow):
            if nocc:
                return s.dma(G, lambda g: g.dma_start(cout[0:nrow, :], cin), "ou")
            return s.collective(kind, op, [cin], [cout])

        s.dma(G, lambda g: g.dma_start(pv[:, :], pvec), "wq")
        s.dma(G, lambda g: g.dma_start(m0[:, :], mask0), "wq")
        s.dma(G, lambda g: g.dma_start(scf[:, :], scoef), "wq")
        tldw = None
        for l in range(L):
            s.dma(G, lambda g, l=l: g.dma_start(omgs[0:64, l * M:(l + 1) * M], omg_in[l]), "wq")
            tldw = s.dma(G, lambda g, l=l: g.dma_start(omgs[64:128, l * M:(l + 1) * M], omg_in[l]), "wq")
        s.op(G, lambda g: g.memset(onesf[:, :], 1.0))
        s.op(G, lambda g: g.memset(onesbr[:, :], 1.0))
        s.op(G, lambda g: g.memset(halfo[:, :], 0.5))
        s.op(G, lambda g: g.memset(biasm[:, 0:1], LOGM))
        s.op(G, lambda g: g.memset(biasm[:, 1:2], LOGEPS))
        s.op(G, lambda g: g.memset(cbt[:, :], 0.0))
        s.op(G, lambda g: g.memset(hm2[:, :], 0.0))
        s.op(G, lambda g: g.memset(obs[:, :], 0.0))
        s.op(G, lambda g: g.memset(sbe[:, :], 0.0))
        s.op(G, lambda g: g.memset(sbo[:, :], 0.0))
        for sl in range(2):
            for i in range(16):
                s.op(G, lambda g, sl=sl, i=i: g.memset(
                    vts[:, sl * 1040 + i * 65 + 64:sl * 1040 + i * 65 + 65], 1.0))
        s.op(G, lambda g: g.affine_select(idf[:, :], onesf[:, :], pattern=[[1, 128]],
                                          compare_op=ALU.is_equal, fill=0.0,
                                          base=0, channel_multiplier=-1))
        tgm = s.op(G, lambda g: g.affine_select(maskf[:, :], onesf[:, :], pattern=[[1, 128]],
                                               compare_op=ALU.is_ge, fill=0.0,
                                               base=0, channel_multiplier=-1))
        s.w(A, tgm, tldw)
        s.op(A, lambda e: e.activation(identb[:, :], idf[:, :], AF.Identity, bias=0.0, scale=1.0))
        s.op(A, lambda e: e.activation(cbt[0:64, 0:64], onesf[0:64, 0:64], AF.Identity, bias=0.0, scale=-CSQ))
        s.op(A, lambda e: e.activation(cbt[64:128, 64:128], onesf[64:128, 0:64], AF.Identity, bias=0.0, scale=-CSQ))
        s.op(A, lambda e: e.activation(hm2[0:64, 0:1], onesf[0:64, 0:1], AF.Identity, bias=0.0, scale=CSQ))
        s.op(A, lambda e: e.activation(hm2[64:128, 1:2], onesf[64:128, 0:1], AF.Identity, bias=0.0, scale=CSQ))
        tcons = None
        for l in range(L):
            ob = obs[:, l * 128:(l + 1) * 128]
            om0 = omgs[0:64, l * M:(l + 1) * M]
            om1 = omgs[64:128, l * M:(l + 1) * M]
            s.op(A, lambda e, ob=ob, om0=om0: e.activation(ob[0:64, 0:32], om0, AF.Identity, bias=0.0, scale=SC))
            s.op(A, lambda e, ob=ob, om0=om0: e.activation(ob[0:64, 32:64], om0, AF.Identity, bias=0.0, scale=-SC))
            s.op(A, lambda e, ob=ob, om1=om1: e.activation(ob[64:128, 64:96], om1, AF.Identity, bias=0.0, scale=SC))
            tcons = s.op(A, lambda e, ob=ob, om1=om1: e.activation(ob[64:128, 96:128], om1, AF.Identity, bias=0.0, scale=-SC))

        def build_ewbs(gate):
            tld = s.dma(G, lambda g: g.dma_start(ewts[:, :], ewT), "wq", flip=True)
            s.w(TE, tld, gate, tcons)
            t1 = s.op(TE, lambda e: e.matmul(pf[4][0:1, 0:MTOK], halfo[:, :], ewts[:, :], start=True, stop=True))
            s.w(V, t1)
            t2 = s.op(V, lambda e: e.tensor_copy(rows[0:1, 0:MTOK], pf[4][0:1, 0:MTOK]))
            s.w(TE, t2)
            t3 = s.op(TE, lambda e: e.matmul(pf[6][:, 0:MTOK], onesf[0:1, :], rows[0:1, 0:MTOK], start=True, stop=True))
            s.w(A, t3)
            return s.op(A, lambda e: e.activation(ewbs[:, :], pf[6][:, 0:MTOK], AF.Identity, bias=0.0, scale=1.0))

        def emit_ln_v1(rbA, rbB, pbA, pbB, gates=()):
            s.w(V, *gates)
            s.w(A, *gates)
            tv = s.op(V, lambda e: e.tensor_add(tscr[:, 0:SW], fbufA[:, 0:SW], fbufA[:, SW:2 * SW]))
            for k in range(2, KC):
                tv = s.op(V, lambda e, k=k: e.tensor_add(tscr[:, 0:SW], tscr[:, 0:SW], fbufA[:, k * SW:(k + 1) * SW]))
            ta = s.op(A, lambda e: e.activation(sq[:, :], fbufA[:, 0:SW], AF.Square))
            s.w(V, ta)
            tq = s.op(V, lambda e: e.tensor_copy(tscr[:, SW:2 * SW], sq[:, :]))
            for k in range(1, KC):
                s.w(A, tq)
                ta = s.op(A, lambda e, k=k: e.activation(sq[:, :], fbufA[:, k * SW:(k + 1) * SW], AF.Square))
                s.w(V, ta)
                tq = s.op(V, lambda e: e.tensor_add(tscr[:, SW:2 * SW], tscr[:, SW:2 * SW], sq[:, :]))
            s.w(TE, tq, tv)
            tm1 = s.op(TE, lambda e, rbA=rbA: e.matmul(rbA[0:1, 0:SW], onesf[:, 0:1], tscr[:, 0:SW], start=True, stop=True))
            tm2 = s.op(TE, lambda e, rbB=rbB: e.matmul(rbB[0:1, 0:SW], onesf[:, 0:1], tscr[:, SW:2 * SW], start=True, stop=True))
            R = lambda i: rows[0:1, i * SW:(i + 1) * SW]
            s.w(V, tm1, tm2)
            s.op(V, lambda e, rbA=rbA: e.tensor_scalar(R(0), rbA[0:1, 0:SW], 1.0 / D, None, ALU.mult))
            tr0 = s.op(V, lambda e, rbB=rbB: e.tensor_scalar(R(1), rbB[0:1, 0:SW], 1.0 / D, LN_EPS, ALU.mult, ALU.add))
            s.w(A, tr0)
            ta1 = s.op(A, lambda e: e.activation(R(2), R(0), AF.Square))
            s.w(V, ta1)
            tvv = s.op(V, lambda e: e.tensor_sub(R(1), R(1), R(2)))
            s.w(A, tvv)
            s.op(A, lambda e: e.activation(R(2), R(1), AF.Ln))
            tsq2 = s.op(A, lambda e: e.activation(R(3), R(2), AF.Exp, bias=0.0, scale=-0.5))
            s.w(V, tsq2)
            s.op(V, lambda e: e.tensor_mul(R(2), R(0), R(3)))
            trB = s.op(V, lambda e: e.tensor_scalar(R(2), R(2), -1.0, None, ALU.mult))
            s.w(TE, trB, tsq2)
            tbA = s.op(TE, lambda e, pbA=pbA: e.matmul(pbA[:, 0:SW], onesf[0:1, :], R(3), start=True, stop=True))
            tbB = s.op(TE, lambda e, pbB=pbB: e.matmul(pbB[:, 0:SW], onesf[0:1, :], R(2), start=True, stop=True))
            return tbA, tbB

        def emit_ln(rbA, rbB, pbA, pbB, gates=()):
            s.w(TE, *gates)
            s.w(A, *gates)
            for k in range(KC):
                s.op(TE, lambda e, k=k, rbA=rbA: e.matmul(
                    rbA[0:1, 0:SW], onesf[:, 0:1], fbufA[:, k * SW:(k + 1) * SW],
                    start=(k == 0), stop=(k == KC - 1)))
            tmq = {}
            for k in range(KC):
                kl = k % 2
                if k >= 2:
                    s.w(A, tmq[k - 2])
                tsq = s.op(A, lambda e, k=k, kl=kl: e.activation(
                    tscr[:, kl * SW:(kl + 1) * SW], fbufA[:, k * SW:(k + 1) * SW], AF.Square))
                s.w(TE, tsq)
                tmq[k] = s.op(TE, lambda e, k=k, kl=kl, rbB=rbB: e.matmul(
                    rbB[0:1, 0:SW], onesf[:, 0:1], tscr[:, kl * SW:(kl + 1) * SW],
                    start=(k == 0), stop=(k == KC - 1)))
            R = lambda i: rows[0:1, i * SW:(i + 1) * SW]
            s.w(V, tmq[KC - 1])
            s.op(V, lambda e, rbA=rbA: e.tensor_scalar(R(0), rbA[0:1, 0:SW], 1.0 / D, None, ALU.mult))
            tr0 = s.op(V, lambda e, rbB=rbB: e.tensor_scalar(R(1), rbB[0:1, 0:SW], 1.0 / D, LN_EPS, ALU.mult, ALU.add))
            s.w(A, tr0)
            ta1 = s.op(A, lambda e: e.activation(R(2), R(0), AF.Square))
            s.w(V, ta1)
            tvv = s.op(V, lambda e: e.tensor_sub(R(1), R(1), R(2)))
            s.w(A, tvv)
            s.op(A, lambda e: e.activation(R(2), R(1), AF.Ln))
            trs = s.op(A, lambda e: e.activation(R(3), R(2), AF.Exp, bias=0.0, scale=-0.5))
            s.w(V, trs)
            s.op(V, lambda e: e.tensor_mul(R(2), R(0), R(3)))
            trB = s.op(V, lambda e: e.tensor_scalar(R(2), R(2), -1.0, None, ALU.mult))
            s.w(TE, trB, trs)
            tbA = s.op(TE, lambda e, pbA=pbA: e.matmul(pbA[:, 0:SW], onesf[0:1, :], R(3), start=True, stop=True))
            tbB = s.op(TE, lambda e, pbB=pbB: e.matmul(pbB[:, 0:SW], onesf[0:1, :], R(2), start=True, stop=True))
            return tbA, tbB

        if not LN_V2:
            emit_ln = emit_ln_v1

        def emit_ln_sums(gates=()):
            s.w(V, *gates)
            s.w(A, *gates)
            tv = s.op(V, lambda e: e.tensor_add(tscr[:, 0:SW], fbufA[:, 0:SW], fbufA[:, SW:2 * SW]))
            for k in range(2, KC):
                tv = s.op(V, lambda e, k=k: e.tensor_add(tscr[:, 0:SW], tscr[:, 0:SW], fbufA[:, k * SW:(k + 1) * SW]))
            hq = hsl[:, 0:SW]
            ta = s.op(A, lambda e, hq=hq: e.activation(hq, fbufA[:, 0:SW], AF.Square))
            s.w(V, ta)
            tq = s.op(V, lambda e, hq=hq: e.tensor_copy(tscr[:, SW:2 * SW], hq))
            for k in range(1, KC):
                s.w(A, tq)
                ta = s.op(A, lambda e, k=k, hq=hq: e.activation(hq, fbufA[:, k * SW:(k + 1) * SW], AF.Square))
                s.w(V, ta)
                tq = s.op(V, lambda e, hq=hq: e.tensor_add(tscr[:, SW:2 * SW], tscr[:, SW:2 * SW], hq))

            def phaseB(rbA, rbB, pbA, pbB):
                s.w(TE, tq, tv)
                tm1 = s.op(TE, lambda e, rbA=rbA: e.matmul(rbA[0:1, 0:SW], onesf[:, 0:1], tscr[:, 0:SW], start=True, stop=True))
                tm2 = s.op(TE, lambda e, rbB=rbB: e.matmul(rbB[0:1, 0:SW], onesf[:, 0:1], tscr[:, SW:2 * SW], start=True, stop=True))
                R = lambda i: rows[0:1, i * SW:(i + 1) * SW]
                s.w(V, tm1, tm2)
                s.op(V, lambda e, rbA=rbA: e.tensor_scalar(R(0), rbA[0:1, 0:SW], 1.0 / D, None, ALU.mult))
                tr0 = s.op(V, lambda e, rbB=rbB: e.tensor_scalar(R(1), rbB[0:1, 0:SW], 1.0 / D, LN_EPS, ALU.mult, ALU.add))
                s.w(A, tr0)
                ta1 = s.op(A, lambda e: e.activation(R(2), R(0), AF.Square))
                s.w(V, ta1)
                tvv = s.op(V, lambda e: e.tensor_sub(R(1), R(1), R(2)))
                s.w(A, tvv)
                s.op(A, lambda e: e.activation(R(2), R(1), AF.Ln))
                tsq2 = s.op(A, lambda e: e.activation(R(3), R(2), AF.Exp, bias=0.0, scale=-0.5))
                s.w(V, tsq2)
                s.op(V, lambda e: e.tensor_mul(R(2), R(0), R(3)))
                trB = s.op(V, lambda e: e.tensor_scalar(R(2), R(2), -1.0, None, ALU.mult))
                s.w(TE, trB, tsq2)
                tbA = s.op(TE, lambda e, pbA=pbA: e.matmul(pbA[:, 0:SW], onesf[0:1, :], R(3), start=True, stop=True))
                tbB = s.op(TE, lambda e, pbB=pbB: e.matmul(pbB[:, 0:SW], onesf[0:1, :], R(2), start=True, stop=True))
                return tbA, tbB
            return phaseB

        def emit_moe(tgxg, wr_fn):
            gelu_done = None
            gelu_hist = {}
            cons1 = {}
            for j in range(MOE_J):
                sl = j % 2
                if cons1.get(j - 2) is not None:
                    s.w(G, cons1[j - 2])
                tl = s.dma(G, lambda g, j=j, sl=sl: g.dma_start(
                    mw1b[sl].rearrange("p (kc i) -> p kc i", i=128),
                    mw1_in[:, j * 128:(j + 1) * 128].rearrange("(kc p) i -> p kc i", p=128)),
                    "wf", flip=True)
                pm = pf[j % 2]
                s.w(TE, tl, tgxg, gelu_hist.get(j - 2))
                tk = None
                for k in range(KC):
                    tk = s.op(TE, lambda e, k=k, sl=sl, pm=pm: e.matmul(
                        pm[:, 0:SW], mw1b[sl][:, k * 128:(k + 1) * 128],
                        xg[:, k * SW:(k + 1) * SW], start=(k == 0), stop=(k == KC - 1)))
                cons1[j] = tk
                s.w(A, tk, gelu_done)
                th0 = s.op(A, lambda e, pm=pm: e.activation(tscr[:, 0:SW], pm[:, 0:SW], AF.Identity, bias=0.0, scale=1.0))
                th1 = s.op(A, lambda e, pm=pm: e.activation(sq[:, :], pm[:, 0:SW], AF.Square))
                s.w(V, th1, th0)
                s.op(V, lambda e: e.tensor_scalar(sq[:, :], sq[:, :], GC1, 1.0, ALU.mult, ALU.add))
                tg3 = s.op(V, lambda e: e.tensor_mul(sq[:, :], sq[:, :], tscr[:, 0:SW]))
                s.w(A, tg3)
                tg4 = s.op(A, lambda e: e.activation(sq[:, :], sq[:, :], AF.Sigmoid, bias=0.0, scale=GC2))
                s.w(V, tg4)
                gelu_done = s.op(V, lambda e, j=j: e.tensor_mul(h1m[:, j * SW:(j + 1) * SW], sq[:, :], tscr[:, 0:SW]))
                gelu_hist[j] = gelu_done
            cons2 = {}
            tds = {}
            last = None
            for mo in range(KC):
                sl = mo % 2
                if cons2.get(mo - 2) is not None:
                    s.w(G, cons2[mo - 2])
                tl = s.dma(G, lambda g, mo=mo, sl=sl: g.dma_start(
                    mw2b[sl].rearrange("p (j i) -> p j i", i=128),
                    mw2_in[:, mo * 128:(mo + 1) * 128].rearrange("(j p) i -> p j i", p=128)),
                    "wf", flip=True)
                pm = pf[2 + mo % 2]
                s.w(TE, tl, gelu_done, tds.get(mo - 2))
                tk = None
                for j in range(MOE_J):
                    tk = s.op(TE, lambda e, j=j, sl=sl, pm=pm: e.matmul(
                        pm[:, 0:SW], mw2b[sl][:, j * 128:(j + 1) * 128],
                        h1m[:, j * SW:(j + 1) * SW], start=(j == 0), stop=(j == MOE_J - 1)))
                cons2[mo] = tk
                s.w(V, tk, tewb)
                td = s.op(V, lambda e, pm=pm: e.tensor_mul(tscr[:, SW:2 * SW], pm[:, 0:SW], ewbs[:, :]))
                tds[mo] = td
                last = wr_fn(mo, td, tscr[:, SW:2 * SW])
                s.w(V, last)
            return last

        tewb = build_ewbs(None)
        s.w(SY, tcons, tewb)
        tx = None
        for k in range(KC):
            tx = s.dma(SY, lambda e, k=k: e.dma_start(fbufA[:, k * SW:(k + 1) * SW], xmoe[k]), "xf")
        tbA, tbB = emit_ln(pf[4], pf[5], pf[5], pf[6], gates=(tx, tewb))
        tgxg = None
        for k in range(KC):
            kl = k % 2
            s.w(V, tbA, tbB, tgxg)
            s.op(V, lambda e, k=k, kl=kl: e.tensor_mul(tscr[:, kl * SW:(kl + 1) * SW], fbufA[:, k * SW:(k + 1) * SW], pf[5][:, 0:SW]))
            tt2 = s.op(V, lambda e, kl=kl: e.tensor_add(tscr[:, kl * SW:(kl + 1) * SW], tscr[:, kl * SW:(kl + 1) * SW], pf[6][:, 0:SW]))
            s.w(A, tt2)
            tgxg = s.op(A, lambda e, k=k, kl=kl: e.activation(xg[:, k * SW:(k + 1) * SW], tscr[:, kl * SW:(kl + 1) * SW], AF.Identity, bias=0.0, scale=1.0))

        def wr_moe1(mo, td, src):
            s.w(SY, td)
            return s.dma(SY, lambda e, mo=mo, src=src: e.dma_start(
                ccm1_in[:, mo * SW:(mo + 1) * SW], src), "hw", flip=True)
        tm1w = emit_moe(tgxg, wr_moe1)
        s.drain_stores()
        s.w(G, *s.chtok("hw"))
        if nocc:
            for r in range(NCORE):
                tccm1 = s.dma(G, lambda g, r=r: g.dma_start(
                    ccm1_out[r * 128:(r + 1) * 128, :], ccm1_in), "wq", flip=(r == 0))
        else:
            tccm1 = s.collective("AllGather", ALU.bypass, [ccm1_in], [ccm1_out])
        s.barrier()

        amul = {}
        thwA = {}
        tx1s = {}
        stt_prev = None
        tbA_prev = None
        for sw in range(NSW):
            txl = None
            for k in range(KC):
                s.w(SY, amul.get((sw - 1) * KC + k), tx1s.get((sw - 1) * KC + k))
                txl = s.dma(SY, lambda e, k=k, sw=sw: e.dma_start(
                    fbufA[:, k * SW:(k + 1) * SW], xshard[k][:, sw * SW:(sw + 1) * SW]),
                    "xf", flip=(k == 0))
            tvmask = None
            for k in range(KC):
                kl = k % 2
                if k >= 2:
                    s.w(SY, tvmask)
                else:
                    s.w(SY, tccm1, stt_prev, tbA_prev)
                tgl = s.dma(SY, lambda e, k=k, sw=sw, kl=kl: e.dma_start(
                    tscr[:, kl * SW:(kl + 1) * SW],
                    ccm1_out[sw * 128:(sw + 1) * 128, k * SW:(k + 1) * SW]),
                    "ld", flip=True)
                s.w(V, txl, tgl, tvmask)
                tvmask = s.op(V, lambda e, k=k, kl=kl: e.scalar_tensor_tensor(
                    fbufA[:, k * SW:(k + 1) * SW], tscr[:, kl * SW:(kl + 1) * SW],
                    m0[:, 0:1], fbufA[:, k * SW:(k + 1) * SW], ALU.mult, ALU.add))
                s.w(SY, tvmask)
                tx1s[sw * KC + k] = s.dma(SY, lambda e, k=k, sw=sw: e.dma_start(
                    st_x1[k][:, sw * SW:(sw + 1) * SW], fbufA[:, k * SW:(k + 1) * SW]),
                    "st", flip=True)
            stt_prev = tvmask
            tbA, tbB = emit_ln(pf[0], pf[1], pf[1], pf[2],
                               gates=(tvmask, amul.get((sw - 1) * KC + KC - 1)))
            tbA_prev = tbA
            for k in range(KC):
                kl = k % 2
                s.w(V, tbA, tbB, thwA.get(sw * KC + k - 2))
                amul[sw * KC + k] = s.op(V, lambda e, k=k, kl=kl: e.tensor_mul(
                    hsl[:, kl * SW:(kl + 1) * SW],
                    fbufA[:, k * SW:(k + 1) * SW], pf[1][:, 0:SW]))
                tt2 = s.op(V, lambda e, kl=kl: e.tensor_add(hsl[:, kl * SW:(kl + 1) * SW],
                                                            hsl[:, kl * SW:(kl + 1) * SW], pf[2][:, 0:SW]))
                s.w(SY, tt2)
                thwA[sw * KC + k] = s.dma(SY, lambda e, k=k, sw=sw, kl=kl: e.dma_start(
                    st_a[k][:, sw * SW:(sw + 1) * SW], hsl[:, kl * SW:(kl + 1) * SW]), "hw", flip=True)
        s.barrier()

        for l in range(L):
            ob = obs[:, l * 128:(l + 1) * 128]
            s.w(G, *s.chtok("st"), *s.chtok("hw"), *s.chtok("ou"))
            trv = s.dma(G, lambda g, l=l: g.dma_start(rvf[:, :], rowv[0:1, l * D:(l + 1) * D]), "wq")
            s.op(G, lambda g: g.memset(sbe[64:128, :], 0.0))
            s.op(G, lambda g: g.memset(sbo[0:64, :], 0.0))
            for sl_ in range(2):
                for i_ in range(16):
                    s.op(G, lambda g, sl_=sl_, i_=i_: g.memset(
                        vts[:, sl_ * 1040 + i_ * 65 + 64:sl_ * 1040 + i_ * 65 + 65], 1.0))
                    s.op(G, lambda g, sl_=sl_, i_=i_: g.memset(
                        vts1[:, sl_ * 1040 + i_ * 65 + 64:sl_ * 1040 + i_ * 65 + 65], 1.0))
            trvb0 = s.token(G)
            s.w(A, trv)
            trvb = s.op(A, lambda e: e.activation(rvb[:, :], rvf[:, :], AF.Identity, bias=0.0, scale=1.0))
            twq = None
            for wi, (dst, src) in enumerate(((wqsb, wq_in), (wksb, wk_in),
                                             (wvsb, wv_in), (wosb, wo_in))):
                for k in range(KC):
                    twq = s.dma(G, lambda g, dst=dst, src=src, k=k, l=l: g.dma_start(
                        dst[:, k * 1024:(k + 1) * 1024], src[l][k * 128:(k + 1) * 128, :]),
                        "wq", flip=(k == 0 and wi == 0))

            tsz = s.op(G, lambda g: g.memset(srun[:, :], 0.0))
            s.w(SY, tsz)
            tsnap = s.dma(SY, lambda e: e.dma_start(snap_d[0], srun[:, 0:520]), "sn", flip=True)
            tsadd = None
            tdm_last = None
            tevK = {}
            tfvK = {}
            tdmK = {}
            tvwK = {}
            kxdone = {}
            tvev = None
            ttr_done = None
            for sw in range(NSW):
                sl = sw % 2
                txb = None
                for k in range(KC):
                    if k == 0:
                        s.w(G, *(kxdone.get(sw - 2) or []))
                    txb = s.dma(G, lambda g, k=k, sw=sw, sl=sl: g.dma_start(
                        xbw[sl][:, k * SW:(k + 1) * SW], st_a[k][:, sw * SW:(sw + 1) * SW]),
                        "xb", flip=(k == 0))
                tev = {-2: (None, tevK.get((sw - 1) * KC + KC - 2)),
                       -1: (None, tevK.get((sw - 1) * KC + KC - 1))}
                for mc in range(KC):
                    pm = pf[mc % 2]
                    g2 = tev.get(mc - 2)
                    if mc == 0:
                        s.w(TE, tsadd)
                    s.w(TE, txb, twq, g2[1] if g2 else None)
                    tk = None
                    for k in range(KC):
                        tk = s.op(TE, lambda e, mc=mc, k=k, sl=sl, pm=pm: e.matmul(
                            pm[:, 0:SW], wksb[:, k * 1024 + mc * 128:k * 1024 + mc * 128 + 128],
                            xbw[sl][:, k * SW:(k + 1) * SW], start=(k == 0), stop=(k == KC - 1)))
                    s.w(A, tk)
                    te1 = s.op(A, lambda e, mc=mc, pm=pm, l=l: e.activation(
                        ksb[:, mc * SW:(mc + 1) * SW], pm[:, 0:SW], AF.Identity,
                        bias=pvcol(l, "bk", mc), scale=1.0))
                    te2 = s.op(A, lambda e, mc=mc, pm=pm, l=l: e.activation(
                        k2sb[:, mc * SW:(mc + 1) * SW], pm[:, 0:SW], AF.Square,
                        bias=pvcol(l, "bk", mc), scale=1.0))
                    tev[mc] = (te1, te2)
                    tevK[sw * KC + mc] = te2
                tfv = {-2: tfvK.get((sw - 1) * NP + NP - 2),
                       -1: tfvK.get((sw - 1) * NP + NP - 1)}
                for p in range(NP):
                    pm = pf[4 + p % 2]
                    if p == 0:
                        s.w(A, *s.chtok("st"))
                    s.w(TE, tev[p][0], tev[p][1], tfv.get(p - 2))
                    s.op(TE, lambda e, p=p, pm=pm, ob=ob: e.matmul(
                        pm[:, 0:SW], ob, ksb[:, p * SW:(p + 1) * SW], start=True, stop=False))
                    tk = s.op(TE, lambda e, p=p, pm=pm: e.matmul(
                        pm[:, 0:SW], cbt[:, :], k2sb[:, p * SW:(p + 1) * SW], start=False, stop=True))
                    s.w(A, tk)
                    tfv[p] = s.op(A, lambda e, p=p, pm=pm: e.activation(
                        phkb[:, p * SW:(p + 1) * SW], pm[:, 0:SW], AF.Exp,
                        bias=biasm[:, 0:1], scale=1.0))
                    tfvK[sw * NP + p] = tfv[p]
                    s.w(SY, tfv[p])
                    s.dma(SY, lambda e, p=p, sw=sw: e.dma_start(
                        phk_d[p][:, sw * SW:(sw + 1) * SW], phkb[:, p * SW:(p + 1) * SW]), "st")
                tvts = {}
                for c in range(NCHS):
                    cg = sw * NCHS + c
                    vsl = c % 2
                    s.w(TE, twq, tvev, trvb)
                    tk = None
                    for k in range(KC):
                        for half in range(2):
                            tk = s.op(TE, lambda e, c=c, k=k, half=half, sl=sl: e.matmul(
                                pf[2 + half][:, 0:SW],
                                xbw[sl][:, k * SW + c * CH:k * SW + c * CH + CH],
                                wvsb[:, k * 1024 + half * SW:k * 1024 + (half + 1) * SW],
                                start=(k == 0), stop=False))
                    for half in range(2):
                        tk = s.op(TE, lambda e, half=half: e.matmul(
                            pf[2 + half][:, 0:SW], onesbr[:, :],
                            rvb[0:1, half * SW:(half + 1) * SW],
                            start=False, stop=True))
                    s.w(A, tk, tvwK.get(cg - 2))
                    tva = s.op(A, lambda e, vsl=vsl: e.activation(
                        vstage[:, vsl * 1024:vsl * 1024 + SW], pf[2][:, 0:SW], AF.Identity, bias=0.0, scale=1.0))
                    tvev = s.op(A, lambda e, vsl=vsl: e.activation(
                        vstage[:, vsl * 1024 + SW:vsl * 1024 + 1024], pf[3][:, 0:SW], AF.Identity, bias=0.0, scale=1.0))
                    s.w(SY, tva, tvev)
                    tvwK[cg] = s.dma(SY, lambda e, cg=cg, vsl=vsl: e.dma_start(
                        vraw_d[cg * CH:(cg + 1) * CH, :], vstage[:, vsl * 1024:(vsl + 1) * 1024]), "st", flip=True)
                    s.w(V, tva, tvev, tdmK.get(cg - 2))
                    tvts[c] = s.op(V, lambda e, vsl=vsl: e.tensor_copy(
                        vts[:, vsl * 1040:(vsl + 1) * 1040].rearrange("p (i c) -> p i c", c=65)[:, :, 0:64],
                        vstage[:, vsl * 1024:(vsl + 1) * 1024].rearrange("p (i c) -> p i c", c=64)))
                    ttr = None
                    s.w(TE, tfv[NP - 1], ttr_done)
                    for p in range(NP):
                        ttr = s.op(TE, lambda e, p=p, c=c: e.transpose(
                            pbt[:, p * 128:(p + 1) * 128],
                            phkb[:, p * SW + c * CH:p * SW + (c + 1) * CH], identb[:, :]))
                    tpe = None
                    s.w(V, ttr, tdmK.get(cg - 1))
                    for p in range(NP):
                        tpe = s.op(V, lambda e, p=p: e.tensor_copy(
                            phkt[:, p * 128:(p + 1) * 128], pbt[:, p * 128:(p + 1) * 128]))
                    ttr_done = tpe
                    s.w(TE, tev[KC - 1][1], tpe, tvts[c], tsadd, trvb0)
                    tdmc = None
                    for p in range(NP):
                        bank = pf[0] if p < 4 else pf[1]
                        off = (p % 4) * 65
                        for hh in range(2):
                            tp = None if hh == 0 else (0, 64)
                            orow = slice(64 * hh, 64 * hh + 64)
                            tdmc = s.op(TE, lambda e, p=p, hh=hh, bank=bank, off=off, orow=orow, tp=tp, vsl=vsl: e.matmul(
                                bank[orow, off:off + 65],
                                phkt[:, p * 128 + 64 * hh:p * 128 + 64 * hh + 64],
                                vts[:, vsl * 1040 + (p * 2 + hh) * 65:vsl * 1040 + (p * 2 + hh) * 65 + 65],
                                start=True, stop=True, tile_position=tp))
                    tdmK[cg] = tdmc
                    tdm_last = tdmc
                    cur = cg % 2
                    nxt = (cg + 1) % 2
                    s.w(V, tdmc, tsnap)
                    s.op(V, lambda e, cur=cur, nxt=nxt: e.tensor_add(
                        srun[:, nxt * 520:nxt * 520 + 260], srun[:, cur * 520:cur * 520 + 260],
                        pf[0][:, 0:260]))
                    tsadd = s.op(V, lambda e, cur=cur, nxt=nxt: e.tensor_add(
                        srun[:, nxt * 520 + 260:nxt * 520 + 520], srun[:, cur * 520 + 260:cur * 520 + 520],
                        pf[1][:, 0:260]))
                    if cg < 31:
                        s.w(SY, tsadd)
                        tsnap = s.dma(SY, lambda e, cg=cg, nxt=nxt: e.dma_start(
                            snap_d[cg + 1], srun[:, nxt * 520:(nxt + 1) * 520]), "sn", flip=True)
                kxdone[sw] = [tevK[sw * KC + KC - 1], tvev]
            s.w(SY, tsadd)
            tst_w = s.dma(SY, lambda e: e.dma_start(ccst_in, srun[:, 0:520]), "ou")
            s.w(G, tst_w)
            if nocc:
                for r in range(NCORE):
                    tcst = s.dma(G, lambda g, r=r: g.dma_start(
                        ccst_out[r * 128:(r + 1) * 128, :], ccst_in), "wq", flip=(r == 0))
            else:
                tcst = s.collective("AllGather", ALU.bypass, [ccst_in], [ccst_out])
            sst = {}

            def emit_sinit():
                s.w(SY, tcst)
                tsi = None
                tsis = {}
                for r in range(NCORE):
                    rl = r % 2
                    if r >= 2:
                        s.w(SY, tsis[r - 2])
                    tbl = s.dma(SY, lambda e, r=r, rl=rl: e.dma_start(
                        snapb[:, rl * 520:(rl + 1) * 520], ccst_out[r * 128:(r + 1) * 128, :]),
                        "ld", flip=True)
                    s.w(V, tbl)
                    if r == 0:
                        tsi = s.op(V, lambda e, rl=rl: e.tensor_scalar(
                            sinit[:, :], snapb[:, rl * 520:(rl + 1) * 520], scf[:, 0:1], None, ALU.mult))
                    else:
                        tsi = s.op(V, lambda e, r=r, rl=rl: e.scalar_tensor_tensor(
                            sinit[:, :], snapb[:, rl * 520:(rl + 1) * 520], scf[:, r:r + 1],
                            sinit[:, :], ALU.mult, ALU.add))
                    tsis[r] = tsi
                sst["tsi"] = tsis[NCORE - 1]
                sst["guard"] = tsis[NCORE - 2]

            s.w(G, tsadd, tdm_last, *s.chtok("st"))
            s.op(G, lambda g: g.memset(tscr[:, 0:256], 1.0))
            s.op(G, lambda g: g.affine_select(mw[:, 0:256], tscr[:, 0:256],
                                              pattern=[[1, 256]], compare_op=ALU.is_ge,
                                              fill=0.0, base=0, channel_multiplier=-1))
            tmkw = s.op(G, lambda g: g.memset(phz[0:64, :], 0.0))
            s.w(G, tevK[(NSW - 1) * KC + KC - 1], tvev)
            tW1pre = {}
            tW2pre = {}
            for j in range(8, 24):
                tW1pre[j] = s.dma(G, lambda g, j=j, l=l: g.dma_start(
                    w1sb[:, j * 1024:(j + 1) * 1024].rearrange("p (kc i) -> p kc i", i=128),
                    w1_in[l][:, j * 128:(j + 1) * 128].rearrange("(kc p) i -> p kc i", p=128)),
                    "wf", flip=True)
            gdh = {}
            tmk_hist = {}
            vq = [vts, vts1]
            prev_core = [tsadd, tvev, tfvK[(NSW - 1) * NP + NP - 1]]
            twoprev = None
            lnq = {}
            pend_ln1 = [None]
            amulQ = {}
            thwQ = {}
            qdone = {-2: kxdone[NSW - 2][0], -1: kxdone[NSW - 1][0]}
            tpe_last = ttr_done
            for sw in range(NSW):
                sl = sw % 2
                txb = None
                for k in range(KC):
                    if k == 0:
                        s.w(G, qdone.get(sw - 2))
                        if sw < 2:
                            s.w(G, *(kxdone.get(NSW - 2 + sw) or []))
                    txb = s.dma(G, lambda g, k=k, sw=sw, sl=sl: g.dma_start(
                        xbw[sl][:, k * SW:(k + 1) * SW], st_a[k][:, sw * SW:(sw + 1) * SW]),
                        "xb", flip=(k == 0))
                tphk = None
                for p in range(NP):
                    if p == 0:
                        if sw == 0:
                            s.w(SY, tpe_last, *s.chtok("st"))
                        s.w(SY, gdh.get((sw - 1) * 2 + 1))
                    tphk = s.dma(SY, lambda e, p=p, sw=sw: e.dma_start(
                        phkb[:, p * SW:(p + 1) * SW], phk_d[p][:, sw * SW:(sw + 1) * SW]),
                        "ph", flip=(p == 0))
                if sw == 0:
                    emit_sinit()
                tev = {}
                for mc in range(KC):
                    pm = pf[mc % 2]
                    g2 = tev.get(mc - 2)
                    if mc == 0:
                        s.w(TE, *prev_core, twoprev)
                    s.w(TE, txb, g2[1] if g2 else None)
                    tk = None
                    for k in range(KC):
                        tk = s.op(TE, lambda e, mc=mc, k=k, sl=sl, pm=pm: e.matmul(
                            pm[:, 0:SW], wqsb[:, k * 1024 + mc * 128:k * 1024 + mc * 128 + 128],
                            xbw[sl][:, k * SW:(k + 1) * SW], start=(k == 0), stop=(k == KC - 1)))
                    s.w(A, tk)
                    te1 = s.op(A, lambda e, mc=mc, pm=pm, l=l: e.activation(
                        ksb[:, mc * SW:(mc + 1) * SW], pm[:, 0:SW], AF.Identity,
                        bias=pvcol(l, "bq", mc), scale=1.0))
                    te2 = s.op(A, lambda e, mc=mc, pm=pm, l=l: e.activation(
                        k2sb[:, mc * SW:(mc + 1) * SW], pm[:, 0:SW], AF.Square,
                        bias=pvcol(l, "bq", mc), scale=1.0))
                    tev[mc] = (te1, te2)
                if pend_ln1[0] is not None:
                    pend_ln1[0]()
                    pend_ln1[0] = None
                if sw == NSW - 1:
                    s.w(G, tev[KC - 1][1])
                    for j in range(8):
                        tW1pre[j] = s.dma(G, lambda g, j=j, l=l: g.dma_start(
                            w1sb[:, j * 1024:(j + 1) * 1024].rearrange("p (kc i) -> p kc i", i=128),
                            w1_in[l][:, j * 128:(j + 1) * 128].rearrange("(kc p) i -> p kc i", p=128)),
                            "wf", flip=True)
                txf = None
                for k in range(KC):
                    s.w(SY, amulQ.get((sw - 1) * KC + k))
                    txf = s.dma(SY, lambda e, k=k, sw=sw: e.dma_start(
                        fbufA[:, k * SW:(k + 1) * SW], st_a[k][:, sw * SW:(sw + 1) * SW]),
                        "xf", flip=(k == 0))
                teps = {}
                for p in range(NP):
                    pm = pf[4 + p % 2]
                    s.w(TE, tev[p][1], teps.get(p - 2))
                    th = s.op(TE, lambda e, p=p, pm=pm: e.matmul(
                        pm[0:2, 0:SW], hm2[:, :], k2sb[:, p * SW:(p + 1) * SW], start=True, stop=True))
                    s.w(A, th)
                    teps[p] = s.op(A, lambda e, p=p, pm=pm: e.activation(
                        eps2[:, p * SW:(p + 1) * SW], pm[0:2, 0:SW], AF.Exp,
                        bias=biasm[0:2, 1:2], scale=1.0))
                tphq = {}
                tzs = {}
                for p in range(NP):
                    pm = pf[2 + p % 2]
                    s.w(TE, tev[p][0], tphq.get(p - 2))
                    tk = s.op(TE, lambda e, p=p, pm=pm, ob=ob: e.matmul(
                        pm[:, 0:SW], ob, ksb[:, p * SW:(p + 1) * SW], start=True, stop=True))
                    s.w(A, tk)
                    tphq[p] = s.op(A, lambda e, p=p, pm=pm: e.activation(
                        phqb[:, p * SW:(p + 1) * SW], pm[:, 0:SW], AF.Exp,
                        bias=biasm[:, 0:1], scale=1.0))
                    s.w(SY, tphq[p])
                    tzs[p] = s.dma(SY, lambda e, p=p: e.dma_start(
                        phz[64:128, p * SW:(p + 1) * SW], phqb[64:128, p * SW:(p + 1) * SW]),
                        "hw", flip=True)
                if sw == NSW - 1:
                    s.w(G, tphq[NP - 1], teps[NP - 1], tev[KC - 1][1])
                    for mo in range(4):
                        tW2pre[mo] = s.dma(G, lambda g, mo=mo, l=l: g.dma_start(
                            w2sb[:, mo * 4096:(mo + 1) * 4096].rearrange("p (j i) -> p j i", i=128),
                            w2_in[l][:, mo * 128:(mo + 1) * 128].rearrange("(j p) i -> p j i", p=128)),
                            "wf", flip=True)
                ev = {0: [], 1: []}
                for g in range(2):
                    gg2 = sw * 2 + g
                    gl = g
                    s.w(SY, gdh.get(gg2 - 2),
                        sst["guard"] if gg2 == 0 else (sst["tsi"] if gg2 == 1 else None))
                    for kb in range(2):
                        cg = gg2 * 2 + kb
                        s.dma(SY, lambda e, cg=cg, gl=gl, kb=kb: e.dma_start(
                            vq[gl][:, kb * 1040:(kb + 1) * 1040].rearrange("p (i c) -> p i c", c=65)[:, :, 0:64],
                            vraw_d[cg * CH:(cg + 1) * CH, :].rearrange("t (i c) -> t i c", c=64)),
                            "ld", flip=(kb == 0))
                    tsl_ = s.dma(SY, lambda e, gg2=gg2, gl=gl: e.dma_start(
                        snapb[:, gl * 520:(gl + 1) * 520], snap_d[2 * gg2]), "ld")
                    s.w(V, tsl_, gdh.get(gg2 - 1), sst["tsi"])
                    s.op(V, lambda e, gl=gl: e.tensor_add(
                        srq[:, :], snapb[:, gl * 520:(gl + 1) * 520], sinit[:, :]))
                    tce = s.op(V, lambda e: e.tensor_copy(sbe[0:64, :], srq[0:64, :]))
                    tco = s.op(V, lambda e: e.tensor_copy(sbo[64:128, :], srq[64:128, :]))

                    def emit_at(p):
                        pi = p % 2
                        co = g * 256
                        bA = pf[2 + pi]
                        bB = pf[4 + pi]
                        pg = gg2 * NP + p
                        s.w(TE, tphq[NP - 1], teps[NP - 1], tphk, trvb0, tmkw,
                            tzs[p], *(tmk_hist.get(pg - 2) or []))
                        tat = {}
                        tat[0] = s.op(TE, lambda e, p=p, co=co, bA=bA: e.matmul(
                            bA[:, 0:256],
                            phkb[0:64, p * SW + co:p * SW + co + CH],
                            phqb[0:64, p * SW + co:p * SW + co + 256],
                            start=True, stop=True))
                        tat[2] = s.op(TE, lambda e, p=p, co=co, bB=bB: e.matmul(
                            bB[:, 0:CH],
                            phkb[0:64, p * SW + co + CH:p * SW + co + 2 * CH],
                            phqb[0:64, p * SW + co + CH:p * SW + co + 2 * CH],
                            start=True, stop=True))
                        tat[1] = s.op(TE, lambda e, p=p, co=co, bA=bA: e.matmul(
                            bA[:, 256:512],
                            phkb[:, p * SW + co:p * SW + co + CH],
                            phz[:, p * SW + co:p * SW + co + 256],
                            start=True, stop=True))
                        tat[3] = s.op(TE, lambda e, p=p, co=co, bB=bB: e.matmul(
                            bB[:, CH:2 * CH],
                            phkb[:, p * SW + co + CH:p * SW + co + 2 * CH],
                            phz[:, p * SW + co + CH:p * SW + co + 2 * CH],
                            start=True, stop=True))
                        tmk = {}
                        s.w(V, tat[0], tat[1], tat[2], tat[3])
                        for hh in range(2):
                            tmk[hh] = s.op(V, lambda e, pi=pi, hh=hh, bA=bA: e.tensor_tensor(
                                atk0(pi, hh), bA[:, hh * 256:(hh + 1) * 256], mw[:, 0:256], ALU.mult))
                            tmk[2 + hh] = s.op(V, lambda e, pi=pi, hh=hh, bB=bB: e.tensor_tensor(
                                atk1(pi, hh), bB[:, hh * CH:(hh + 1) * CH], mw[:, 0:CH], ALU.mult))
                        tmk_hist[pg] = [tmk[i] for i in range(4)]
                        return tmk

                    def emit_ph2(dv):
                        (p, pi, co, bN, bD, tb) = dv
                        rq = rows[0:1, 1024 + pi * 512:1024 + (pi + 1) * 512]
                        s.w(A, tb[0], tb[1], lnq.get("A"), lnq.get("B"))
                        s.op(A, lambda e, rq=rq, bD=bD: e.activation(
                            rq[0:1, 0:512], bD[0:1, 0:512], AF.Ln))
                        trd = s.op(A, lambda e, rq=rq: e.activation(
                            rq[0:1, 0:512], rq[0:1, 0:512], AF.Exp, bias=0.0, scale=-1.0))
                        s.w(TE, trd)
                        trb0 = s.op(TE, lambda e, rq=rq, bN=bN: e.matmul(
                            bN[0:64, 256:512], onesf[0:1, 0:64],
                            rq[0:1, 0:256], start=True, stop=True))
                        trb1 = s.op(TE, lambda e, rq=rq, bN=bN: e.matmul(
                            bN[64:128, 256:512], onesf[0:1, 0:64],
                            rq[0:1, 256:512], start=True, stop=True,
                            tile_position=(0, 64)))
                        s.w(A, trb0, trb1)
                        trde = s.op(A, lambda e, pi=pi, bN=bN: e.activation(
                            rdbs2[pi][:, :], bN[:, 256:512], AF.Identity, bias=0.0, scale=1.0))
                        s.w(V, trde)
                        tout = s.op(V, lambda e, p=p, co=co, pi=pi, bN=bN: e.tensor_tensor(
                            asw[:, p * SW + co:p * SW + co + 256], bN[:, 0:256],
                            rdbs2[pi][:, :], ALU.mult))
                        ev[pi] = [tout, trde]
                        return tout

                    pend = None
                    tmks = {0: emit_at(0)}
                    for p in range(NP):
                        pi = p % 2
                        co = g * 256
                        bN = pf[pi]
                        bD = pf[6] if pi == 0 else pbtF
                        if p + 1 < NP:
                            tmks[p + 1] = emit_at(p + 1)
                        tmk = tmks[p]
                        tb = {}
                        s.w(TE, tmk[0], tmk[1], tmk[2], tmk[3], tce, tco,
                            amulQ.get((sw - 1) * KC + KC - 1), *ev[pi])
                        for hh in range(2):
                            tp = None if hh == 0 else (0, 64)
                            orow = slice(64 * hh, 64 * hh + 64)
                            sb = sbe if hh == 0 else sbo
                            s.op(TE, lambda e, p=p, co=co, tp=tp, orow=orow, sb=sb, bN=bN: e.matmul(
                                bN[orow, 0:256], sb[:, p * 65:p * 65 + 64],
                                phqb[:, p * SW + co:p * SW + co + 256],
                                start=True, stop=False, tile_position=tp))
                            s.op(TE, lambda e, p=p, hh=hh, pi=pi, gl=gl, tp=tp, orow=orow, bN=bN: e.matmul(
                                bN[orow, CH:256],
                                vq[gl][:, 1040 + (p * 2 + hh) * 65:1040 + (p * 2 + hh) * 65 + 64],
                                atk1(pi, hh), start=False, stop=False, tile_position=tp))
                            s.op(TE, lambda e, p=p, hh=hh, pi=pi, gl=gl, tp=tp, orow=orow, bN=bN: e.matmul(
                                bN[orow, 0:256],
                                vq[gl][:, (p * 2 + hh) * 65:(p * 2 + hh) * 65 + 64],
                                atk0(pi, hh), start=False, stop=True, tile_position=tp))
                            dn = bD[0:1, hh * 256:(hh + 1) * 256]
                            dn1 = bD[0:1, hh * 256 + 128:(hh + 1) * 256]
                            s.op(TE, lambda e, p=p, co=co, sb=sb, dn=dn: e.matmul(
                                dn, sb[:, p * 65 + 64:p * 65 + 65],
                                phqb[:, p * SW + co:p * SW + co + 256],
                                start=True, stop=False))
                            s.op(TE, lambda e, p=p, hh=hh, pi=pi, gl=gl, dn1=dn1: e.matmul(
                                dn1,
                                vq[gl][:, 1040 + (p * 2 + hh) * 65 + 64:1040 + (p * 2 + hh) * 65 + 65],
                                atk1(pi, hh), start=False, stop=False))
                            s.op(TE, lambda e, p=p, hh=hh, pi=pi, gl=gl, dn=dn: e.matmul(
                                dn,
                                vq[gl][:, (p * 2 + hh) * 65 + 64:(p * 2 + hh) * 65 + 65],
                                atk0(pi, hh), start=False, stop=False))
                            tb[hh] = s.op(TE, lambda e, p=p, hh=hh, co=co, dn=dn: e.matmul(
                                dn, identb[0:2, hh:hh + 1],
                                eps2[:, p * SW + co:p * SW + co + 256], start=False, stop=True))
                        if pend is not None:
                            emit_ph2(pend)
                        pend = (p, pi, co, bN, bD, tb)
                    gdh[gg2] = emit_ph2(pend)
                two = {}
                for mo in range(KC):
                    pm = pf[6] if mo % 2 == 0 else pf[0]
                    if mo < 2:
                        s.w(A, thwQ.get((sw - 1) * KC + KC - 2 + mo))
                    if mo == 0:
                        s.w(TE, amulQ.get((sw - 1) * KC + KC - 1))
                    s.w(TE, *ev[0], *ev[1], two.get(mo - 2))
                    tk = None
                    for k in range(KC):
                        tk = s.op(TE, lambda e, mo=mo, k=k, pm=pm: e.matmul(
                            pm[:, 0:SW], wosb[:, k * 1024 + mo * 128:k * 1024 + mo * 128 + 128],
                            asw[:, k * SW:(k + 1) * SW], start=(k == 0), stop=(k == KC - 1)))
                    s.w(A, tk, two.get(mo - 2))
                    ta = s.op(A, lambda e, mo=mo, pm=pm, l=l: e.activation(
                        tscr[:, (mo % 2) * SW:(mo % 2 + 1) * SW], pm[:, 0:SW], AF.Identity,
                        bias=pvcol(l, "bo", mo), scale=1.0))
                    s.w(V, ta, txf)
                    two[mo] = s.op(V, lambda e, mo=mo: e.tensor_add(
                        fbufA[:, mo * SW:(mo + 1) * SW], fbufA[:, mo * SW:(mo + 1) * SW],
                        tscr[:, (mo % 2) * SW:(mo % 2 + 1) * SW]))
                if sw == NSW - 1:
                    s.w(G, *ev[0], *ev[1], two[KC - 1])
                    for j in range(24, MJ):
                        tW1pre[j] = s.dma(G, lambda g, j=j, l=l: g.dma_start(
                            w1sb[:, j * 1024:(j + 1) * 1024].rearrange("p (kc i) -> p kc i", i=128),
                            w1_in[l][:, j * 128:(j + 1) * 128].rearrange("(kc p) i -> p kc i", p=128)),
                            "wf", flip=True)
                    for mo in range(4, KC):
                        tW2pre[mo] = s.dma(G, lambda g, mo=mo, l=l: g.dma_start(
                            w2sb[:, mo * 4096:(mo + 1) * 4096].rearrange("p (j i) -> p j i", i=128),
                            w2_in[l][:, mo * 128:(mo + 1) * 128].rearrange("(j p) i -> p j i", p=128)),
                            "wf", flip=True)
                phB = emit_ln_sums(gates=(two[KC - 1], thwQ.get((sw - 1) * KC + KC - 2)))

                def mk_ln1(sw, phB, l=l):
                    def flush():
                        tbA, tbB = phB(pf[6], pbtF, pf[6], pbtF)
                        thw = {-2: thwQ.get((sw - 1) * KC + KC - 2),
                               -1: thwQ.get((sw - 1) * KC + KC - 1)}
                        for k in range(KC):
                            kl = k % 2
                            s.w(V, tbA, tbB, thw.get(k - 2))
                            amulQ[sw * KC + k] = s.op(V, lambda e, k=k, kl=kl: e.tensor_mul(
                                tscr[:, kl * SW:(kl + 1) * SW],
                                fbufA[:, k * SW:(k + 1) * SW], pf[6][:, 0:SW]))
                            tt2 = s.op(V, lambda e, kl=kl: e.tensor_add(
                                tscr[:, kl * SW:(kl + 1) * SW],
                                tscr[:, kl * SW:(kl + 1) * SW], pbtF[:, 0:SW]))
                            s.w(A, tt2)
                            th = s.op(A, lambda e, k=k, kl=kl, l=l: e.activation(
                                hsl[:, kl * SW:(kl + 1) * SW], tscr[:, kl * SW:(kl + 1) * SW], AF.Identity,
                                bias=pvcol(l, "ln1b", k), scale=pvcol(l, "ln1g", k)))
                            s.w(SY, th)
                            thw[k] = s.dma(SY, lambda e, k=k, sw=sw, kl=kl: e.dma_start(
                                st_h[k][:, sw * SW:(sw + 1) * SW], hsl[:, kl * SW:(kl + 1) * SW]), "hw", flip=True)
                            thwQ[sw * KC + k] = thw[k]
                        lnq["A"], lnq["B"] = tbA, tbB
                    return flush
                pend_ln1[0] = mk_ln1(sw, phB)
                twoprev = two[KC - 1]
                prev_core = ev[0] + ev[1]
                qdone[sw] = tev[KC - 1][1]
            if pend_ln1[0] is not None:
                pend_ln1[0]()
                pend_ln1[0] = None
            s.barrier()

            tW1s = tW1pre
            tW2s = tW2pre
            treF = {}
            tyaddF = {}
            thwF = {}
            fblast = {}
            pend_ln2 = [None]
            for sw in range(NSW):
                thb = None
                for k in range(KC):
                    if k == 0:
                        s.w(G, treF.get((sw - 1) * MJ + MJ - 1))
                    thb = s.dma(G, lambda g, k=k, sw=sw: g.dma_start(
                        hbf[:, k * SW:(k + 1) * SW], st_h[k][:, sw * SW:(sw + 1) * SW]),
                        "xb", flip=(k == 0))
                tyadd = {-2: tyaddF.get((sw - 1) * KC + KC - 2),
                         -1: tyaddF.get((sw - 1) * KC + KC - 1)}
                tre = {-2: treF.get((sw - 1) * MJ + MJ - 2),
                       -1: treF.get((sw - 1) * MJ + MJ - 1)}
                txf = None
                for j in range(MJ):
                    if j == 2:
                        if pend_ln2[0] is not None:
                            pend_ln2[0]()
                            pend_ln2[0] = None
                        for k in range(KC):
                            s.w(SY, fblast.get((sw - 1) * KC + k))
                            txf = s.dma(SY, lambda e, k=k, sw=sw: e.dma_start(
                                fbufA[:, k * SW:(k + 1) * SW], st_h[k][:, sw * SW:(sw + 1) * SW]),
                                "xf", flip=(k == 0))
                    pm = pf[j % 2]
                    s.w(TE, tW1s[j], thb, tre.get(j - 2), tyadd.get(-1))
                    tk = None
                    for k in range(KC):
                        tk = s.op(TE, lambda e, j=j, k=k, pm=pm: e.matmul(
                            pm[:, 0:TG], w1sb[:, j * 1024 + k * 128:j * 1024 + k * 128 + 128],
                            hbf[:, k * SW:k * SW + TG], start=(k == 0), stop=(k == KC - 1)))
                    s.w(A, tk)
                    tre[j] = s.op(A, lambda e, j=j, pm=pm, l=l: e.activation(
                        h1f[:, j * TG:(j + 1) * TG], pm[:, 0:TG], AF.Relu,
                        bias=pvcol(l, "bf1", j), scale=1.0))
                    treF[sw * MJ + j] = tre[j]
                for mo in range(KC):
                    pm = pf[2 + mo % 2]
                    s.w(TE, tW2s[mo], tre[MJ - 1], tyadd.get(mo - 2))
                    if mo < 2:
                        s.w(A, *(thwF.get((sw - 1) * KC + KC - 2 + mo) or []))
                    tk = None
                    for j in range(MJ):
                        tk = s.op(TE, lambda e, mo=mo, j=j, pm=pm: e.matmul(
                            pm[:, 0:TG], w2sb[:, mo * 4096 + j * 128:mo * 4096 + j * 128 + 128],
                            h1f[:, j * TG:(j + 1) * TG], start=(j == 0), stop=(j == MJ - 1)))
                    s.w(A, tk, tyadd.get(mo - 2))
                    ta = s.op(A, lambda e, mo=mo, pm=pm, l=l: e.activation(
                        tscr[:, (mo % 2) * SW:(mo % 2) * SW + TG],
                        pm[:, 0:TG], AF.Identity, bias=pvcol(l, "bf2", mo), scale=1.0))
                    s.w(V, ta, txf)
                    tyadd[mo] = s.op(V, lambda e, mo=mo: e.tensor_add(
                        fbufA[:, mo * SW:mo * SW + TG],
                        fbufA[:, mo * SW:mo * SW + TG],
                        tscr[:, (mo % 2) * SW:(mo % 2) * SW + TG]))
                    tyaddF[sw * KC + mo] = tyadd[mo]
                gates = [tyadd[mo] for mo in range(KC)] + (thwF.get((sw - 1) * KC + KC - 2) or [])
                phB2 = emit_ln_sums(gates=gates)

                def mk_ln2(sw, phB2, l=l):
                    def flush():
                        tbA, tbB = phB2(pf[4], pf[5], pf[4], pf[5])
                        thw = {-2: thwF.get((sw - 1) * KC + KC - 2),
                               -1: thwF.get((sw - 1) * KC + KC - 1)}
                        for k in range(KC):
                            kl = k % 2
                            s.w(V, tbA, tbB, *(thw.get(k - 2) or []))
                            t1k = s.op(V, lambda e, k=k, kl=kl: e.tensor_mul(
                                tscr[:, kl * SW:(kl + 1) * SW],
                                fbufA[:, k * SW:(k + 1) * SW], pf[4][:, 0:SW]))
                            tt2 = s.op(V, lambda e, kl=kl: e.tensor_add(
                                tscr[:, kl * SW:(kl + 1) * SW],
                                tscr[:, kl * SW:(kl + 1) * SW], pf[5][:, 0:SW]))
                            s.w(A, tt2)
                            th = s.op(A, lambda e, k=k, kl=kl, l=l: e.activation(
                                hsl[:, kl * SW:(kl + 1) * SW], tscr[:, kl * SW:(kl + 1) * SW], AF.Identity,
                                bias=pvcol(l, "ln2b", k), scale=pvcol(l, "ln2g", k)))
                            if l < L - 1:
                                s.w(SY, th)
                                thw[k] = [s.dma(SY, lambda e, k=k, sw=sw, kl=kl: e.dma_start(
                                    st_a[k][:, sw * SW:(sw + 1) * SW], hsl[:, kl * SW:(kl + 1) * SW]),
                                    "hw", flip=True)]
                                thwF[sw * KC + k] = thw[k]
                                fblast[sw * KC + k] = t1k
                            else:
                                s.w(SY, t1k)
                                tx1l = s.dma(SY, lambda e, k=k, sw=sw: e.dma_start(
                                    fbufA[:, k * SW:(k + 1) * SW], st_x1[k][:, sw * SW:(sw + 1) * SW]),
                                    "ld", flip=True)
                                s.w(V, th, tx1l)
                                tx2 = s.op(V, lambda e, k=k, kl=kl: e.tensor_add(
                                    tscr[:, kl * SW:(kl + 1) * SW], hsl[:, kl * SW:(kl + 1) * SW],
                                    fbufA[:, k * SW:(k + 1) * SW]))
                                s.w(SY, tx2)
                                tyd = s.dma(SY, lambda e, k=k, sw=sw, kl=kl: e.dma_start(
                                    yshard[k][:, sw * SW:(sw + 1) * SW], tscr[:, kl * SW:(kl + 1) * SW]),
                                    "ou", flip=True)
                                tvm = s.op(V, lambda e, kl=kl: e.tensor_scalar(
                                    hsl[:, kl * SW:(kl + 1) * SW], tscr[:, kl * SW:(kl + 1) * SW],
                                    m0[:, 0:1], None, ALU.mult))
                                s.w(SY, tvm)
                                tcxk = s.dma(SY, lambda e, k=k, sw=sw, kl=kl: e.dma_start(
                                    ccx2_in[sw * SW + (k // 2) * 128:sw * SW + (k // 2) * 128 + 128,
                                            (k % 2) * SW:(k % 2 + 1) * SW],
                                    hsl[:, kl * SW:(kl + 1) * SW]), "hw", flip=True)
                                thw[k] = [tyd, tcxk]
                                thwF[sw * KC + k] = thw[k]
                                fblast[sw * KC + k] = tx2
                    return flush
                pend_ln2[0] = mk_ln2(sw, phB2)
            if pend_ln2[0] is not None:
                pend_ln2[0]()
                pend_ln2[0] = None
            s.barrier()

        s.w(G, *s.chtok("hw"))
        if nocc:
            tccx = s.dma(G, lambda g: g.dma_start(ccx2_out, ccx2_in[0:MTOK, :]), "wq", flip=True)
        else:
            tccx = s.collective("ReduceScatter", ALU.add, [ccx2_in], [ccx2_out])

        tewb = build_ewbs(s.token(V))
        s.w(SY, tccx)
        tx5 = None
        for k in range(KC):
            tx5 = s.dma(SY, lambda e, k=k: e.dma_start(
                fbufA[:, k * SW:(k + 1) * SW],
                ccx2_out[(k // 2) * 128:(k // 2) * 128 + 128, (k % 2) * SW:(k % 2 + 1) * SW]),
                "xf", flip=(k == 0))
        tbA, tbB = emit_ln(pf[4], pf[5], pf[5], pf[6], gates=(tx5, tewb))
        tgxg = None
        for k in range(KC):
            kl = k % 2
            s.w(V, tbA, tbB, tgxg)
            s.op(V, lambda e, k=k, kl=kl: e.tensor_mul(hsl[:, kl * SW:(kl + 1) * SW],
                                                       fbufA[:, k * SW:(k + 1) * SW], pf[5][:, 0:SW]))
            tt2 = s.op(V, lambda e, kl=kl: e.tensor_add(hsl[:, kl * SW:(kl + 1) * SW],
                                                        hsl[:, kl * SW:(kl + 1) * SW], pf[6][:, 0:SW]))
            s.w(A, tt2)
            tgxg = s.op(A, lambda e, k=k, kl=kl: e.activation(
                xg[:, k * SW:(k + 1) * SW], hsl[:, kl * SW:(kl + 1) * SW], AF.Identity, bias=0.0, scale=1.0))

        ydma = {}

        def wr_moe3(mo, td, src):
            s.w(V, td, ydma.get(mo - 2))
            ty = s.op(V, lambda e, mo=mo, src=src: e.tensor_add(
                hsl[:, (mo % 2) * SW:(mo % 2 + 1) * SW],
                fbufA[:, mo * SW:(mo + 1) * SW], src))
            s.w(SY, ty)
            ydma[mo] = s.dma(SY, lambda e, mo=mo: e.dma_start(
                ymoe3[mo], hsl[:, (mo % 2) * SW:(mo % 2 + 1) * SW]), "hw", flip=True)
            return ydma[mo]
        emit_moe(tgxg, wr_moe3)

        s.drain_stores()
        s.op(SY, lambda e: e.nop())
        s.replay(block)
    return nc


_NC_CACHE = {}


def kernel(L0_attn_x, L0_attn_Wq, L0_attn_bq, L0_attn_Wk, L0_attn_bk,
           L0_attn_Wv, L0_attn_bv, L0_attn_Wo, L0_attn_bo, L0_attn_omega,
           L0_attn_ln1_g, L0_attn_ln1_b, L0_attn_ln2_g, L0_attn_ln2_b,
           L0_attn_W1, L0_attn_bf1, L0_attn_W2, L0_attn_bf2,
           L0_ffn_scores, L0_ffn_expert_weights, L0_ffn_top_experts,
           L0_ffn_w1, L0_ffn_w2):
    x = np.ascontiguousarray(np.asarray(L0_attn_x, np.float32))
    B, S, _ = x.shape

    def fm(a):
        return np.ascontiguousarray(np.asarray(a, np.float32).T.reshape(KC, 128, -1))

    def cols(v):
        return np.asarray(v, np.float32).reshape(8, 128).T

    pvec = np.zeros((128, 312), np.float32)
    for l in range(L):
        base = l * 104
        pvec[:, base + 0:base + 8] = cols(L0_attn_bq[l])
        pvec[:, base + 8:base + 16] = cols(L0_attn_bk[l])
        pvec[:, base + 16:base + 24] = cols(L0_attn_bv[l])
        pvec[:, base + 24:base + 32] = cols(L0_attn_bo[l])
        pvec[:, base + 32:base + 40] = cols(L0_attn_ln1_g[l])
        pvec[:, base + 40:base + 48] = cols(L0_attn_ln1_b[l])
        pvec[:, base + 48:base + 56] = cols(L0_attn_ln2_g[l])
        pvec[:, base + 56:base + 64] = cols(L0_attn_ln2_b[l])
        pvec[:, base + 64:base + 72] = cols(L0_attn_bf2[l])
        pvec[:, base + 72:base + 104] = np.asarray(L0_attn_bf1[l], np.float32).reshape(32, 128).T
    rowv = np.ascontiguousarray(np.asarray(L0_attn_bv, np.float32).reshape(1, L * D))
    ew = np.asarray(L0_ffn_expert_weights, np.float32)

    if "nc" not in _NC_CACHE:
        _NC_CACHE["nc"] = build()
    nc = _NC_CACHE["nc"]

    in_maps = []
    for i in range(NCORE):
        b, half = i // 2, i % 2
        m0v = np.full((128, 1), 1.0 if i == 0 else 0.0, np.float32)
        sc = np.zeros((128, 8), np.float32)
        if i % 2 == 1:
            sc[:, i - 1] = 1.0
        in_maps.append({
            "xshard": fm(x[b, half * T:(half + 1) * T, :]),
            "xmoe": fm(x[0, i * MTOK:(i + 1) * MTOK, :]),
            "ewT": np.ascontiguousarray(ew[i * MTOK:(i + 1) * MTOK, :].T),
            "pvec": pvec, "rowv": rowv,
            "omg_in": np.asarray(L0_attn_omega, np.float32),
            "wq_in": np.asarray(L0_attn_Wq, np.float32),
            "wk_in": np.asarray(L0_attn_Wk, np.float32),
            "wv_in": np.asarray(L0_attn_Wv, np.float32),
            "wo_in": np.asarray(L0_attn_Wo, np.float32),
            "w1_in": np.asarray(L0_attn_W1, np.float32),
            "w2_in": np.asarray(L0_attn_W2, np.float32),
            "mw1_in": np.asarray(L0_ffn_w1, np.float32),
            "mw2_in": np.asarray(L0_ffn_w2, np.float32),
            "mask0": m0v, "scoef": sc,
        })
    res = run_bass_kernel_spmd(nc, in_maps, core_ids=list(range(NCORE)))
    out = np.empty((B, S, D), np.float32)
    for i in range(NCORE):
        b, half = i // 2, i % 2
        ys = np.asarray(res.results[i]["yshard"])
        out[b, half * T:(half + 1) * T, :] = ys.transpose(2, 0, 1).reshape(T, D)
    for i in range(NCORE):
        ym = np.asarray(res.results[i]["ymoe3"])
        out[0, i * MTOK:(i + 1) * MTOK, :] = ym.transpose(2, 0, 1).reshape(MTOK, D)
    return out


# version 30
# speedup vs baseline: 1.1214x; 1.0159x over previous
import numpy as np
import concourse.bass as bass
import concourse.mybir as mybir
from concourse.bass_utils import run_bass_kernel_spmd

F32 = mybir.dt.float32
BF16 = mybir.dt.bfloat16
AF = mybir.ActivationFunctionType
ALU = mybir.AluOpType

NCORE = 8
D = 1024
KC = 8
T = 4096
NSW = 8
SW = 512
CH = 128
NCHS = 4
NP = 8
DH = 64
M = 32
DFF = 4096
MJ = 32
MOE_I = 8192
MOE_J = 64
MTOK = 512
L = 3
LN_EPS = 1e-5
ATTN_EPS = 1e-6
SC = float(DH) ** -0.25
CSQ = 0.0625
LOGM = float(np.log((2.0 * M) ** -0.5))
LOGEPS = float(np.log(ATTN_EPS))
GC1 = 0.044715
GC2 = float(2.0 * np.sqrt(2.0 / np.pi))
RG = [[0, 1, 2, 3, 4, 5, 6, 7]]
TG = 512
LN_V2 = False

ENGINES = ("tensor", "vector", "scalar", "gpsimd", "sync")
G, V, A, TE, SY = "gpsimd", "vector", "scalar", "tensor", "sync"


class Sched:
    AUTO_CHAIN = ("vector", "scalar", "gpsimd")

    def __init__(self, nc, ctx):
        self.nc = nc
        self.ops = {e: [] for e in ENGINES}
        self.cnt = {}
        self.last_waited = {e: {} for e in ENGINES}
        self.sems = {}
        self._ctx = ctx
        for e in ENGINES:
            self.sems[e] = ctx.enter_context(nc.semaphore(f"s_{e}"))
            self.cnt[e] = 0
        self.chan = {}

    def op(self, eng, fn):
        sem = self.sems[eng]
        if eng in self.AUTO_CHAIN and self.cnt[eng] > 0:
            prev = self.cnt[eng]
            if self.last_waited[eng].get(eng, 0) < prev:
                self.last_waited[eng][eng] = prev
                def emitw(e, sem=sem, val=prev):
                    e.wait_ge(sem, val)
                self.ops[eng].append(emitw)
        self.cnt[eng] += 1
        want = self.cnt[eng]
        def emit(e, fn=fn, sem=sem):
            fn(e).then_inc(sem, 1)
        self.ops[eng].append(emit)
        return (eng, want)

    def mkchan(self, name):
        sems = [self._ctx.enter_context(self.nc.semaphore(f"c_{name}{i}"))
                for i in range(2)]
        self.chan[name] = {"sems": sems, "slot": 0}
        for i in range(2):
            self.sems[f"{name}{i}"] = sems[i]
            self.cnt[f"{name}{i}"] = 0

    def dma(self, eng, fn, ch, flip=False):
        c = self.chan[ch]
        if flip:
            c["slot"] ^= 1
        slot = c["slot"]
        key = f"{ch}{slot}"
        sem = c["sems"][slot]
        if flip and self.cnt[key] > 0:
            prev = self.cnt[key]
            if self.last_waited[eng].get(key, 0) < prev:
                self.last_waited[eng][key] = prev
                def emitw(e, sem=sem, val=prev):
                    e.wait_ge(sem, val)
                self.ops[eng].append(emitw)
        self.cnt[key] += 16
        want = self.cnt[key]
        def emit(e, fn=fn, sem=sem):
            fn(e).then_inc(sem, 16)
        self.ops[eng].append(emit)
        return (key, want)

    def chtok(self, ch):
        return [(f"{ch}{i}", self.cnt[f"{ch}{i}"]) for i in range(2)]

    def collective(self, kind, op, ins, outs):
        name = f"cc{len([k for k in self.sems if k.startswith('cc')])}"
        sem = self._ctx.enter_context(self.nc.semaphore(name))
        self.sems[name] = sem
        self.cnt[name] = 1
        def emit(e, sem=sem):
            e.collective_compute(kind, op, replica_groups=RG,
                                 ins=[i.opt() for i in ins],
                                 outs=[o.opt() for o in outs]).then_inc(sem)
        self.ops["gpsimd"].append(emit)
        return (name, 1)

    def w(self, eng, *tokens):
        for t in tokens:
            if t is None:
                continue
            (src, val) = t
            if src == eng:
                continue
            if val <= self.last_waited[eng].get(src, 0):
                continue
            self.last_waited[eng][src] = val
            sem = self.sems[src]
            def emit(e, sem=sem, val=val):
                e.wait_ge(sem, val)
            self.ops[eng].append(emit)

    def token(self, eng):
        return (eng, self.cnt[eng])

    def drain_stores(self):
        self.w(SY, *self.chtok("st"), *self.chtok("ou"),
               *self.chtok("hw"), *self.chtok("sn"))

    def barrier(self):
        self.drain_stores()
        self.op(SY, lambda e: e.nop())
        toks = [self.token(e) for e in ENGINES]
        for e in ENGINES:
            self.w(e, *toks)

    def replay(self, block):
        for e in ENGINES:
            ops = self.ops[e]
            def mk(ops):
                def run(engine):
                    for fn in ops:
                        fn(engine)
                return run
            getattr(block, e)(mk(ops))


def build(nocc=False):
    import contextlib
    nc = bass.Bass("TRN2", target_bir_lowering=False)
    DT = nc.dram_tensor
    xshard = DT("xshard", [KC, 128, T], F32, kind="ExternalInput").ap()
    xmoe = DT("xmoe", [KC, 128, MTOK], F32, kind="ExternalInput").ap()
    ewT = DT("ewT", [2, MTOK], F32, kind="ExternalInput").ap()
    pvec = DT("pvec", [128, 312], F32, kind="ExternalInput").ap()
    rowv = DT("rowv", [1, L * D], F32, kind="ExternalInput").ap()
    omg_in = DT("omg_in", [L, DH, M], F32, kind="ExternalInput").ap()
    wq_in = DT("wq_in", [L, D, D], F32, kind="ExternalInput").ap()
    wk_in = DT("wk_in", [L, D, D], F32, kind="ExternalInput").ap()
    wv_in = DT("wv_in", [L, D, D], F32, kind="ExternalInput").ap()
    wo_in = DT("wo_in", [L, D, D], F32, kind="ExternalInput").ap()
    w1_in = DT("w1_in", [L, D, DFF], F32, kind="ExternalInput").ap()
    w2_in = DT("w2_in", [L, DFF, D], F32, kind="ExternalInput").ap()
    mw1_in = DT("mw1_in", [D, MOE_I], F32, kind="ExternalInput").ap()
    mw2_in = DT("mw2_in", [MOE_I, D], F32, kind="ExternalInput").ap()
    mask0 = DT("mask0", [128, 1], F32, kind="ExternalInput").ap()
    scoef = DT("scoef", [128, 8], F32, kind="ExternalInput").ap()
    yshard = DT("yshard", [KC, 128, T], F32, kind="ExternalOutput").ap()
    ymoe3 = DT("ymoe3", [KC, 128, MTOK], F32, kind="ExternalOutput").ap()
    st_x1 = DT("st_x1", [KC, 128, T], F32).ap()
    st_a = DT("st_a", [KC, 128, T], F32).ap()
    st_h = DT("st_h", [KC, 128, T], F32).ap()
    phk_d = DT("phk_d", [NP, 128, T], BF16).ap()
    vraw_d = DT("vraw_d", [T, D], BF16).ap()
    snap_d = DT("snap_d", [32, 128, 520], F32).ap()
    ccm1_in = DT("ccm1_in", [128, KC * SW], F32).ap()
    ccm1_out = DT("ccm1_out", [NCORE * 128, KC * SW], F32,
                  addr_space=("Local" if nocc else "Shared")).ap()
    ccst_in = DT("ccst_in", [128, 520], F32).ap()
    ccst_out = DT("ccst_out", [NCORE * 128, 520], F32,
                  addr_space=("Local" if nocc else "Shared")).ap()
    ccx2_in = DT("ccx2_in", [T, D], F32).ap()
    ccx2_out = DT("ccx2_out", [MTOK, D], F32).ap()

    ctx = contextlib.ExitStack()
    with ctx:
        Sb = lambda *a: ctx.enter_context(nc.sbuf_tensor(*a))
        Ps = lambda *a: ctx.enter_context(nc.psum_tensor(*a))
        pv = Sb("pv", [128, 312], F32)
        rvb = Sb("rvb", [1, D], BF16)
        omgs = Sb("omgs", [128, L * M], F32)
        obs = Sb("obs", [128, L * 128], BF16)
        cbt = Sb("cbt", [128, 128], BF16)
        hm2 = Sb("hm2", [128, 2], BF16)
        identb = Sb("identb", [128, 128], BF16)
        maskf = Sb("maskf", [128, 128], F32)
        onesf = Sb("onesf", [128, 128], F32)
        onesbr = Sb("onesbr", [1, 128], BF16)
        halfo = Sb("halfo", [2, 1], F32)
        biasm = Sb("biasm", [128, 2], F32)
        m0 = Sb("m0", [128, 1], F32)
        scf = Sb("scf", [128, 8], F32)
        fbufA = Sb("fbufA", [128, KC * SW], F32)
        tscr = Sb("tscr", [128, 2 * SW], F32)
        hsl = Sb("hsl", [128, 2 * SW], F32)
        rows = Sb("rows", [1, 4 * SW], F32)
        wreg = Sb("wreg", [128, 65536], BF16)
        arena = Sb("arena", [128, 20480], BF16)
        srun = arena[:, 0:2080].bitcast(F32)
        sinit = arena[:, 2080:3120].bitcast(F32)
        srq = arena[:, 3120:4160].bitcast(F32)
        snapb = arena[:, 4160:6240].bitcast(F32)
        sbe = arena[:, 6240:6760]
        sbo = arena[:, 6760:7280]
        vts = arena[:, 7280:9360]
        phkt = arena[:, 9360:10384]
        eps2 = arena[0:2, 10384:14480]
        h1f = arena[:, 0:16384]
        hbf = arena[:, 16384:20480]
        ATMB = 14480
        atk0 = lambda pi, hh: arena[:, ATMB + pi * 768 + hh * 256:ATMB + pi * 768 + (hh + 1) * 256]
        atk1 = lambda pi, hh: arena[:, ATMB + pi * 768 + 512 + hh * 128:ATMB + pi * 768 + 512 + (hh + 1) * 128]
        vts1 = arena[:, 16528:18608]
        rdbs2 = [arena[:, 16016:16528].bitcast(F32),
                 arena[:, 19632:20144].bitcast(F32)]
        mw = arena[:, 9360:10384].bitcast(F32)
        sq = arena[:, 14480:15504].bitcast(F32)
        ewbs = arena[:, 15504:16528].bitcast(F32)
        ewts = arena[0:2, 16528:17552].bitcast(F32)
        idf = fbufA[:, 0:128]
        rvf = fbufA[0:1, SW:SW + D]

        wqsb = wreg[:, 0:8192]
        wksb = wreg[:, 8192:16384]
        wvsb = wreg[:, 16384:24576]
        wosb = wreg[:, 24576:32768]
        xbw = [wreg[:, 32768 + i * 4096:32768 + (i + 1) * 4096] for i in range(2)]
        ksb = wreg[:, 40960:45056]
        k2sb = wreg[:, 45056:49152]
        phkb = wreg[:, 49152:53248]
        phqb = wreg[:, 53248:57344]
        asw = wreg[:, 57344:61440]
        vstage = wreg[:, 61440:63488]
        phz = wreg[:, 61440:65536]
        w1sb = wreg[:, 0:32768]
        w2sb = wreg[:, 32768:65536]
        mw1b = [wreg[:, i * 1024:(i + 1) * 1024] for i in range(4)]
        mw2b = [wreg[:, 4096 + i * 8192:4096 + (i + 1) * 8192] for i in range(2)]
        h1m = wreg[:, 20480:53248]
        xg = wreg[:, 53248:57344]

        pf = [Ps(f"pf{i}", [128, 512], F32) for i in range(7)]
        pbt = Ps("pbt", [128, 1024], BF16)
        pbtF = pbt[:, 0:1024].bitcast(F32)

        s = Sched(nc, ctx)
        for chn in ("wq", "wf", "xb", "xf", "st", "ld", "ou", "ph", "sn", "hw"):
            s.mkchan(chn)
        block = ctx.enter_context(nc.Block())

        def pvcol(l, name, k):
            off = {"bq": 0, "bk": 8, "bv": 16, "bo": 24, "ln1g": 32, "ln1b": 40,
                   "ln2g": 48, "ln2b": 56, "bf2": 64, "bf1": 72}[name]
            cidx = l * 104 + off + k
            return pv[:, cidx:cidx + 1]

        def do_cc(kind, op, cin, cout, nrow):
            if nocc:
                return s.dma(G, lambda g: g.dma_start(cout[0:nrow, :], cin), "ou")
            return s.collective(kind, op, [cin], [cout])

        s.dma(G, lambda g: g.dma_start(pv[:, :], pvec), "wq")
        s.dma(G, lambda g: g.dma_start(m0[:, :], mask0), "wq")
        s.dma(G, lambda g: g.dma_start(scf[:, :], scoef), "wq")
        tldw = None
        for l in range(L):
            s.dma(G, lambda g, l=l: g.dma_start(omgs[0:64, l * M:(l + 1) * M], omg_in[l]), "wq")
            tldw = s.dma(G, lambda g, l=l: g.dma_start(omgs[64:128, l * M:(l + 1) * M], omg_in[l]), "wq")
        s.op(G, lambda g: g.memset(onesf[:, :], 1.0))
        s.op(G, lambda g: g.memset(onesbr[:, :], 1.0))
        s.op(G, lambda g: g.memset(halfo[:, :], 0.5))
        s.op(G, lambda g: g.memset(biasm[:, 0:1], LOGM))
        s.op(G, lambda g: g.memset(biasm[:, 1:2], LOGEPS))
        s.op(G, lambda g: g.memset(cbt[:, :], 0.0))
        s.op(G, lambda g: g.memset(hm2[:, :], 0.0))
        s.op(G, lambda g: g.memset(obs[:, :], 0.0))
        s.op(G, lambda g: g.memset(sbe[:, :], 0.0))
        s.op(G, lambda g: g.memset(sbo[:, :], 0.0))
        for sl in range(2):
            for i in range(16):
                s.op(G, lambda g, sl=sl, i=i: g.memset(
                    vts[:, sl * 1040 + i * 65 + 64:sl * 1040 + i * 65 + 65], 1.0))
        s.op(G, lambda g: g.affine_select(idf[:, :], onesf[:, :], pattern=[[1, 128]],
                                          compare_op=ALU.is_equal, fill=0.0,
                                          base=0, channel_multiplier=-1))
        tgm = s.op(G, lambda g: g.affine_select(maskf[:, :], onesf[:, :], pattern=[[1, 128]],
                                               compare_op=ALU.is_ge, fill=0.0,
                                               base=0, channel_multiplier=-1))
        s.w(A, tgm, tldw)
        s.op(A, lambda e: e.activation(identb[:, :], idf[:, :], AF.Identity, bias=0.0, scale=1.0))
        s.op(A, lambda e: e.activation(cbt[0:64, 0:64], onesf[0:64, 0:64], AF.Identity, bias=0.0, scale=-CSQ))
        s.op(A, lambda e: e.activation(cbt[64:128, 64:128], onesf[64:128, 0:64], AF.Identity, bias=0.0, scale=-CSQ))
        s.op(A, lambda e: e.activation(hm2[0:64, 0:1], onesf[0:64, 0:1], AF.Identity, bias=0.0, scale=CSQ))
        s.op(A, lambda e: e.activation(hm2[64:128, 1:2], onesf[64:128, 0:1], AF.Identity, bias=0.0, scale=CSQ))
        tcons = None
        for l in range(L):
            ob = obs[:, l * 128:(l + 1) * 128]
            om0 = omgs[0:64, l * M:(l + 1) * M]
            om1 = omgs[64:128, l * M:(l + 1) * M]
            s.op(A, lambda e, ob=ob, om0=om0: e.activation(ob[0:64, 0:32], om0, AF.Identity, bias=0.0, scale=SC))
            s.op(A, lambda e, ob=ob, om0=om0: e.activation(ob[0:64, 32:64], om0, AF.Identity, bias=0.0, scale=-SC))
            s.op(A, lambda e, ob=ob, om1=om1: e.activation(ob[64:128, 64:96], om1, AF.Identity, bias=0.0, scale=SC))
            tcons = s.op(A, lambda e, ob=ob, om1=om1: e.activation(ob[64:128, 96:128], om1, AF.Identity, bias=0.0, scale=-SC))

        def build_ewbs(gate):
            tld = s.dma(G, lambda g: g.dma_start(ewts[:, :], ewT), "wq", flip=True)
            s.w(TE, tld, gate, tcons)
            t1 = s.op(TE, lambda e: e.matmul(pf[4][0:1, 0:MTOK], halfo[:, :], ewts[:, :], start=True, stop=True))
            s.w(V, t1)
            t2 = s.op(V, lambda e: e.tensor_copy(rows[0:1, 0:MTOK], pf[4][0:1, 0:MTOK]))
            s.w(TE, t2)
            t3 = s.op(TE, lambda e: e.matmul(pf[6][:, 0:MTOK], onesf[0:1, :], rows[0:1, 0:MTOK], start=True, stop=True))
            s.w(A, t3)
            return s.op(A, lambda e: e.activation(ewbs[:, :], pf[6][:, 0:MTOK], AF.Identity, bias=0.0, scale=1.0))

        def emit_ln_v1(rbA, rbB, pbA, pbB, gates=()):
            s.w(V, *gates)
            s.w(A, *gates)
            tv = s.op(V, lambda e: e.tensor_add(tscr[:, 0:SW], fbufA[:, 0:SW], fbufA[:, SW:2 * SW]))
            for k in range(2, KC):
                tv = s.op(V, lambda e, k=k: e.tensor_add(tscr[:, 0:SW], tscr[:, 0:SW], fbufA[:, k * SW:(k + 1) * SW]))
            ta = s.op(A, lambda e: e.activation(sq[:, :], fbufA[:, 0:SW], AF.Square))
            s.w(V, ta)
            tq = s.op(V, lambda e: e.tensor_copy(tscr[:, SW:2 * SW], sq[:, :]))
            for k in range(1, KC):
                s.w(A, tq)
                ta = s.op(A, lambda e, k=k: e.activation(sq[:, :], fbufA[:, k * SW:(k + 1) * SW], AF.Square))
                s.w(V, ta)
                tq = s.op(V, lambda e: e.tensor_add(tscr[:, SW:2 * SW], tscr[:, SW:2 * SW], sq[:, :]))
            s.w(TE, tq, tv)
            tm1 = s.op(TE, lambda e, rbA=rbA: e.matmul(rbA[0:1, 0:SW], onesf[:, 0:1], tscr[:, 0:SW], start=True, stop=True))
            tm2 = s.op(TE, lambda e, rbB=rbB: e.matmul(rbB[0:1, 0:SW], onesf[:, 0:1], tscr[:, SW:2 * SW], start=True, stop=True))
            R = lambda i: rows[0:1, i * SW:(i + 1) * SW]
            s.w(V, tm1, tm2)
            s.op(V, lambda e, rbA=rbA: e.tensor_scalar(R(0), rbA[0:1, 0:SW], 1.0 / D, None, ALU.mult))
            tr0 = s.op(V, lambda e, rbB=rbB: e.tensor_scalar(R(1), rbB[0:1, 0:SW], 1.0 / D, LN_EPS, ALU.mult, ALU.add))
            s.w(A, tr0)
            ta1 = s.op(A, lambda e: e.activation(R(2), R(0), AF.Square))
            s.w(V, ta1)
            tvv = s.op(V, lambda e: e.tensor_sub(R(1), R(1), R(2)))
            s.w(A, tvv)
            s.op(A, lambda e: e.activation(R(2), R(1), AF.Ln))
            tsq2 = s.op(A, lambda e: e.activation(R(3), R(2), AF.Exp, bias=0.0, scale=-0.5))
            s.w(V, tsq2)
            s.op(V, lambda e: e.tensor_mul(R(2), R(0), R(3)))
            trB = s.op(V, lambda e: e.tensor_scalar(R(2), R(2), -1.0, None, ALU.mult))
            s.w(TE, trB, tsq2)
            tbA = s.op(TE, lambda e, pbA=pbA: e.matmul(pbA[:, 0:SW], onesf[0:1, :], R(3), start=True, stop=True))
            tbB = s.op(TE, lambda e, pbB=pbB: e.matmul(pbB[:, 0:SW], onesf[0:1, :], R(2), start=True, stop=True))
            return tbA, tbB

        def emit_ln(rbA, rbB, pbA, pbB, gates=()):
            s.w(TE, *gates)
            s.w(A, *gates)
            for k in range(KC):
                s.op(TE, lambda e, k=k, rbA=rbA: e.matmul(
                    rbA[0:1, 0:SW], onesf[:, 0:1], fbufA[:, k * SW:(k + 1) * SW],
                    start=(k == 0), stop=(k == KC - 1)))
            tmq = {}
            for k in range(KC):
                kl = k % 2
                if k >= 2:
                    s.w(A, tmq[k - 2])
                tsq = s.op(A, lambda e, k=k, kl=kl: e.activation(
                    tscr[:, kl * SW:(kl + 1) * SW], fbufA[:, k * SW:(k + 1) * SW], AF.Square))
                s.w(TE, tsq)
                tmq[k] = s.op(TE, lambda e, k=k, kl=kl, rbB=rbB: e.matmul(
                    rbB[0:1, 0:SW], onesf[:, 0:1], tscr[:, kl * SW:(kl + 1) * SW],
                    start=(k == 0), stop=(k == KC - 1)))
            R = lambda i: rows[0:1, i * SW:(i + 1) * SW]
            s.w(V, tmq[KC - 1])
            s.op(V, lambda e, rbA=rbA: e.tensor_scalar(R(0), rbA[0:1, 0:SW], 1.0 / D, None, ALU.mult))
            tr0 = s.op(V, lambda e, rbB=rbB: e.tensor_scalar(R(1), rbB[0:1, 0:SW], 1.0 / D, LN_EPS, ALU.mult, ALU.add))
            s.w(A, tr0)
            ta1 = s.op(A, lambda e: e.activation(R(2), R(0), AF.Square))
            s.w(V, ta1)
            tvv = s.op(V, lambda e: e.tensor_sub(R(1), R(1), R(2)))
            s.w(A, tvv)
            s.op(A, lambda e: e.activation(R(2), R(1), AF.Ln))
            trs = s.op(A, lambda e: e.activation(R(3), R(2), AF.Exp, bias=0.0, scale=-0.5))
            s.w(V, trs)
            s.op(V, lambda e: e.tensor_mul(R(2), R(0), R(3)))
            trB = s.op(V, lambda e: e.tensor_scalar(R(2), R(2), -1.0, None, ALU.mult))
            s.w(TE, trB, trs)
            tbA = s.op(TE, lambda e, pbA=pbA: e.matmul(pbA[:, 0:SW], onesf[0:1, :], R(3), start=True, stop=True))
            tbB = s.op(TE, lambda e, pbB=pbB: e.matmul(pbB[:, 0:SW], onesf[0:1, :], R(2), start=True, stop=True))
            return tbA, tbB

        if not LN_V2:
            emit_ln = emit_ln_v1

        def emit_ln_sums(gates=()):
            s.w(V, *gates)
            s.w(A, *gates)
            tv = s.op(V, lambda e: e.tensor_add(tscr[:, 0:SW], fbufA[:, 0:SW], fbufA[:, SW:2 * SW]))
            for k in range(2, KC):
                tv = s.op(V, lambda e, k=k: e.tensor_add(tscr[:, 0:SW], tscr[:, 0:SW], fbufA[:, k * SW:(k + 1) * SW]))
            hq = hsl[:, 0:SW]
            ta = s.op(A, lambda e, hq=hq: e.activation(hq, fbufA[:, 0:SW], AF.Square))
            s.w(V, ta)
            tq = s.op(V, lambda e, hq=hq: e.tensor_copy(tscr[:, SW:2 * SW], hq))
            for k in range(1, KC):
                s.w(A, tq)
                ta = s.op(A, lambda e, k=k, hq=hq: e.activation(hq, fbufA[:, k * SW:(k + 1) * SW], AF.Square))
                s.w(V, ta)
                tq = s.op(V, lambda e, hq=hq: e.tensor_add(tscr[:, SW:2 * SW], tscr[:, SW:2 * SW], hq))

            def phaseB(rbA, rbB, pbA, pbB):
                s.w(TE, tq, tv)
                tm1 = s.op(TE, lambda e, rbA=rbA: e.matmul(rbA[0:1, 0:SW], onesf[:, 0:1], tscr[:, 0:SW], start=True, stop=True))
                tm2 = s.op(TE, lambda e, rbB=rbB: e.matmul(rbB[0:1, 0:SW], onesf[:, 0:1], tscr[:, SW:2 * SW], start=True, stop=True))
                R = lambda i: rows[0:1, i * SW:(i + 1) * SW]
                s.w(V, tm1, tm2)
                s.op(V, lambda e, rbA=rbA: e.tensor_scalar(R(0), rbA[0:1, 0:SW], 1.0 / D, None, ALU.mult))
                tr0 = s.op(V, lambda e, rbB=rbB: e.tensor_scalar(R(1), rbB[0:1, 0:SW], 1.0 / D, LN_EPS, ALU.mult, ALU.add))
                s.w(A, tr0)
                ta1 = s.op(A, lambda e: e.activation(R(2), R(0), AF.Square))
                s.w(V, ta1)
                tvv = s.op(V, lambda e: e.tensor_sub(R(1), R(1), R(2)))
                s.w(A, tvv)
                s.op(A, lambda e: e.activation(R(2), R(1), AF.Ln))
                tsq2 = s.op(A, lambda e: e.activation(R(3), R(2), AF.Exp, bias=0.0, scale=-0.5))
                s.w(V, tsq2)
                s.op(V, lambda e: e.tensor_mul(R(2), R(0), R(3)))
                trB = s.op(V, lambda e: e.tensor_scalar(R(2), R(2), -1.0, None, ALU.mult))
                s.w(TE, trB, tsq2)
                tbA = s.op(TE, lambda e, pbA=pbA: e.matmul(pbA[:, 0:SW], onesf[0:1, :], R(3), start=True, stop=True))
                tbB = s.op(TE, lambda e, pbB=pbB: e.matmul(pbB[:, 0:SW], onesf[0:1, :], R(2), start=True, stop=True))
                return tbA, tbB
            return phaseB

        def emit_moe(tgxg, wr_fn):
            gelu_done = None
            gelu_hist = {}
            cons1 = {}
            for j in range(MOE_J):
                sl = j % 4
                if cons1.get(j - 4) is not None:
                    s.w(G, cons1[j - 4])
                tl = s.dma(G, lambda g, j=j, sl=sl: g.dma_start(
                    mw1b[sl].rearrange("p (kc i) -> p kc i", i=128),
                    mw1_in[:, j * 128:(j + 1) * 128].rearrange("(kc p) i -> p kc i", p=128)),
                    "wf", flip=True)
                pm = pf[j % 2]
                s.w(TE, tl, tgxg, gelu_hist.get(j - 2))
                tk = None
                for k in range(KC):
                    tk = s.op(TE, lambda e, k=k, sl=sl, pm=pm: e.matmul(
                        pm[:, 0:SW], mw1b[sl][:, k * 128:(k + 1) * 128],
                        xg[:, k * SW:(k + 1) * SW], start=(k == 0), stop=(k == KC - 1)))
                cons1[j] = tk
                s.w(A, tk, gelu_done)
                th0 = s.op(A, lambda e, pm=pm: e.activation(tscr[:, 0:SW], pm[:, 0:SW], AF.Identity, bias=0.0, scale=1.0))
                th1 = s.op(A, lambda e, pm=pm: e.activation(sq[:, :], pm[:, 0:SW], AF.Square))
                s.w(V, th1, th0)
                s.op(V, lambda e: e.tensor_scalar(sq[:, :], sq[:, :], GC1, 1.0, ALU.mult, ALU.add))
                tg3 = s.op(V, lambda e: e.tensor_mul(sq[:, :], sq[:, :], tscr[:, 0:SW]))
                s.w(A, tg3)
                tg4 = s.op(A, lambda e: e.activation(sq[:, :], sq[:, :], AF.Sigmoid, bias=0.0, scale=GC2))
                s.w(V, tg4)
                gelu_done = s.op(V, lambda e, j=j: e.tensor_mul(h1m[:, j * SW:(j + 1) * SW], sq[:, :], tscr[:, 0:SW]))
                gelu_hist[j] = gelu_done
            cons2 = {}
            tds = {}
            last = None
            for mo in range(KC):
                sl = mo % 2
                if cons2.get(mo - 2) is not None:
                    s.w(G, cons2[mo - 2])
                tl = s.dma(G, lambda g, mo=mo, sl=sl: g.dma_start(
                    mw2b[sl].rearrange("p (j i) -> p j i", i=128),
                    mw2_in[:, mo * 128:(mo + 1) * 128].rearrange("(j p) i -> p j i", p=128)),
                    "wf", flip=True)
                pm = pf[2 + mo % 2]
                s.w(TE, tl, gelu_done, tds.get(mo - 2))
                tk = None
                for j in range(MOE_J):
                    tk = s.op(TE, lambda e, j=j, sl=sl, pm=pm: e.matmul(
                        pm[:, 0:SW], mw2b[sl][:, j * 128:(j + 1) * 128],
                        h1m[:, j * SW:(j + 1) * SW], start=(j == 0), stop=(j == MOE_J - 1)))
                cons2[mo] = tk
                s.w(V, tk, tewb)
                td = s.op(V, lambda e, pm=pm: e.tensor_mul(tscr[:, SW:2 * SW], pm[:, 0:SW], ewbs[:, :]))
                tds[mo] = td
                last = wr_fn(mo, td, tscr[:, SW:2 * SW])
                s.w(V, last)
            return last

        tewb = build_ewbs(None)
        s.w(SY, tcons, tewb)
        tx = None
        for k in range(KC):
            tx = s.dma(SY, lambda e, k=k: e.dma_start(fbufA[:, k * SW:(k + 1) * SW], xmoe[k]), "xf")
        tbA, tbB = emit_ln(pf[4], pf[5], pf[5], pf[6], gates=(tx, tewb))
        tgxg = None
        for k in range(KC):
            kl = k % 2
            s.w(V, tbA, tbB, tgxg)
            s.op(V, lambda e, k=k, kl=kl: e.tensor_mul(tscr[:, kl * SW:(kl + 1) * SW], fbufA[:, k * SW:(k + 1) * SW], pf[5][:, 0:SW]))
            tt2 = s.op(V, lambda e, kl=kl: e.tensor_add(tscr[:, kl * SW:(kl + 1) * SW], tscr[:, kl * SW:(kl + 1) * SW], pf[6][:, 0:SW]))
            s.w(A, tt2)
            tgxg = s.op(A, lambda e, k=k, kl=kl: e.activation(xg[:, k * SW:(k + 1) * SW], tscr[:, kl * SW:(kl + 1) * SW], AF.Identity, bias=0.0, scale=1.0))

        def wr_moe1(mo, td, src):
            s.w(SY, td)
            return s.dma(SY, lambda e, mo=mo, src=src: e.dma_start(
                ccm1_in[:, mo * SW:(mo + 1) * SW], src), "hw", flip=True)
        tm1w = emit_moe(tgxg, wr_moe1)
        s.drain_stores()
        s.w(G, *s.chtok("hw"))
        if nocc:
            for r in range(NCORE):
                tccm1 = s.dma(G, lambda g, r=r: g.dma_start(
                    ccm1_out[r * 128:(r + 1) * 128, :], ccm1_in), "wq", flip=(r == 0))
        else:
            tccm1 = s.collective("AllGather", ALU.bypass, [ccm1_in], [ccm1_out])
        s.barrier()

        amul = {}
        thwA = {}
        tx1s = {}
        stt_prev = None
        tbA_prev = None
        for sw in range(NSW):
            txl = None
            for k in range(KC):
                s.w(SY, amul.get((sw - 1) * KC + k), tx1s.get((sw - 1) * KC + k))
                txl = s.dma(SY, lambda e, k=k, sw=sw: e.dma_start(
                    fbufA[:, k * SW:(k + 1) * SW], xshard[k][:, sw * SW:(sw + 1) * SW]),
                    "xf", flip=(k == 0))
            tvmask = None
            for k in range(KC):
                kl = k % 2
                if k >= 2:
                    s.w(SY, tvmask)
                else:
                    s.w(SY, tccm1, stt_prev, tbA_prev)
                tgl = s.dma(SY, lambda e, k=k, sw=sw, kl=kl: e.dma_start(
                    tscr[:, kl * SW:(kl + 1) * SW],
                    ccm1_out[sw * 128:(sw + 1) * 128, k * SW:(k + 1) * SW]),
                    "ld", flip=True)
                s.w(V, txl, tgl, tvmask)
                tvmask = s.op(V, lambda e, k=k, kl=kl: e.scalar_tensor_tensor(
                    fbufA[:, k * SW:(k + 1) * SW], tscr[:, kl * SW:(kl + 1) * SW],
                    m0[:, 0:1], fbufA[:, k * SW:(k + 1) * SW], ALU.mult, ALU.add))
                s.w(SY, tvmask)
                tx1s[sw * KC + k] = s.dma(SY, lambda e, k=k, sw=sw: e.dma_start(
                    st_x1[k][:, sw * SW:(sw + 1) * SW], fbufA[:, k * SW:(k + 1) * SW]),
                    "st", flip=True)
            stt_prev = tvmask
            tbA, tbB = emit_ln(pf[0], pf[1], pf[1], pf[2],
                               gates=(tvmask, amul.get((sw - 1) * KC + KC - 1)))
            tbA_prev = tbA
            for k in range(KC):
                kl = k % 2
                s.w(V, tbA, tbB, thwA.get(sw * KC + k - 2))
                amul[sw * KC + k] = s.op(V, lambda e, k=k, kl=kl: e.tensor_mul(
                    hsl[:, kl * SW:(kl + 1) * SW],
                    fbufA[:, k * SW:(k + 1) * SW], pf[1][:, 0:SW]))
                tt2 = s.op(V, lambda e, kl=kl: e.tensor_add(hsl[:, kl * SW:(kl + 1) * SW],
                                                            hsl[:, kl * SW:(kl + 1) * SW], pf[2][:, 0:SW]))
                s.w(SY, tt2)
                thwA[sw * KC + k] = s.dma(SY, lambda e, k=k, sw=sw, kl=kl: e.dma_start(
                    st_a[k][:, sw * SW:(sw + 1) * SW], hsl[:, kl * SW:(kl + 1) * SW]), "hw", flip=True)
        s.barrier()

        for l in range(L):
            ob = obs[:, l * 128:(l + 1) * 128]
            s.w(G, *s.chtok("st"), *s.chtok("hw"), *s.chtok("ou"))
            trv = s.dma(G, lambda g, l=l: g.dma_start(rvf[:, :], rowv[0:1, l * D:(l + 1) * D]), "wq")
            s.op(G, lambda g: g.memset(sbe[64:128, :], 0.0))
            s.op(G, lambda g: g.memset(sbo[0:64, :], 0.0))
            for sl_ in range(2):
                for i_ in range(16):
                    s.op(G, lambda g, sl_=sl_, i_=i_: g.memset(
                        vts[:, sl_ * 1040 + i_ * 65 + 64:sl_ * 1040 + i_ * 65 + 65], 1.0))
                    s.op(G, lambda g, sl_=sl_, i_=i_: g.memset(
                        vts1[:, sl_ * 1040 + i_ * 65 + 64:sl_ * 1040 + i_ * 65 + 65], 1.0))
            trvb0 = s.token(G)
            s.w(A, trv)
            trvb = s.op(A, lambda e: e.activation(rvb[:, :], rvf[:, :], AF.Identity, bias=0.0, scale=1.0))
            twq = None
            for wi, (dst, src) in enumerate(((wqsb, wq_in), (wksb, wk_in),
                                             (wvsb, wv_in), (wosb, wo_in))):
                for k in range(KC):
                    twq = s.dma(G, lambda g, dst=dst, src=src, k=k, l=l: g.dma_start(
                        dst[:, k * 1024:(k + 1) * 1024], src[l][k * 128:(k + 1) * 128, :]),
                        "wq", flip=(k == 0 and wi == 0))

            tsz = s.op(G, lambda g: g.memset(srun[:, :], 0.0))
            s.w(SY, tsz)
            tsnap = s.dma(SY, lambda e: e.dma_start(snap_d[0], srun[:, 0:520]), "sn", flip=True)
            tsadd = None
            tdm_last = None
            tevK = {}
            tfvK = {}
            tdmK = {}
            tvwK = {}
            kxdone = {}
            tvev = None
            ttr_done = None
            for sw in range(NSW):
                sl = sw % 2
                txb = None
                for k in range(KC):
                    if k == 0:
                        s.w(G, *(kxdone.get(sw - 2) or []))
                    txb = s.dma(G, lambda g, k=k, sw=sw, sl=sl: g.dma_start(
                        xbw[sl][:, k * SW:(k + 1) * SW], st_a[k][:, sw * SW:(sw + 1) * SW]),
                        "xb", flip=(k == 0))
                tev = {-2: (None, tevK.get((sw - 1) * KC + KC - 2)),
                       -1: (None, tevK.get((sw - 1) * KC + KC - 1))}
                for mc in range(KC):
                    pm = pf[mc % 2]
                    g2 = tev.get(mc - 2)
                    if mc == 0:
                        s.w(TE, tsadd)
                    s.w(TE, txb, twq, g2[1] if g2 else None)
                    tk = None
                    for k in range(KC):
                        tk = s.op(TE, lambda e, mc=mc, k=k, sl=sl, pm=pm: e.matmul(
                            pm[:, 0:SW], wksb[:, k * 1024 + mc * 128:k * 1024 + mc * 128 + 128],
                            xbw[sl][:, k * SW:(k + 1) * SW], start=(k == 0), stop=(k == KC - 1)))
                    s.w(A, tk)
                    te1 = s.op(A, lambda e, mc=mc, pm=pm, l=l: e.activation(
                        ksb[:, mc * SW:(mc + 1) * SW], pm[:, 0:SW], AF.Identity,
                        bias=pvcol(l, "bk", mc), scale=1.0))
                    te2 = s.op(A, lambda e, mc=mc, pm=pm, l=l: e.activation(
                        k2sb[:, mc * SW:(mc + 1) * SW], pm[:, 0:SW], AF.Square,
                        bias=pvcol(l, "bk", mc), scale=1.0))
                    tev[mc] = (te1, te2)
                    tevK[sw * KC + mc] = te2
                tfv = {-2: tfvK.get((sw - 1) * NP + NP - 2),
                       -1: tfvK.get((sw - 1) * NP + NP - 1)}
                for p in range(NP):
                    pm = pf[4 + p % 2]
                    if p == 0:
                        s.w(A, *s.chtok("st"))
                    s.w(TE, tev[p][0], tev[p][1], tfv.get(p - 2))
                    s.op(TE, lambda e, p=p, pm=pm, ob=ob: e.matmul(
                        pm[:, 0:SW], ob, ksb[:, p * SW:(p + 1) * SW], start=True, stop=False))
                    tk = s.op(TE, lambda e, p=p, pm=pm: e.matmul(
                        pm[:, 0:SW], cbt[:, :], k2sb[:, p * SW:(p + 1) * SW], start=False, stop=True))
                    s.w(A, tk)
                    tfv[p] = s.op(A, lambda e, p=p, pm=pm: e.activation(
                        phkb[:, p * SW:(p + 1) * SW], pm[:, 0:SW], AF.Exp,
                        bias=biasm[:, 0:1], scale=1.0))
                    tfvK[sw * NP + p] = tfv[p]
                    s.w(SY, tfv[p])
                    s.dma(SY, lambda e, p=p, sw=sw: e.dma_start(
                        phk_d[p][:, sw * SW:(sw + 1) * SW], phkb[:, p * SW:(p + 1) * SW]), "st")
                tvts = {}
                for c in range(NCHS):
                    cg = sw * NCHS + c
                    vsl = c % 2
                    s.w(TE, twq, tvev, trvb)
                    tk = None
                    for k in range(KC):
                        for half in range(2):
                            tk = s.op(TE, lambda e, c=c, k=k, half=half, sl=sl: e.matmul(
                                pf[2 + half][:, 0:SW],
                                xbw[sl][:, k * SW + c * CH:k * SW + c * CH + CH],
                                wvsb[:, k * 1024 + half * SW:k * 1024 + (half + 1) * SW],
                                start=(k == 0), stop=False))
                    for half in range(2):
                        tk = s.op(TE, lambda e, half=half: e.matmul(
                            pf[2 + half][:, 0:SW], onesbr[:, :],
                            rvb[0:1, half * SW:(half + 1) * SW],
                            start=False, stop=True))
                    s.w(A, tk, tvwK.get(cg - 2))
                    tva = s.op(A, lambda e, vsl=vsl: e.activation(
                        vstage[:, vsl * 1024:vsl * 1024 + SW], pf[2][:, 0:SW], AF.Identity, bias=0.0, scale=1.0))
                    tvev = s.op(A, lambda e, vsl=vsl: e.activation(
                        vstage[:, vsl * 1024 + SW:vsl * 1024 + 1024], pf[3][:, 0:SW], AF.Identity, bias=0.0, scale=1.0))
                    s.w(SY, tva, tvev)
                    tvwK[cg] = s.dma(SY, lambda e, cg=cg, vsl=vsl: e.dma_start(
                        vraw_d[cg * CH:(cg + 1) * CH, :], vstage[:, vsl * 1024:(vsl + 1) * 1024]), "st", flip=True)
                    s.w(V, tva, tvev, tdmK.get(cg - 2))
                    tvts[c] = s.op(V, lambda e, vsl=vsl: e.tensor_copy(
                        vts[:, vsl * 1040:(vsl + 1) * 1040].rearrange("p (i c) -> p i c", c=65)[:, :, 0:64],
                        vstage[:, vsl * 1024:(vsl + 1) * 1024].rearrange("p (i c) -> p i c", c=64)))
                    ttr = None
                    s.w(TE, tfv[NP - 1], ttr_done)
                    for p in range(NP):
                        ttr = s.op(TE, lambda e, p=p, c=c: e.transpose(
                            pbt[:, p * 128:(p + 1) * 128],
                            phkb[:, p * SW + c * CH:p * SW + (c + 1) * CH], identb[:, :]))
                    s.w(A, ttr, tdmK.get(cg - 1))
                    tpe = s.op(A, lambda e: e.activation(
                        phkt[:, 0:1024], pbt[:, 0:1024], AF.Identity, bias=0.0, scale=1.0))
                    ttr_done = tpe
                    s.w(TE, tev[KC - 1][1], tpe, tvts[c], tsadd, trvb0)
                    tdmc = None
                    for p in range(NP):
                        bank = pf[0] if p < 4 else pf[1]
                        off = (p % 4) * 65
                        for hh in range(2):
                            tp = None if hh == 0 else (0, 64)
                            orow = slice(64 * hh, 64 * hh + 64)
                            tdmc = s.op(TE, lambda e, p=p, hh=hh, bank=bank, off=off, orow=orow, tp=tp, vsl=vsl: e.matmul(
                                bank[orow, off:off + 65],
                                phkt[:, p * 128 + 64 * hh:p * 128 + 64 * hh + 64],
                                vts[:, vsl * 1040 + (p * 2 + hh) * 65:vsl * 1040 + (p * 2 + hh) * 65 + 65],
                                start=True, stop=True, tile_position=tp))
                    tdmK[cg] = tdmc
                    tdm_last = tdmc
                    cur = cg % 2
                    nxt = (cg + 1) % 2
                    s.w(V, tdmc, tsnap)
                    s.op(V, lambda e, cur=cur, nxt=nxt: e.tensor_add(
                        srun[:, nxt * 520:nxt * 520 + 260], srun[:, cur * 520:cur * 520 + 260],
                        pf[0][:, 0:260]))
                    tsadd = s.op(V, lambda e, cur=cur, nxt=nxt: e.tensor_add(
                        srun[:, nxt * 520 + 260:nxt * 520 + 520], srun[:, cur * 520 + 260:cur * 520 + 520],
                        pf[1][:, 0:260]))
                    if cg < 31:
                        s.w(SY, tsadd)
                        tsnap = s.dma(SY, lambda e, cg=cg, nxt=nxt: e.dma_start(
                            snap_d[cg + 1], srun[:, nxt * 520:(nxt + 1) * 520]), "sn", flip=True)
                kxdone[sw] = [tevK[sw * KC + KC - 1], tvev]
            s.w(SY, tsadd)
            tst_w = s.dma(SY, lambda e: e.dma_start(ccst_in, srun[:, 0:520]), "ou")
            s.w(G, tst_w)
            if nocc:
                for r in range(NCORE):
                    tcst = s.dma(G, lambda g, r=r: g.dma_start(
                        ccst_out[r * 128:(r + 1) * 128, :], ccst_in), "wq", flip=(r == 0))
            else:
                tcst = s.collective("AllGather", ALU.bypass, [ccst_in], [ccst_out])
            sst = {}

            def emit_sinit():
                s.w(SY, tcst)
                tsi = None
                tsis = {}
                for r in range(NCORE):
                    rl = r % 2
                    if r >= 2:
                        s.w(SY, tsis[r - 2])
                    tbl = s.dma(SY, lambda e, r=r, rl=rl: e.dma_start(
                        snapb[:, rl * 520:(rl + 1) * 520], ccst_out[r * 128:(r + 1) * 128, :]),
                        "ld", flip=True)
                    s.w(V, tbl)
                    if r == 0:
                        tsi = s.op(V, lambda e, rl=rl: e.tensor_scalar(
                            sinit[:, :], snapb[:, rl * 520:(rl + 1) * 520], scf[:, 0:1], None, ALU.mult))
                    else:
                        tsi = s.op(V, lambda e, r=r, rl=rl: e.scalar_tensor_tensor(
                            sinit[:, :], snapb[:, rl * 520:(rl + 1) * 520], scf[:, r:r + 1],
                            sinit[:, :], ALU.mult, ALU.add))
                    tsis[r] = tsi
                sst["tsi"] = tsis[NCORE - 1]
                sst["guard"] = tsis[NCORE - 2]

            s.w(G, tsadd, tdm_last, *s.chtok("st"))
            s.op(G, lambda g: g.memset(tscr[:, 0:256], 1.0))
            s.op(G, lambda g: g.affine_select(mw[:, 0:256], tscr[:, 0:256],
                                              pattern=[[1, 256]], compare_op=ALU.is_ge,
                                              fill=0.0, base=0, channel_multiplier=-1))
            tmkw = s.op(G, lambda g: g.memset(phz[0:64, :], 0.0))
            s.w(G, tevK[(NSW - 1) * KC + KC - 1], tvev)
            tW1pre = {}
            tW2pre = {}
            for j in range(8, 24):
                tW1pre[j] = s.dma(G, lambda g, j=j, l=l: g.dma_start(
                    w1sb[:, j * 1024:(j + 1) * 1024].rearrange("p (kc i) -> p kc i", i=128),
                    w1_in[l][:, j * 128:(j + 1) * 128].rearrange("(kc p) i -> p kc i", p=128)),
                    "wf", flip=True)
            gdh = {}
            tmk_hist = {}
            vq = [vts, vts1]
            prev_core = [tsadd, tvev, tfvK[(NSW - 1) * NP + NP - 1]]
            twoprev = None
            lnq = {}
            pend_ln1 = [None]
            amulQ = {}
            thwQ = {}
            qdone = {-2: kxdone[NSW - 2][0], -1: kxdone[NSW - 1][0]}
            tpe_last = ttr_done
            for sw in range(NSW):
                sl = sw % 2
                txb = None
                for k in range(KC):
                    if k == 0:
                        s.w(G, qdone.get(sw - 2))
                        if sw < 2:
                            s.w(G, *(kxdone.get(NSW - 2 + sw) or []))
                    txb = s.dma(G, lambda g, k=k, sw=sw, sl=sl: g.dma_start(
                        xbw[sl][:, k * SW:(k + 1) * SW], st_a[k][:, sw * SW:(sw + 1) * SW]),
                        "xb", flip=(k == 0))
                tphk = None
                for p in range(NP):
                    if p == 0:
                        if sw == 0:
                            s.w(SY, tpe_last, *s.chtok("st"))
                        s.w(SY, gdh.get((sw - 1) * 2 + 1))
                    tphk = s.dma(SY, lambda e, p=p, sw=sw: e.dma_start(
                        phkb[:, p * SW:(p + 1) * SW], phk_d[p][:, sw * SW:(sw + 1) * SW]),
                        "ph", flip=(p == 0))
                if sw == 0:
                    emit_sinit()
                tev = {}
                for mc in range(KC):
                    pm = pf[mc % 2]
                    g2 = tev.get(mc - 2)
                    if mc == 0:
                        s.w(TE, *prev_core, twoprev)
                    s.w(TE, txb, g2[1] if g2 else None)
                    tk = None
                    for k in range(KC):
                        tk = s.op(TE, lambda e, mc=mc, k=k, sl=sl, pm=pm: e.matmul(
                            pm[:, 0:SW], wqsb[:, k * 1024 + mc * 128:k * 1024 + mc * 128 + 128],
                            xbw[sl][:, k * SW:(k + 1) * SW], start=(k == 0), stop=(k == KC - 1)))
                    s.w(A, tk)
                    te1 = s.op(A, lambda e, mc=mc, pm=pm, l=l: e.activation(
                        ksb[:, mc * SW:(mc + 1) * SW], pm[:, 0:SW], AF.Identity,
                        bias=pvcol(l, "bq", mc), scale=1.0))
                    te2 = s.op(A, lambda e, mc=mc, pm=pm, l=l: e.activation(
                        k2sb[:, mc * SW:(mc + 1) * SW], pm[:, 0:SW], AF.Square,
                        bias=pvcol(l, "bq", mc), scale=1.0))
                    tev[mc] = (te1, te2)
                if pend_ln1[0] is not None:
                    pend_ln1[0]()
                    pend_ln1[0] = None
                if sw == NSW - 1:
                    s.w(G, tev[KC - 1][1])
                    for j in range(8):
                        tW1pre[j] = s.dma(G, lambda g, j=j, l=l: g.dma_start(
                            w1sb[:, j * 1024:(j + 1) * 1024].rearrange("p (kc i) -> p kc i", i=128),
                            w1_in[l][:, j * 128:(j + 1) * 128].rearrange("(kc p) i -> p kc i", p=128)),
                            "wf", flip=True)
                txf = None
                for k in range(KC):
                    s.w(SY, amulQ.get((sw - 1) * KC + k))
                    txf = s.dma(SY, lambda e, k=k, sw=sw: e.dma_start(
                        fbufA[:, k * SW:(k + 1) * SW], st_a[k][:, sw * SW:(sw + 1) * SW]),
                        "xf", flip=(k == 0))
                teps = {}
                for p in range(NP):
                    pm = pf[4 + p % 2]
                    s.w(TE, tev[p][1], teps.get(p - 2))
                    th = s.op(TE, lambda e, p=p, pm=pm: e.matmul(
                        pm[0:2, 0:SW], hm2[:, :], k2sb[:, p * SW:(p + 1) * SW], start=True, stop=True))
                    s.w(A, th)
                    teps[p] = s.op(A, lambda e, p=p, pm=pm: e.activation(
                        eps2[:, p * SW:(p + 1) * SW], pm[0:2, 0:SW], AF.Exp,
                        bias=biasm[0:2, 1:2], scale=1.0))
                tphq = {}
                tzs = {}
                for p in range(NP):
                    pm = pf[2 + p % 2]
                    s.w(TE, tev[p][0], tphq.get(p - 2))
                    tk = s.op(TE, lambda e, p=p, pm=pm, ob=ob: e.matmul(
                        pm[:, 0:SW], ob, ksb[:, p * SW:(p + 1) * SW], start=True, stop=True))
                    s.w(A, tk)
                    tphq[p] = s.op(A, lambda e, p=p, pm=pm: e.activation(
                        phqb[:, p * SW:(p + 1) * SW], pm[:, 0:SW], AF.Exp,
                        bias=biasm[:, 0:1], scale=1.0))
                    s.w(SY, tphq[p])
                    tzs[p] = s.dma(SY, lambda e, p=p: e.dma_start(
                        phz[64:128, p * SW:(p + 1) * SW], phqb[64:128, p * SW:(p + 1) * SW]),
                        "hw", flip=True)
                if sw == NSW - 1:
                    s.w(G, tphq[NP - 1], teps[NP - 1], tev[KC - 1][1])
                    for mo in range(4):
                        tW2pre[mo] = s.dma(G, lambda g, mo=mo, l=l: g.dma_start(
                            w2sb[:, mo * 4096:(mo + 1) * 4096].rearrange("p (j i) -> p j i", i=128),
                            w2_in[l][:, mo * 128:(mo + 1) * 128].rearrange("(j p) i -> p j i", p=128)),
                            "wf", flip=True)
                ev = {0: [], 1: []}
                for g in range(2):
                    gg2 = sw * 2 + g
                    gl = g
                    s.w(SY, gdh.get(gg2 - 2),
                        sst["guard"] if gg2 == 0 else (sst["tsi"] if gg2 == 1 else None))
                    for kb in range(2):
                        cg = gg2 * 2 + kb
                        s.dma(SY, lambda e, cg=cg, gl=gl, kb=kb: e.dma_start(
                            vq[gl][:, kb * 1040:(kb + 1) * 1040].rearrange("p (i c) -> p i c", c=65)[:, :, 0:64],
                            vraw_d[cg * CH:(cg + 1) * CH, :].rearrange("t (i c) -> t i c", c=64)),
                            "ld", flip=(kb == 0))
                    tsl_ = s.dma(SY, lambda e, gg2=gg2, gl=gl: e.dma_start(
                        snapb[:, gl * 520:(gl + 1) * 520], snap_d[2 * gg2]), "ld")
                    s.w(V, tsl_, gdh.get(gg2 - 1), sst["tsi"])
                    s.op(V, lambda e, gl=gl: e.tensor_add(
                        srq[:, :], snapb[:, gl * 520:(gl + 1) * 520], sinit[:, :]))
                    tce = s.op(V, lambda e: e.tensor_copy(sbe[0:64, :], srq[0:64, :]))
                    tco = s.op(V, lambda e: e.tensor_copy(sbo[64:128, :], srq[64:128, :]))

                    def emit_at(p):
                        pi = p % 2
                        co = g * 256
                        bA = pf[2 + pi]
                        bB = pf[4 + pi]
                        pg = gg2 * NP + p
                        s.w(TE, tphq[NP - 1], teps[NP - 1], tphk, trvb0, tmkw,
                            tzs[p], *(tmk_hist.get(pg - 2) or []))
                        tat = {}
                        tat[0] = s.op(TE, lambda e, p=p, co=co, bA=bA: e.matmul(
                            bA[:, 0:256],
                            phkb[0:64, p * SW + co:p * SW + co + CH],
                            phqb[0:64, p * SW + co:p * SW + co + 256],
                            start=True, stop=True))
                        tat[2] = s.op(TE, lambda e, p=p, co=co, bB=bB: e.matmul(
                            bB[:, 0:CH],
                            phkb[0:64, p * SW + co + CH:p * SW + co + 2 * CH],
                            phqb[0:64, p * SW + co + CH:p * SW + co + 2 * CH],
                            start=True, stop=True))
                        tat[1] = s.op(TE, lambda e, p=p, co=co, bA=bA: e.matmul(
                            bA[:, 256:512],
                            phkb[:, p * SW + co:p * SW + co + CH],
                            phz[:, p * SW + co:p * SW + co + 256],
                            start=True, stop=True))
                        tat[3] = s.op(TE, lambda e, p=p, co=co, bB=bB: e.matmul(
                            bB[:, CH:2 * CH],
                            phkb[:, p * SW + co + CH:p * SW + co + 2 * CH],
                            phz[:, p * SW + co + CH:p * SW + co + 2 * CH],
                            start=True, stop=True))
                        tmk = {}
                        s.w(V, tat[0], tat[1], tat[2], tat[3])
                        for hh in range(2):
                            tmk[hh] = s.op(V, lambda e, pi=pi, hh=hh, bA=bA: e.tensor_tensor(
                                atk0(pi, hh), bA[:, hh * 256:(hh + 1) * 256], mw[:, 0:256], ALU.mult))
                            tmk[2 + hh] = s.op(V, lambda e, pi=pi, hh=hh, bB=bB: e.tensor_tensor(
                                atk1(pi, hh), bB[:, hh * CH:(hh + 1) * CH], mw[:, 0:CH], ALU.mult))
                        tmk_hist[pg] = [tmk[i] for i in range(4)]
                        return tmk

                    def emit_ph2(dv):
                        (p, pi, co, bN, bD, tb) = dv
                        rq = rows[0:1, 1024 + pi * 512:1024 + (pi + 1) * 512]
                        s.w(A, tb[0], tb[1], lnq.get("A"), lnq.get("B"))
                        s.op(A, lambda e, rq=rq, bD=bD: e.activation(
                            rq[0:1, 0:512], bD[0:1, 0:512], AF.Ln))
                        trd = s.op(A, lambda e, rq=rq: e.activation(
                            rq[0:1, 0:512], rq[0:1, 0:512], AF.Exp, bias=0.0, scale=-1.0))
                        s.w(TE, trd)
                        trb0 = s.op(TE, lambda e, rq=rq, bN=bN: e.matmul(
                            bN[0:64, 256:512], onesf[0:1, 0:64],
                            rq[0:1, 0:256], start=True, stop=True))
                        trb1 = s.op(TE, lambda e, rq=rq, bN=bN: e.matmul(
                            bN[64:128, 256:512], onesf[0:1, 0:64],
                            rq[0:1, 256:512], start=True, stop=True,
                            tile_position=(0, 64)))
                        s.w(A, trb0, trb1)
                        trde = s.op(A, lambda e, pi=pi, bN=bN: e.activation(
                            rdbs2[pi][:, :], bN[:, 256:512], AF.Identity, bias=0.0, scale=1.0))
                        s.w(V, trde)
                        tout = s.op(V, lambda e, p=p, co=co, pi=pi, bN=bN: e.tensor_tensor(
                            asw[:, p * SW + co:p * SW + co + 256], bN[:, 0:256],
                            rdbs2[pi][:, :], ALU.mult))
                        ev[pi] = [tout, trde]
                        return tout

                    pend = None
                    tmks = {0: emit_at(0)}
                    for p in range(NP):
                        pi = p % 2
                        co = g * 256
                        bN = pf[pi]
                        bD = pf[6] if pi == 0 else pbtF
                        if p + 1 < NP:
                            tmks[p + 1] = emit_at(p + 1)
                        tmk = tmks[p]
                        tb = {}
                        s.w(TE, tmk[0], tmk[1], tmk[2], tmk[3], tce, tco,
                            amulQ.get((sw - 1) * KC + KC - 1), *ev[pi])
                        for hh in range(2):
                            tp = None if hh == 0 else (0, 64)
                            orow = slice(64 * hh, 64 * hh + 64)
                            sb = sbe if hh == 0 else sbo
                            s.op(TE, lambda e, p=p, co=co, tp=tp, orow=orow, sb=sb, bN=bN: e.matmul(
                                bN[orow, 0:256], sb[:, p * 65:p * 65 + 64],
                                phqb[:, p * SW + co:p * SW + co + 256],
                                start=True, stop=False, tile_position=tp))
                            s.op(TE, lambda e, p=p, hh=hh, pi=pi, gl=gl, tp=tp, orow=orow, bN=bN: e.matmul(
                                bN[orow, CH:256],
                                vq[gl][:, 1040 + (p * 2 + hh) * 65:1040 + (p * 2 + hh) * 65 + 64],
                                atk1(pi, hh), start=False, stop=False, tile_position=tp))
                            s.op(TE, lambda e, p=p, hh=hh, pi=pi, gl=gl, tp=tp, orow=orow, bN=bN: e.matmul(
                                bN[orow, 0:256],
                                vq[gl][:, (p * 2 + hh) * 65:(p * 2 + hh) * 65 + 64],
                                atk0(pi, hh), start=False, stop=True, tile_position=tp))
                            dn = bD[0:1, hh * 256:(hh + 1) * 256]
                            dn1 = bD[0:1, hh * 256 + 128:(hh + 1) * 256]
                            s.op(TE, lambda e, p=p, co=co, sb=sb, dn=dn: e.matmul(
                                dn, sb[:, p * 65 + 64:p * 65 + 65],
                                phqb[:, p * SW + co:p * SW + co + 256],
                                start=True, stop=False))
                            s.op(TE, lambda e, p=p, hh=hh, pi=pi, gl=gl, dn1=dn1: e.matmul(
                                dn1,
                                vq[gl][:, 1040 + (p * 2 + hh) * 65 + 64:1040 + (p * 2 + hh) * 65 + 65],
                                atk1(pi, hh), start=False, stop=False))
                            s.op(TE, lambda e, p=p, hh=hh, pi=pi, gl=gl, dn=dn: e.matmul(
                                dn,
                                vq[gl][:, (p * 2 + hh) * 65 + 64:(p * 2 + hh) * 65 + 65],
                                atk0(pi, hh), start=False, stop=False))
                            tb[hh] = s.op(TE, lambda e, p=p, hh=hh, co=co, dn=dn: e.matmul(
                                dn, identb[0:2, hh:hh + 1],
                                eps2[:, p * SW + co:p * SW + co + 256], start=False, stop=True))
                        if pend is not None:
                            emit_ph2(pend)
                        pend = (p, pi, co, bN, bD, tb)
                    gdh[gg2] = emit_ph2(pend)
                two = {}
                for mo in range(KC):
                    pm = pf[6] if mo % 2 == 0 else pf[0]
                    if mo < 2:
                        s.w(A, thwQ.get((sw - 1) * KC + KC - 2 + mo))
                    if mo == 0:
                        s.w(TE, amulQ.get((sw - 1) * KC + KC - 1))
                    s.w(TE, *ev[0], *ev[1], two.get(mo - 2))
                    tk = None
                    for k in range(KC):
                        tk = s.op(TE, lambda e, mo=mo, k=k, pm=pm: e.matmul(
                            pm[:, 0:SW], wosb[:, k * 1024 + mo * 128:k * 1024 + mo * 128 + 128],
                            asw[:, k * SW:(k + 1) * SW], start=(k == 0), stop=(k == KC - 1)))
                    s.w(A, tk, two.get(mo - 2))
                    ta = s.op(A, lambda e, mo=mo, pm=pm, l=l: e.activation(
                        tscr[:, (mo % 2) * SW:(mo % 2 + 1) * SW], pm[:, 0:SW], AF.Identity,
                        bias=pvcol(l, "bo", mo), scale=1.0))
                    s.w(V, ta, txf)
                    two[mo] = s.op(V, lambda e, mo=mo: e.tensor_add(
                        fbufA[:, mo * SW:(mo + 1) * SW], fbufA[:, mo * SW:(mo + 1) * SW],
                        tscr[:, (mo % 2) * SW:(mo % 2 + 1) * SW]))
                if sw == NSW - 1:
                    s.w(G, *ev[0], *ev[1], two[KC - 1])
                    for j in range(24, MJ):
                        tW1pre[j] = s.dma(G, lambda g, j=j, l=l: g.dma_start(
                            w1sb[:, j * 1024:(j + 1) * 1024].rearrange("p (kc i) -> p kc i", i=128),
                            w1_in[l][:, j * 128:(j + 1) * 128].rearrange("(kc p) i -> p kc i", p=128)),
                            "wf", flip=True)
                    for mo in range(4, KC):
                        tW2pre[mo] = s.dma(G, lambda g, mo=mo, l=l: g.dma_start(
                            w2sb[:, mo * 4096:(mo + 1) * 4096].rearrange("p (j i) -> p j i", i=128),
                            w2_in[l][:, mo * 128:(mo + 1) * 128].rearrange("(j p) i -> p j i", p=128)),
                            "wf", flip=True)
                phB = emit_ln_sums(gates=(two[KC - 1], thwQ.get((sw - 1) * KC + KC - 2)))

                def mk_ln1(sw, phB, l=l):
                    def flush():
                        tbA, tbB = phB(pf[6], pbtF, pf[6], pbtF)
                        thw = {-2: thwQ.get((sw - 1) * KC + KC - 2),
                               -1: thwQ.get((sw - 1) * KC + KC - 1)}
                        for k in range(KC):
                            kl = k % 2
                            s.w(V, tbA, tbB, thw.get(k - 2))
                            amulQ[sw * KC + k] = s.op(V, lambda e, k=k, kl=kl: e.tensor_mul(
                                tscr[:, kl * SW:(kl + 1) * SW],
                                fbufA[:, k * SW:(k + 1) * SW], pf[6][:, 0:SW]))
                            tt2 = s.op(V, lambda e, kl=kl: e.tensor_add(
                                tscr[:, kl * SW:(kl + 1) * SW],
                                tscr[:, kl * SW:(kl + 1) * SW], pbtF[:, 0:SW]))
                            s.w(A, tt2)
                            th = s.op(A, lambda e, k=k, kl=kl, l=l: e.activation(
                                hsl[:, kl * SW:(kl + 1) * SW], tscr[:, kl * SW:(kl + 1) * SW], AF.Identity,
                                bias=pvcol(l, "ln1b", k), scale=pvcol(l, "ln1g", k)))
                            s.w(SY, th)
                            thw[k] = s.dma(SY, lambda e, k=k, sw=sw, kl=kl: e.dma_start(
                                st_h[k][:, sw * SW:(sw + 1) * SW], hsl[:, kl * SW:(kl + 1) * SW]), "hw", flip=True)
                            thwQ[sw * KC + k] = thw[k]
                        lnq["A"], lnq["B"] = tbA, tbB
                    return flush
                pend_ln1[0] = mk_ln1(sw, phB)
                twoprev = two[KC - 1]
                prev_core = ev[0] + ev[1]
                qdone[sw] = tev[KC - 1][1]
            if pend_ln1[0] is not None:
                pend_ln1[0]()
                pend_ln1[0] = None
            s.barrier()

            tW1s = tW1pre
            tW2s = tW2pre
            treF = {}
            tyaddF = {}
            thwF = {}
            fblast = {}
            pend_ln2 = [None]
            for sw in range(NSW):
                thb = None
                for k in range(KC):
                    if k == 0:
                        s.w(G, treF.get((sw - 1) * MJ + MJ - 1))
                    thb = s.dma(G, lambda g, k=k, sw=sw: g.dma_start(
                        hbf[:, k * SW:(k + 1) * SW], st_h[k][:, sw * SW:(sw + 1) * SW]),
                        "xb", flip=(k == 0))
                tyadd = {-2: tyaddF.get((sw - 1) * KC + KC - 2),
                         -1: tyaddF.get((sw - 1) * KC + KC - 1)}
                tre = {-2: treF.get((sw - 1) * MJ + MJ - 2),
                       -1: treF.get((sw - 1) * MJ + MJ - 1)}
                txf = None
                for j in range(MJ):
                    if j == 2:
                        if pend_ln2[0] is not None:
                            pend_ln2[0]()
                            pend_ln2[0] = None
                        for k in range(KC):
                            s.w(SY, fblast.get((sw - 1) * KC + k))
                            txf = s.dma(SY, lambda e, k=k, sw=sw: e.dma_start(
                                fbufA[:, k * SW:(k + 1) * SW], st_h[k][:, sw * SW:(sw + 1) * SW]),
                                "xf", flip=(k == 0))
                    pm = pf[j % 2]
                    s.w(TE, tW1s[j], thb, tre.get(j - 2), tyadd.get(-1))
                    tk = None
                    for k in range(KC):
                        tk = s.op(TE, lambda e, j=j, k=k, pm=pm: e.matmul(
                            pm[:, 0:TG], w1sb[:, j * 1024 + k * 128:j * 1024 + k * 128 + 128],
                            hbf[:, k * SW:k * SW + TG], start=(k == 0), stop=(k == KC - 1)))
                    s.w(A, tk)
                    tre[j] = s.op(A, lambda e, j=j, pm=pm, l=l: e.activation(
                        h1f[:, j * TG:(j + 1) * TG], pm[:, 0:TG], AF.Relu,
                        bias=pvcol(l, "bf1", j), scale=1.0))
                    treF[sw * MJ + j] = tre[j]
                for mo in range(KC):
                    pm = pf[2 + mo % 2]
                    s.w(TE, tW2s[mo], tre[MJ - 1], tyadd.get(mo - 2))
                    if mo < 2:
                        s.w(A, *(thwF.get((sw - 1) * KC + KC - 2 + mo) or []))
                    tk = None
                    for j in range(MJ):
                        tk = s.op(TE, lambda e, mo=mo, j=j, pm=pm: e.matmul(
                            pm[:, 0:TG], w2sb[:, mo * 4096 + j * 128:mo * 4096 + j * 128 + 128],
                            h1f[:, j * TG:(j + 1) * TG], start=(j == 0), stop=(j == MJ - 1)))
                    s.w(A, tk, tyadd.get(mo - 2))
                    ta = s.op(A, lambda e, mo=mo, pm=pm, l=l: e.activation(
                        tscr[:, (mo % 2) * SW:(mo % 2) * SW + TG],
                        pm[:, 0:TG], AF.Identity, bias=pvcol(l, "bf2", mo), scale=1.0))
                    s.w(V, ta, txf)
                    tyadd[mo] = s.op(V, lambda e, mo=mo: e.tensor_add(
                        fbufA[:, mo * SW:mo * SW + TG],
                        fbufA[:, mo * SW:mo * SW + TG],
                        tscr[:, (mo % 2) * SW:(mo % 2) * SW + TG]))
                    tyaddF[sw * KC + mo] = tyadd[mo]
                gates = [tyadd[mo] for mo in range(KC)] + (thwF.get((sw - 1) * KC + KC - 2) or [])
                phB2 = emit_ln_sums(gates=gates)

                def mk_ln2(sw, phB2, l=l):
                    def flush():
                        tbA, tbB = phB2(pf[4], pf[5], pf[4], pf[5])
                        thw = {-2: thwF.get((sw - 1) * KC + KC - 2),
                               -1: thwF.get((sw - 1) * KC + KC - 1)}
                        for k in range(KC):
                            kl = k % 2
                            s.w(V, tbA, tbB, *(thw.get(k - 2) or []))
                            t1k = s.op(V, lambda e, k=k, kl=kl: e.tensor_mul(
                                tscr[:, kl * SW:(kl + 1) * SW],
                                fbufA[:, k * SW:(k + 1) * SW], pf[4][:, 0:SW]))
                            tt2 = s.op(V, lambda e, kl=kl: e.tensor_add(
                                tscr[:, kl * SW:(kl + 1) * SW],
                                tscr[:, kl * SW:(kl + 1) * SW], pf[5][:, 0:SW]))
                            s.w(A, tt2)
                            th = s.op(A, lambda e, k=k, kl=kl, l=l: e.activation(
                                hsl[:, kl * SW:(kl + 1) * SW], tscr[:, kl * SW:(kl + 1) * SW], AF.Identity,
                                bias=pvcol(l, "ln2b", k), scale=pvcol(l, "ln2g", k)))
                            if l < L - 1:
                                s.w(SY, th)
                                thw[k] = [s.dma(SY, lambda e, k=k, sw=sw, kl=kl: e.dma_start(
                                    st_a[k][:, sw * SW:(sw + 1) * SW], hsl[:, kl * SW:(kl + 1) * SW]),
                                    "hw", flip=True)]
                                thwF[sw * KC + k] = thw[k]
                                fblast[sw * KC + k] = t1k
                            else:
                                s.w(SY, t1k)
                                tx1l = s.dma(SY, lambda e, k=k, sw=sw: e.dma_start(
                                    fbufA[:, k * SW:(k + 1) * SW], st_x1[k][:, sw * SW:(sw + 1) * SW]),
                                    "ld", flip=True)
                                s.w(V, th, tx1l)
                                tx2 = s.op(V, lambda e, k=k, kl=kl: e.tensor_add(
                                    tscr[:, kl * SW:(kl + 1) * SW], hsl[:, kl * SW:(kl + 1) * SW],
                                    fbufA[:, k * SW:(k + 1) * SW]))
                                s.w(SY, tx2)
                                tyd = s.dma(SY, lambda e, k=k, sw=sw, kl=kl: e.dma_start(
                                    yshard[k][:, sw * SW:(sw + 1) * SW], tscr[:, kl * SW:(kl + 1) * SW]),
                                    "ou", flip=True)
                                tvm = s.op(V, lambda e, kl=kl: e.tensor_scalar(
                                    hsl[:, kl * SW:(kl + 1) * SW], tscr[:, kl * SW:(kl + 1) * SW],
                                    m0[:, 0:1], None, ALU.mult))
                                s.w(SY, tvm)
                                tcxk = s.dma(SY, lambda e, k=k, sw=sw, kl=kl: e.dma_start(
                                    ccx2_in[sw * SW + (k // 2) * 128:sw * SW + (k // 2) * 128 + 128,
                                            (k % 2) * SW:(k % 2 + 1) * SW],
                                    hsl[:, kl * SW:(kl + 1) * SW]), "hw", flip=True)
                                thw[k] = [tyd, tcxk]
                                thwF[sw * KC + k] = thw[k]
                                fblast[sw * KC + k] = tx2
                    return flush
                pend_ln2[0] = mk_ln2(sw, phB2)
            if pend_ln2[0] is not None:
                pend_ln2[0]()
                pend_ln2[0] = None
            s.barrier()

        s.w(G, *s.chtok("hw"))
        if nocc:
            tccx = s.dma(G, lambda g: g.dma_start(ccx2_out, ccx2_in[0:MTOK, :]), "wq", flip=True)
        else:
            tccx = s.collective("ReduceScatter", ALU.add, [ccx2_in], [ccx2_out])

        tewb = build_ewbs(s.token(V))
        s.w(SY, tccx)
        tx5 = None
        for k in range(KC):
            tx5 = s.dma(SY, lambda e, k=k: e.dma_start(
                fbufA[:, k * SW:(k + 1) * SW],
                ccx2_out[(k // 2) * 128:(k // 2) * 128 + 128, (k % 2) * SW:(k % 2 + 1) * SW]),
                "xf", flip=(k == 0))
        tbA, tbB = emit_ln(pf[4], pf[5], pf[5], pf[6], gates=(tx5, tewb))
        tgxg = None
        for k in range(KC):
            kl = k % 2
            s.w(V, tbA, tbB, tgxg)
            s.op(V, lambda e, k=k, kl=kl: e.tensor_mul(hsl[:, kl * SW:(kl + 1) * SW],
                                                       fbufA[:, k * SW:(k + 1) * SW], pf[5][:, 0:SW]))
            tt2 = s.op(V, lambda e, kl=kl: e.tensor_add(hsl[:, kl * SW:(kl + 1) * SW],
                                                        hsl[:, kl * SW:(kl + 1) * SW], pf[6][:, 0:SW]))
            s.w(A, tt2)
            tgxg = s.op(A, lambda e, k=k, kl=kl: e.activation(
                xg[:, k * SW:(k + 1) * SW], hsl[:, kl * SW:(kl + 1) * SW], AF.Identity, bias=0.0, scale=1.0))

        ydma = {}

        def wr_moe3(mo, td, src):
            s.w(V, td, ydma.get(mo - 2))
            ty = s.op(V, lambda e, mo=mo, src=src: e.tensor_add(
                hsl[:, (mo % 2) * SW:(mo % 2 + 1) * SW],
                fbufA[:, mo * SW:(mo + 1) * SW], src))
            s.w(SY, ty)
            ydma[mo] = s.dma(SY, lambda e, mo=mo: e.dma_start(
                ymoe3[mo], hsl[:, (mo % 2) * SW:(mo % 2 + 1) * SW]), "hw", flip=True)
            return ydma[mo]
        emit_moe(tgxg, wr_moe3)

        s.drain_stores()
        s.op(SY, lambda e: e.nop())
        s.replay(block)
    return nc


_NC_CACHE = {}


def kernel(L0_attn_x, L0_attn_Wq, L0_attn_bq, L0_attn_Wk, L0_attn_bk,
           L0_attn_Wv, L0_attn_bv, L0_attn_Wo, L0_attn_bo, L0_attn_omega,
           L0_attn_ln1_g, L0_attn_ln1_b, L0_attn_ln2_g, L0_attn_ln2_b,
           L0_attn_W1, L0_attn_bf1, L0_attn_W2, L0_attn_bf2,
           L0_ffn_scores, L0_ffn_expert_weights, L0_ffn_top_experts,
           L0_ffn_w1, L0_ffn_w2):
    x = np.ascontiguousarray(np.asarray(L0_attn_x, np.float32))
    B, S, _ = x.shape

    def fm(a):
        return np.ascontiguousarray(np.asarray(a, np.float32).T.reshape(KC, 128, -1))

    def cols(v):
        return np.asarray(v, np.float32).reshape(8, 128).T

    pvec = np.zeros((128, 312), np.float32)
    for l in range(L):
        base = l * 104
        pvec[:, base + 0:base + 8] = cols(L0_attn_bq[l])
        pvec[:, base + 8:base + 16] = cols(L0_attn_bk[l])
        pvec[:, base + 16:base + 24] = cols(L0_attn_bv[l])
        pvec[:, base + 24:base + 32] = cols(L0_attn_bo[l])
        pvec[:, base + 32:base + 40] = cols(L0_attn_ln1_g[l])
        pvec[:, base + 40:base + 48] = cols(L0_attn_ln1_b[l])
        pvec[:, base + 48:base + 56] = cols(L0_attn_ln2_g[l])
        pvec[:, base + 56:base + 64] = cols(L0_attn_ln2_b[l])
        pvec[:, base + 64:base + 72] = cols(L0_attn_bf2[l])
        pvec[:, base + 72:base + 104] = np.asarray(L0_attn_bf1[l], np.float32).reshape(32, 128).T
    rowv = np.ascontiguousarray(np.asarray(L0_attn_bv, np.float32).reshape(1, L * D))
    ew = np.asarray(L0_ffn_expert_weights, np.float32)

    if "nc" not in _NC_CACHE:
        _NC_CACHE["nc"] = build()
    nc = _NC_CACHE["nc"]

    in_maps = []
    for i in range(NCORE):
        b, half = i // 2, i % 2
        m0v = np.full((128, 1), 1.0 if i == 0 else 0.0, np.float32)
        sc = np.zeros((128, 8), np.float32)
        if i % 2 == 1:
            sc[:, i - 1] = 1.0
        in_maps.append({
            "xshard": fm(x[b, half * T:(half + 1) * T, :]),
            "xmoe": fm(x[0, i * MTOK:(i + 1) * MTOK, :]),
            "ewT": np.ascontiguousarray(ew[i * MTOK:(i + 1) * MTOK, :].T),
            "pvec": pvec, "rowv": rowv,
            "omg_in": np.asarray(L0_attn_omega, np.float32),
            "wq_in": np.asarray(L0_attn_Wq, np.float32),
            "wk_in": np.asarray(L0_attn_Wk, np.float32),
            "wv_in": np.asarray(L0_attn_Wv, np.float32),
            "wo_in": np.asarray(L0_attn_Wo, np.float32),
            "w1_in": np.asarray(L0_attn_W1, np.float32),
            "w2_in": np.asarray(L0_attn_W2, np.float32),
            "mw1_in": np.asarray(L0_ffn_w1, np.float32),
            "mw2_in": np.asarray(L0_ffn_w2, np.float32),
            "mask0": m0v, "scoef": sc,
        })
    res = run_bass_kernel_spmd(nc, in_maps, core_ids=list(range(NCORE)))
    out = np.empty((B, S, D), np.float32)
    for i in range(NCORE):
        b, half = i // 2, i % 2
        ys = np.asarray(res.results[i]["yshard"])
        out[b, half * T:(half + 1) * T, :] = ys.transpose(2, 0, 1).reshape(T, D)
    for i in range(NCORE):
        ym = np.asarray(res.results[i]["ymoe3"])
        out[0, i * MTOK:(i + 1) * MTOK, :] = ym.transpose(2, 0, 1).reshape(MTOK, D)
    return out
